# version 01
# baseline (speedup 1.0000x reference)
import numpy as np
import concourse.bass as bass
import concourse.mybir as mybir
from concourse.bass_utils import run_bass_kernel_spmd

F32 = mybir.dt.float32
BF16 = mybir.dt.bfloat16
I32 = mybir.dt.int32
AF = mybir.ActivationFunctionType
ALU = mybir.AluOpType
AX = mybir.AxisListType

T = 2048
D = 1024
DC = 8
NW = 4
NT = 16
H = 16
DH = 64
FF = 4096
FC = 32
TCN = 8
EPS_IN = 1e-3
EPS_OUT = 1e-5
EPS_QK = 1e-6


class _Cut(Exception):
    pass


def build_nc(debug=(), stop_after=None):
    nc = bass.Bass("TRN2", target_bir_lowering=False)
    dbgn = set(debug)

    def din(name, shape, dtype=F32):
        return nc.dram_tensor(name, list(shape), dtype, kind="ExternalInput").ap()

    xT = din("xT", [D, T])
    WTS = {w: din(w, [D, D]) for w in ("Wq", "Wk", "Wv", "Wq2", "Wk2", "Wv2", "Wo")}
    vecs_in = din("vecs", [128, 7 * DC])
    brow_in = din("brow", [1, 2 * D])
    cT_in = din("cT", [128, DC])
    wmod_in = din("wmod", [D, 6 * D])
    wiT_in = din("wiT", [D, FF])
    woT_in = din("woT", [FF, D])
    Aall_in = din("Aall", [128, D])
    BallT_in = din("BallT", [128, FF])
    idx_in = din("idx", [1, 2], I32)
    tkv_in = din("tkv", [1, 2])
    constB_in = din("constB", [128, 272])
    constF_in = din("constF", [128, 8])
    sel_in = din("sel", [16, 1024])
    ones_in = din("ones_in", [1, 128])

    out_xT = nc.dram_tensor("out_xT", [D, T], F32, kind="ExternalOutput").ap()
    cc_in = nc.dram_tensor("cc_in", [64, H * DH], BF16, kind="Internal").ap()
    cc_out = nc.dram_tensor("cc_out", [64, H * DH], BF16, kind="Internal").ap()

    dbg = {}
    def dbgt(name, shape, dtype=F32):
        if name in dbgn:
            dbg[name] = nc.dram_tensor("dbg_" + name, list(shape), dtype,
                                       kind="ExternalOutput").ap()
    dbgt("modT", [128, 48]); dbgt("hT", [128, 16384], BF16)
    dbgt("q2", [128, 16384], BF16); dbgt("k2", [128, 16384], BF16)
    dbgt("Stot", [64, 1024], BF16); dbgt("qks", [128, 256])
    dbgt("v2", [128, 16384], BF16); dbgt("yT", [128, 16384], BF16)
    dbgt("x1T", [128, 16384], BF16); dbgt("h2tc0", [128, 2048], BF16)
    dbgt("interC0", [128, 8192], BF16); dbgt("BselT", [128, 4096], BF16)
    dbgt("u01", [64, 512], BF16); dbgt("OH", [128, 32], BF16)
    dbgt("qksrT", [128, 2048], BF16); dbgt("ASEL", [128, 256], BF16)
    dbgt("recip", [128, 256], BF16); dbgt("qks2", [128, 256])
    dbgt("DERV", [128, 64])

    A = nc.alloc_sbuf_tensor
    SX = A("SX", [128, 16384], BF16)
    SH = A("SH", [128, 16384], BF16)
    SB = A("SB", [128, 16384], BF16)
    SC = A("SC", [128, 16384], BF16)
    SAa = A("SAa", [128, 4096], BF16)
    SAb = A("SAb", [128, 4096], BF16)
    W0 = A("W0", [128, 8192], BF16)
    W1 = A("W1", [128, 8192], BF16)
    POOL = A("POOL", [128, 5120], BF16)
    CONST = A("CONSTB", [128, 272], BF16)
    P16T = A("P16T", [128, 8], F32)
    VECS = A("VECS", [128, 7 * DC], F32)
    DERV = A("DERV", [128, 8 * DC], F32)
    MODT = A("MODT", [128, 48], F32)
    CT = A("CT", [128, DC], F32)
    SIG = A("SIG", [128, DC], F32)
    SILU = A("SILU", [128, DC], BF16)
    ARENA = A("ARENA", [128, 4096], BF16)
    SEL = A("SEL", [16, 1024], BF16)
    UB = A("UB", [64, 512], BF16)
    BROW = A("BROW", [1, D], BF16)
    STF = A("STF", [128, 1024], F32)
    ASEL = A("ASEL", [128, 256], BF16)
    TKB = A("TKB", [128, 2], F32)
    IDXS = A("IDXS", [1, 2], I32)
    IDXF = A("IDXF", [1, 2], F32)
    IDXB = A("IDXB", [1, 2], BF16)
    OHE = A("OHE", [128, 2], F32)
    OH = A("OH", [128, 32], BF16)
    TKS = A("TKS", [1, 2], BF16)
    ONES1 = A("ONES1", [1, 128], BF16)

    MUT = POOL[:, 0:512]
    RST = POOL[:, 512:1024]
    TMPE = POOL[:, 1024:2048]
    TMPF = POOL[:, 2048:3072]
    QKSRT = POOL[0:16, 3072:5120]
    SPART = POOL[0:64, 1024:2048]
    ROW0 = POOL[0:1, 0:512]
    ROW1 = POOL[0:1, 512:1024]
    H2TC = POOL[:, 0:2048]
    XSQ = POOL[:, 2048:2560]
    R0P = POOL[:, 2560:3072]
    R1P = POOL[:, 3072:3584]
    MUT2 = POOL[:, 3584:3840]
    RST2 = POOL[:, 3840:4096]
    TMP6 = POOL[:, 4096:5120]
    QKS = STF[:, 768:1024]
    STOT = POOL[0:64, 3072:4096]
    STOT64 = POOL[64:128, 3072:4096]

    Jc = CONST[:, 0:128]
    I128 = CONST[:, 128:256]
    I16t = CONST[:, 256:272]

    PA = nc.alloc_psum_tensor("PA", [128, 1024], F32)
    PB = nc.alloc_psum_tensor("PB", [128, 1024], F32)
    PC = nc.alloc_psum_tensor("PC", [128, 1024], F32)
    PD = nc.alloc_psum_tensor("PD", [128, 512], F32)
    PTB = nc.alloc_psum_tensor("PTB", [128, 1024], BF16)

    sems = {}
    for name in ("t", "v", "sc", "g", "s", "dg", "ds"):
        sems[name] = nc.alloc_semaphore("sem_" + name)

    for col, val in ((1, -0.5), (2, 1e-5), (3, 1e-3), (4, 0.0), (5, 1.0)):
        nc.const_aps.register(F32, float(val), P16T[:, col:col + 1])

    def W0b(k, m):
        return W0[:, 1024 * k + 128 * m: 1024 * k + 128 * m + 128]

    def W1b(k, m):
        return W1[:, 1024 * k + 128 * m: 1024 * k + 128 * m + 128]

    def wiTblk(k, fc):
        if k < 4:
            return SB[:, 4096 * k + 128 * fc: 4096 * k + 128 * fc + 128]
        if k < 6:
            return W1[:, 4096 * (k - 4) + 128 * fc: 4096 * (k - 4) + 128 * fc + 128]
        if k == 6:
            return SAa[:, 128 * fc: 128 * fc + 128]
        return SAb[:, 128 * fc: 128 * fc + 128]

    def woTblk(fc, dp):
        if fc < 16:
            return SH[:, 1024 * fc + 128 * dp: 1024 * fc + 128 * dp + 128]
        return SC[:, 1024 * (fc - 16) + 128 * dp: 1024 * (fc - 16) + 128 * dp + 128]

    def wmod_rhs(k, n):
        if k < 6:
            base = [SB, SB, SC, SC, SH, SH][k]
            off = (k % 2) * 6144
            return base[:, off + 512 * n: off + 512 * n + 512]
        if k == 6:
            if n < 8:
                return SAa[:, 512 * n: 512 * n + 512]
            return SAb[:, 512 * (n - 8): 512 * (n - 8) + 512]
        if n < 8:
            return SB[:, 12288 + 512 * n: 12288 + 512 * n + 512]
        return SC[:, 12288 + 512 * (n - 8): 12288 + 512 * (n - 8) + 512]

    def emit(my, e):
        def cut(name):
            if stop_after == name:
                raise _Cut()
        Cn = {k: 0 for k in ("t", "v", "sc", "g", "s")}
        D16 = {"g": 0, "s": 0}
        lastw = {}
        bankfree = {}

        def op(eng2, builder, sem=True):
            if my == eng2:
                ins = builder(e)
                if sem:
                    ins.then_inc(sems[eng2], 1)
            if sem:
                Cn[eng2] += 1
            return Cn[eng2]

        def w(eng2, sem_name, val):
            if my != eng2 or val is None or val <= 0:
                return
            if lastw.get(sem_name, -1) >= val:
                return
            lastw[sem_name] = val
            e.wait_ge(sems[sem_name], val)

        def dma(q, out, in_, **kw):
            D16[q] += 16
            if my == q:
                e.dma_start(out=out, in_=in_, **kw).then_inc(
                    sems["dg" if q == "g" else "ds"], 16)
            return D16[q]

        def wdma(eng2, q, tok):
            if tok:
                w(eng2, "dg" if q == "g" else "ds", tok)

        def fence(eng2):
            w(eng2, eng2, Cn[eng2])

        def mm(out, lhsT, rhs, start, stop, sem=False):
            return op("t", lambda e: e.matmul(out, lhsT, rhs, start=start,
                                              stop=stop), sem=sem)

        def use_bank(*keys):
            for k in keys:
                if k in bankfree:
                    eng2, tok = bankfree[k]
                    w("t", eng2, tok)

        def free_bank(key, eng2, tok):
            bankfree[key] = (eng2, tok)

        tok_p16 = dma("s", P16T[:, :], constF_in[:, :])
        tok_vecs = dma("s", VECS[:, :], vecs_in[:, :])
        tok_ct = dma("s", CT[:, :], cT_in[:, :])
        tok_idx = dma("s", IDXS[:, :], idx_in[:, :])

        tok_consts = dma("g", CONST[:, :], constB_in[:, :])
        tok_ones1 = dma("g", ONES1[:, :], ones_in[:, :])
        tok_sel = dma("g", SEL[:, :], sel_in[:, :])
        tok_brow = dma("g", BROW[:, :], brow_in[0:1, 0:D])
        wmod_tok = []
        for k in range(6):
            dsts = [SB, SB, SC, SC, SH, SH][k]
            off = (k % 2) * 6144
            wmod_tok.append(dma("g", dsts[:, off:off + 6144],
                                wmod_in[128 * k:128 * (k + 1), :]))
        wmod_tok.append(dma("g", SAa[:, 0:4096], wmod_in[768:896, 0:4096]))
        wmod_tok.append(dma("g", SAb[:, 0:2048], wmod_in[768:896, 4096:6144]))
        wmod_tok.append(dma("g", SB[:, 12288:16384], wmod_in[896:1024, 0:4096]))
        wmod_tok.append(dma("g", SC[:, 12288:14336], wmod_in[896:1024, 4096:6144]))
        tok_aall = dma("g", W1[:, 0:1024], Aall_in[:, :])
        tok_ball = dma("g", ARENA[:, :], BallT_in[:, :])
        tok_tkf = dma("g", TKS[:, :], tkv_in[:, :])
        tok_x = dma("g", SX.ap().rearrange("p (d t) -> p d t", d=DC),
                    xT.rearrange("(d p) t -> p d t", p=128))

        def wload(buf, name, gate=None):
            if gate is not None:
                w("g", gate[0], gate[1])
            return dma("g", buf.ap().rearrange("p (k m j) -> p k m j", k=8, j=128),
                       WTS[name].rearrange("(k p) (m j) -> p k m j", p=128, j=128))

        tok_wq = wload(W0, "Wq")

        wdma("sc", "s", tok_ct)
        t_sig = op("sc", lambda e: e.activation(SIG[:, :], CT[:, :], AF.Sigmoid))
        w("v", "sc", t_sig)
        t_silu = op("v", lambda e: e.tensor_mul(SILU[:, :], CT[:, :], SIG[:, :]))

        rowfree = [0, 0]
        wdma("t", "g", tok_consts)
        for n in range(12):
            key = ("PA", n % 2)
            bank = PA[0:1, 0:512] if n % 2 == 0 else PA[0:1, 512:1024]
            rowslot = ROW0 if n % 2 == 0 else ROW1
            wdma("t", "g", wmod_tok[9])
            w("t", "v", t_silu)
            use_bank(key)
            for k in range(8):
                mm(bank, SILU[:, k:k + 1], wmod_rhs(k, n),
                   start=(k == 0), stop=(k == 7), sem=(k == 7))
            w("v", "t", Cn["t"])
            w("v", "t", rowfree[n % 2])
            tv = op("v", lambda e, bank=bank, rowslot=rowslot: e.tensor_copy(
                rowslot, bank))
            free_bank(key, "v", tv)
            w("t", "v", tv)
            use_bank(("PTB7",))
            for c in range(4):
                op("t", lambda e, c=c, n=n, rowslot=rowslot: e.transpose(
                    PTB[:, 8 * (n % 2) + 2 * c: 8 * (n % 2) + 2 * c + 1],
                    rowslot[0:1, 128 * c:128 * c + 128], I128[0:1, 0:1]),
                    sem=(c == 3))
            rowfree[n % 2] = Cn["t"]
            w("v", "t", Cn["t"])
            vv, cc4 = n // 2, 4 * (n % 2)
            ptb_view = PTB[:, 8 * (n % 2): 8 * (n % 2) + 8].rearrange(
                "p (c two) -> p c two", two=2)[:, :, 0:1]
            tvv = op("v", lambda e, vv=vv, cc4=cc4, ptb_view=ptb_view:
                     e.tensor_copy(
                         MODT[:, 8 * vv + cc4: 8 * vv + cc4 + 4], ptb_view))
            free_bank(("PTB7",), "v", tvv)
        t_modr = Cn["v"]

        wdma("v", "s", tok_vecs)
        fence("v")
        op("v", lambda e: e.tensor_scalar(DERV[:, 0:8], MODT[:, 8:16], 1.0, None,
                                          op0=ALU.add))
        op("v", lambda e: e.tensor_copy(DERV[:, 8:16], MODT[:, 0:8]))
        op("v", lambda e: e.tensor_scalar(DERV[:, 16:24], MODT[:, 32:40], 1.0, None,
                                          op0=ALU.add))
        op("v", lambda e: e.tensor_copy(DERV[:, 24:32], MODT[:, 24:32]))
        for (src0, dst) in ((16, 32), (40, 40)):
            op("v", lambda e, s=src0: e.tensor_scalar(
                STF[:, 8:16], MODT[:, s:s + 8], 0.0, None, op0=ALU.is_ge))
            fence("v")
            op("v", lambda e: e.tensor_scalar(
                STF[:, 8:16], STF[:, 8:16], 2.0, -1.0, op0=ALU.mult, op1=ALU.add))
            fence("v")
            op("v", lambda e, s=src0: e.tensor_mul(
                STF[:, 0:8], MODT[:, s:s + 8], STF[:, 8:16]))
            fence("v")
            op("v", lambda e: e.tensor_scalar(
                STF[:, 0:8], STF[:, 0:8], 0.25, None, op0=ALU.max))
            fence("v")
            op("v", lambda e, d=dst: e.tensor_mul(
                DERV[:, d:d + 8], STF[:, 0:8], STF[:, 8:16]))
            fence("v")
        op("v", lambda e: e.tensor_mul(DERV[:, 48:56], DERV[:, 32:40],
                                       VECS[:, 40:48]))
        t_derv = op("v", lambda e: e.tensor_mul(DERV[:, 56:64], DERV[:, 32:40],
                                                VECS[:, 48:56]))
        if "modT" in dbg:
            w("s", "v", t_derv)
            dma("s", dbg["modT"][:, :], MODT[:, :])
        if "DERV" in dbg:
            w("s", "v", t_derv)
            dma("s", dbg["DERV"][:, :], DERV[:, :])
        cut("p0a")

        wdma("v", "s", tok_idx)
        op("v", lambda e: e.tensor_copy(IDXF[:, :], IDXS[:, :]))
        fence("v")
        op("v", lambda e: e.tensor_scalar(IDXF[:, :], IDXF[:, :], 0.0, 7.0,
                                          op0=ALU.max, op1=ALU.min))
        fence("v")
        t_idxb = op("v", lambda e: e.tensor_copy(IDXB[:, :], IDXF[:, :]))
        wdma("t", "g", tok_ones1)
        w("t", "v", t_idxb)
        use_bank(("PD",))
        mm(PD[0:128, 0:2], ONES1[0:1, 0:128], IDXB[0:1, 0:2], True, True, sem=True)
        w("v", "t", Cn["t"])
        wdma("v", "s", tok_p16)
        t_ohe = op("v", lambda e: e.tensor_scalar(OHE[:, :], PD[0:128, 0:2],
                                                  P16T[:, 0:1], None,
                                                  op0=ALU.is_equal))
        t_ohz = op("g", lambda e: e.memset(OH[:, :], 0))
        wdma("v", "g", tok_consts)
        w("v", "g", t_ohz)
        fence("v")
        for k in range(2):
            op("v", lambda e, k=k: e.tensor_scalar(
                OH[:, 16 * k:16 * k + 16], I16t, OHE[:, k:k + 1], None,
                op0=ALU.mult))
        t_oh = Cn["v"]
        free_bank(("PD",), "v", t_ohe)
        wdma("t", "g", tok_tkf)
        use_bank(("PD",))
        mm(PD[0:128, 2:4], ONES1[0:1, 0:128], TKS[0:1, 0:2], True, True, sem=True)
        w("v", "t", Cn["t"])
        t_tkb = op("v", lambda e: e.tensor_copy(TKB[:, :], PD[0:128, 2:4]))
        free_bank(("PD",), "v", t_tkb)

        wdma("t", "g", tok_aall)
        w("t", "v", t_oh)
        for d in range(DC):
            key = ("PC", d % 2)
            bank = PC[:, 0:32] if d % 2 == 0 else PC[:, 512:544]
            use_bank(key)
            mm(bank, W1[:, 128 * d:128 * (d + 1)], OH[:, :], True, True, sem=True)
            w("v", "t", Cn["t"])
            tv = op("v", lambda e, d=d, bank=bank: e.tensor_copy(
                ASEL[:, 32 * d:32 * (d + 1)], bank))
            free_bank(key, "v", tv)
        wdma("t", "g", tok_ball)
        for n in range(8):
            use_bank(("PD",))
            mm(PD[0:32, 0:512], OH[:, :], ARENA[:, 512 * n:512 * (n + 1)],
               True, True, sem=True)
            w("v", "t", Cn["t"])
            tv = op("v", lambda e, n=n: e.tensor_copy(
                ARENA[0:32, 512 * n:512 * (n + 1)], PD[0:32, 0:512]))
            free_bank(("PD",), "v", tv)
        t_bsel = Cn["v"]
        t_p0b = Cn["t"]
        if "OH" in dbg:
            w("s", "v", t_bsel)
            dma("s", dbg["OH"][:, :], OH[:, :])
        if "BselT" in dbg:
            w("s", "v", t_bsel)
            dma("s", dbg["BselT"][:, :], ARENA[:, :])
        if "ASEL" in dbg:
            w("s", "v", t_bsel)
            dma("s", dbg["ASEL"][:, :], ASEL[:, :])
        cut("p0b")

        wdma("t", "g", tok_x)

        def ln_stats(src_chunk_ap, eps, sq_scratch):
            use_bank(("PA", 0))
            for d in range(DC):
                mm(PA[:, 0:512], Jc, src_chunk_ap(d), start=(d == 0),
                   stop=(d == DC - 1), sem=(d == DC - 1))
            t_mu_mm = Cn["t"]
            w("sc", "t", Cn["t"])
            for d in range(DC):
                op("sc", lambda e, d=d: e.activation(
                    sq_scratch(d), src_chunk_ap(d), AF.Square))
            t_sq = Cn["sc"]
            use_bank(("PA", 1))
            w("t", "sc", t_sq)
            for d in range(DC):
                mm(PA[:, 512:1024], Jc, sq_scratch(d), start=(d == 0),
                   stop=(d == DC - 1), sem=(d == DC - 1))
            t_ss_mm = Cn["t"]
            w("v", "t", t_mu_mm)
            op("v", lambda e: e.tensor_copy(MUT, PA[:, 0:512]))
            t_mucp = op("v", lambda e: e.tensor_copy(STF[:, 0:512], PA[:, 0:512]))
            free_bank(("PA", 0), "v", t_mucp)
            w("sc", "v", t_mucp)
            t_mu2 = op("sc", lambda e: e.activation(STF[:, 512:1024],
                                                    STF[:, 0:512], AF.Square))
            w("v", "t", t_ss_mm)
            w("v", "sc", t_mu2)
            t_var = op("v", lambda e: e.tensor_sub(STF[:, 0:512], PA[:, 512:1024],
                                                   STF[:, 512:1024]))
            free_bank(("PA", 1), "v", t_var)
            w("sc", "v", t_var)
            wdma("sc", "s", tok_p16)
            op("sc", lambda e, eps=eps: e.activation(STF[:, 512:1024],
                                                     STF[:, 0:512], AF.Ln,
                                                     bias=float(eps)))
            t_rst = op("sc", lambda e: e.activation(RST, STF[:, 512:1024],
                                                    AF.Exp, scale=-0.5))
            return t_rst

        for w_ in range(NW):
            xs = lambda d, w_=w_: SX[:, 2048 * d + 512 * w_:
                                     2048 * d + 512 * w_ + 512]
            sqs = lambda d: SAa[:, 512 * d: 512 * d + 512]
            t_rst = ln_stats(xs, EPS_OUT, sqs)
            w("v", "sc", t_rst)
            for d in range(DC):
                hs = SH[:, 2048 * d + 512 * w_: 2048 * d + 512 * w_ + 512]
                op("v", lambda e, d=d, w_=w_: e.tensor_sub(
                    TMPE[:, 0:512],
                    SX[:, 2048 * d + 512 * w_: 2048 * d + 512 * w_ + 512],
                    MUT))
                op("v", lambda e: e.tensor_mul(TMPE[:, 0:512], TMPE[:, 0:512],
                                               RST))
                op("v", lambda e, d=d, hs=hs: e.tensor_scalar(
                    hs, TMPE[:, 0:512], DERV[:, d:d + 1],
                    DERV[:, 8 + d:8 + d + 1], op0=ALU.mult, op1=ALU.add))
        t_h = Cn["v"]
        tok_dbg_h = None
        if "hT" in dbg:
            w("s", "v", t_h)
            tok_dbg_h = dma("s", dbg["hT"][:, :], SH[:, :])
        cut("p1")

        tok_wq2 = wload(W1, "Wq2", gate=("t", t_p0b))

        def fm_proj(wblk, hoff_w, dstc, bias_col):
            for m in range(DC):
                key = ("PB", m % 2)
                bank = PB[:, 0:512] if m % 2 == 0 else PB[:, 512:1024]
                use_bank(key)
                for k in range(DC):
                    mm(bank, wblk(k, m),
                       SH[:, 2048 * k + hoff_w: 2048 * k + hoff_w + 512],
                       start=(k == 0), stop=(k == DC - 1), sem=(k == DC - 1))
                w("sc", "t", Cn["t"])
                tv = op("sc", lambda e, m=m, bank=bank: e.activation(
                    dstc(m), bank, AF.Identity,
                    bias=VECS[:, bias_col * 8 + m: bias_col * 8 + m + 1]))
                free_bank(key, "sc", tv)

        def tok_proj(sax, dstsb, wbuf, base_tile):
            for tt in range(4):
                i = base_tile + tt
                use_bank(("PC", 0), ("PC", 1))
                for half in range(2):
                    bank = PC[:, 0:512] if half == 0 else PC[:, 512:1024]
                    for k in range(DC):
                        mm(bank,
                           sax[:, 512 * k + 128 * tt: 512 * k + 128 * tt + 128],
                           wbuf[:, 1024 * k + 512 * half:
                                1024 * k + 512 * half + 512],
                           start=(k == 0), stop=False)
                    mm(bank, ONES1[0:1, 0:128],
                       BROW[0:1, 512 * half: 512 * half + 512],
                       start=False, stop=True, sem=(half == 1))
                t_mm = Cn["t"]
                w("sc", "t", t_mm)
                op("sc", lambda e, i=i: e.activation(
                    dstsb[:, 1024 * i:1024 * (i + 1)], PC[:, :], AF.Relu))
                t_relu = Cn["sc"]
                w("v", "sc", t_relu)
                t_min = op("v", lambda e: e.tensor_scalar(
                    TMPE, PC[:, 0:1024], 0.0, None, op0=ALU.min))
                free_bank(("PC", 0), "v", t_min)
                free_bank(("PC", 1), "v", t_min)
                w("sc", "v", t_min)
                t_exp = op("sc", lambda e: e.activation(TMPE, TMPE, AF.Exp))
                w("v", "sc", t_exp)
                op("v", lambda e, i=i: e.tensor_add(
                    dstsb[:, 1024 * i:1024 * (i + 1)],
                    dstsb[:, 1024 * i:1024 * (i + 1)], TMPE))

        wdma("t", "g", tok_wq)
        wdma("t", "g", tok_brow)
        w("t", "v", t_h)
        for w_ in range(NW):
            sax = SAa if w_ % 2 == 0 else SAb
            dstc = lambda m, sax=sax: sax[:, 512 * m: 512 * m + 512]
            fm_proj(W0b, 512 * w_, dstc, 0)
            wdma("t", "g", tok_wq2)
            w("t", "sc", Cn["sc"])
            tok_proj(sax, SB, W1, 4 * w_)
        t_qchain = Cn["t"]
        t_q2v = Cn["v"]
        cut("qchain")
        tok_wk = wload(W0, "Wk", gate=("t", t_qchain))
        w("g", "v", t_q2v)
        tok_brow2 = dma("g", BROW[:, :], brow_in[0:1, D:2 * D])

        tok_wk2 = None
        for w_ in range(NW):
            sax = SAa if w_ % 2 == 0 else SAb
            wdma("t", "g", tok_wk)
            dstc = lambda m, sax=sax: sax[:, 512 * m: 512 * m + 512]
            fm_proj(W0b, 512 * w_, dstc, 1)
            if w_ == 0:
                tok_wk2 = wload(W1, "Wk2", gate=("t", Cn["t"]))
            wdma("t", "g", tok_wk2)
            wdma("t", "g", tok_brow2)
            w("t", "sc", Cn["sc"])
            tok_proj(sax, SC, W1, 4 * w_)
            for tt in range(4):
                i = 4 * w_ + tt
                op("v", lambda e, i=i: e.tensor_mul(
                    TMPF, SB[:, 1024 * i:1024 * (i + 1)],
                    SC[:, 1024 * i:1024 * (i + 1)]))
                op("v", lambda e, i=i: e.tensor_reduce(
                    QKS[:, 16 * i:16 * (i + 1)],
                    TMPF.rearrange("p (h d) -> p h d", d=DH),
                    axis=AX.X, op=ALU.add))
        t_k2done = Cn["v"]
        t_kchain = Cn["t"]
        tok_wv = wload(W0, "Wv", gate=("t", t_kchain))
        tok_dbg_q2 = tok_dbg_k2 = tok_dbg_qks = None
        if "q2" in dbg:
            w("s", "v", t_k2done)
            tok_dbg_q2 = dma("s", dbg["q2"][:, :], SB[:, :])
        if "k2" in dbg:
            w("s", "v", t_k2done)
            tok_dbg_k2 = dma("s", dbg["k2"][:, :], SC[:, :])
        if "qks" in dbg:
            w("s", "v", t_k2done)
            tok_dbg_qks = dma("s", dbg["qks"][:, :], QKS[:, :])
        cut("kchain")

        w("t", "v", t_k2done)
        use_bank(("PA", 0), ("PA", 1))
        for h in range(H):
            for i in range(NT):
                mm(PA[0:64, 64 * h:64 * (h + 1)],
                   SC[:, 1024 * i + 64 * h: 1024 * i + 64 * h + 64],
                   SB[:, 1024 * i + 64 * h: 1024 * i + 64 * h + 64],
                   start=(i == 0), stop=(i == NT - 1),
                   sem=(i == NT - 1 and h in (7, 15)))
        t_sdone = Cn["t"]
        w("v", "t", t_sdone)
        op("v", lambda e: e.tensor_copy(SPART[:, 0:512], PA[0:64, 0:512]))
        t_sev = op("v", lambda e: e.tensor_copy(SPART[:, 512:1024],
                                                PA[0:64, 512:1024]))
        free_bank(("PA", 0), "v", t_sev)
        free_bank(("PA", 1), "v", t_sev)
        cut("sphase")
        w("g", "v", t_sev)
        tok_ccin = dma("g", cc_in[:, :], SPART[:, :])
        wdma("g", "g", tok_ccin)
        t_cc = op("g", lambda e: e.collective_compute(
            "AllReduce", ALU.add,
            replica_groups=[[0, 1], [2, 3], [4, 5], [6, 7]],
            ins=[cc_in[:, :]], outs=[cc_out[:, :]]))

        cut("cc")
        for i in range(NT):
            op("v", lambda e, i=i: e.tensor_scalar(
                STF[:, 16 * i:16 * (i + 1)], QKS[:, 16 * i:16 * (i + 1)],
                float(EPS_QK), None, op0=ALU.add))
        t_qe = Cn["v"]
        w("sc", "v", t_qe)
        op("sc", lambda e: e.activation(STF[:, 512:768], STF[:, 0:256], AF.Ln))
        fence("sc")
        t_recip = op("sc", lambda e: e.activation(
            TMPF[:, 0:256], STF[:, 512:768], AF.Exp, scale=-1.0))
        if "recip" in dbg:
            w("s", "sc", t_recip)
            dma("s", dbg["recip"][:, :], TMPF[:, 0:256])
        if "qks2" in dbg:
            w("s", "v", t_qe)
            dma("s", dbg["qks2"][:, :], QKS[:, :])
        w("t", "sc", t_recip)
        for i in range(NT):
            po = 128 * (i % 8)
            use_bank(("PTB7",))
            op("t", lambda e, i=i, po=po: e.transpose(
                PTB[0:16, po:po + 128], TMPF[:, 16 * i:16 * (i + 1)], I128),
                sem=True)
            w("v", "t", Cn["t"])
            tv = op("v", lambda e, i=i, po=po: e.tensor_copy(
                QKSRT[:, 128 * i:128 * (i + 1)], PTB[0:16, po:po + 128]))
            free_bank(("PTB7",), "v", tv)
        t_qksr = Cn["v"]
        if "qksrT" in dbg:
            w("s", "v", t_qksr)
            dma("s", dbg["qksrT"][:, :], POOL[:, 3072:5120])

        tok_wv2 = None
        for w_ in range(NW):
            sax = SAa if w_ % 2 == 0 else SAb
            wdma("t", "g", tok_wv)
            dstc = lambda m, sax=sax: sax[:, 512 * m: 512 * m + 512]
            fm_proj(W0b, 512 * w_, dstc, 2)
            if w_ == 0:
                tok_wv2 = wload(W1, "Wv2", gate=("t", Cn["t"]))
            wdma("t", "g", tok_wv2)
            w("t", "sc", Cn["sc"])
            for m in range(DC):
                key = ("PC", m % 2)
                bank = PC[:, 0:512] if m % 2 == 0 else PC[:, 512:1024]
                use_bank(key)
                for k in range(DC):
                    mm(bank, W1b(k, m), sax[:, 512 * k: 512 * k + 512],
                       start=(k == 0), stop=(k == DC - 1), sem=(k == DC - 1))
                w("sc", "t", Cn["t"])
                w("sc", "t", t_sdone)
                if tok_dbg_q2:
                    wdma("sc", "s", tok_dbg_q2)
                if tok_dbg_qks:
                    wdma("sc", "s", tok_dbg_qks)
                w("sc", "v", t_k2done)
                tv = op("sc", lambda e, m=m, w_=w_, bank=bank: e.activation(
                    SB[:, 2048 * m + 512 * w_: 2048 * m + 512 * w_ + 512], bank,
                    AF.Identity, bias=VECS[:, 3 * 8 + m: 3 * 8 + m + 1]))
                free_bank(key, "sc", tv)
        t_v2 = Cn["sc"]
        t_vchain = Cn["t"]
        tok_wo = wload(W0, "Wo", gate=("t", t_vchain))

        wdma("t", "g", tok_sel)
        w("t", "v", t_qksr)
        for c in range(DC):
            for nn in range(NW):
                j = c * NW + nn
                key = ("PB", j % 2)
                bank = PB[:, 0:512] if j % 2 == 0 else PB[:, 512:1024]
                use_bank(key)
                mm(bank, SEL[0:16, 128 * c:128 * (c + 1)],
                   QKSRT[:, 512 * nn:512 * (nn + 1)], True, True, sem=True)
                w("v", "t", Cn["t"])
                w("v", "sc", t_v2)
                tv = op("v", lambda e, c=c, nn=nn, bank=bank: e.tensor_mul(
                    SB[:, 2048 * c + 512 * nn: 2048 * c + 512 * nn + 512],
                    SB[:, 2048 * c + 512 * nn: 2048 * c + 512 * nn + 512], bank))
                free_bank(key, "v", tv)
        t_v2s = Cn["v"]
        t_bcast_t = Cn["t"]
        tok_dbg_v2 = None
        if "v2" in dbg:
            w("s", "v", t_v2s)
            tok_dbg_v2 = dma("s", dbg["v2"][:, :], SB[:, :])
        cut("v2s")

        w("g", "g", t_cc)
        w("g", "t", t_bcast_t)
        dma("g", STOT[:, :], cc_out[:, :])
        tok_stot = dma("g", STOT64[:, :], cc_out[:, :])
        if "Stot" in dbg:
            wdma("s", "g", tok_stot)
            dma("s", dbg["Stot"][:, :], STOT[:, :])

        wdma("t", "g", tok_stot)
        w("t", "v", t_v2s)
        for nn in range(NW):
            sax = SAa if nn % 2 == 0 else SAb
            for c in range(DC):
                for hh in range(2):
                    h = 2 * c + hh
                    j = c * 2 + hh
                    key = ("PB", j % 2)
                    bank = PB[0:64, 0:512] if j % 2 == 0 else PB[0:64, 512:1024]
                    use_bank(key)
                    ST = STOT if hh == 0 else STOT64
                    mm(bank, ST[:, 64 * h:64 * (h + 1)],
                       SB[64 * hh:64 * (hh + 1),
                          2048 * c + 512 * nn: 2048 * c + 512 * nn + 512],
                       True, True, sem=True)
                    w("sc", "t", Cn["t"])
                    tv = op("sc", lambda e, c=c, hh=hh, bank=bank, sax=sax:
                            e.copy(
                                sax[64 * hh:64 * (hh + 1),
                                    512 * c:512 * (c + 1)], bank))
                    free_bank(key, "sc", tv)
            t_aout = Cn["sc"]
            wdma("t", "g", tok_wo)
            w("t", "sc", t_aout)
            for m in range(DC):
                key = ("PC", m % 2)
                bank = PC[:, 0:512] if m % 2 == 0 else PC[:, 512:1024]
                tslot = POOL[:, 1024 + 512 * (m % 2):1024 + 512 * (m % 2) + 512]
                use_bank(key)
                for k in range(DC):
                    mm(bank, W0b(k, m), sax[:, 512 * k: 512 * k + 512],
                       start=(k == 0), stop=(k == DC - 1), sem=(k == DC - 1))
                w("sc", "t", Cn["t"])
                wdma("sc", "g", tok_ccin)
                w("sc", "v", Cn["v"])
                op("sc", lambda e, bank=bank, tslot=tslot, m=m: e.activation(
                    tslot, bank, AF.Identity,
                    bias=VECS[:, 4 * 8 + m: 4 * 8 + m + 1]))
                t_bo = Cn["sc"]
                free_bank(key, "sc", t_bo)
                w("v", "sc", t_bo)
                w("v", "t", t_sdone)
                if tok_dbg_k2:
                    wdma("v", "s", tok_dbg_k2)
                op("v", lambda e, m=m, nn=nn, tslot=tslot: e.tensor_add(
                    SC[:, 2048 * m + 512 * nn: 2048 * m + 512 * nn + 512],
                    tslot,
                    SH[:, 2048 * m + 512 * nn: 2048 * m + 512 * nn + 512]))
        t_y = Cn["v"]
        t_p3t = Cn["t"]
        tok_dbg_y = None
        if "yT" in dbg:
            w("s", "v", t_y)
            tok_dbg_y = dma("s", dbg["yT"][:, :], SC[:, :])
        cut("p3")

        for w_ in range(NW):
            ys = lambda d, w_=w_: SC[:, 2048 * d + 512 * w_:
                                     2048 * d + 512 * w_ + 512]
            sqs = lambda d: SAa[:, 512 * d: 512 * d + 512]
            if w_ == 0:
                w("sc", "v", t_y)
                w("t", "v", t_y)
            t_rst = ln_stats(ys, EPS_IN, sqs)
            w("v", "sc", t_rst)
            for d in range(DC):
                op("v", lambda e, d=d, w_=w_: e.tensor_sub(
                    TMPE[:, 0:512],
                    SC[:, 2048 * d + 512 * w_: 2048 * d + 512 * w_ + 512],
                    MUT))
                op("v", lambda e: e.tensor_mul(TMPE[:, 0:512], TMPE[:, 0:512],
                                               RST))
                op("v", lambda e, d=d: e.tensor_scalar(
                    TMPE[:, 0:512], TMPE[:, 0:512], DERV[:, 48 + d:48 + d + 1],
                    DERV[:, 56 + d:56 + d + 1], op0=ALU.mult, op1=ALU.add))
                op("v", lambda e, d=d, w_=w_: e.tensor_add(
                    SX[:, 2048 * d + 512 * w_: 2048 * d + 512 * w_ + 512],
                    SX[:, 2048 * d + 512 * w_: 2048 * d + 512 * w_ + 512],
                    TMPE[:, 0:512]))
        t_x1 = Cn["v"]
        t_p4t = Cn["t"]
        tok_dbg_x1 = None
        if "x1T" in dbg:
            w("s", "v", t_x1)
            tok_dbg_x1 = dma("s", dbg["x1T"][:, :], SX[:, :])
        cut("p4")

        wi_tok = []
        for kc in range(8):
            if kc < 4:
                dstap = SB[:, 4096 * kc:4096 * (kc + 1)]
                gate = ("t", t_p3t)
                wdma("g", "s", tok_dbg_v2)
            elif kc < 6:
                dstap = W1[:, 4096 * (kc - 4):4096 * (kc - 4) + 4096]
                gate = ("t", t_vchain)
            elif kc == 6:
                dstap = SAa[:, :]
                gate = ("t", t_p4t)
            else:
                dstap = SAb[:, :]
                gate = ("t", t_p4t)
            w("g", gate[0], gate[1])
            wi_tok.append(dma("g", dstap, wiT_in[128 * kc:128 * (kc + 1), :]))
        wo_tok = []
        for fc in range(FC):
            if fc < 16:
                dstap = SH[:, 1024 * fc:1024 * (fc + 1)]
                w("g", "v", t_y)
                wdma("g", "s", tok_dbg_h)
            else:
                dstap = SC[:, 1024 * (fc - 16):1024 * (fc - 16) + 1024]
                w("g", "t", t_p4t)
                w("g", "v", t_x1)
                wdma("g", "s", tok_dbg_y)
            wo_tok.append(dma("g", dstap, woT_in[128 * fc:128 * (fc + 1), :]))

        for tc in range(TCN):
            xs2 = lambda d, tc=tc: SX[:, 2048 * d + 256 * tc:
                                      2048 * d + 256 * tc + 256]
            use_bank(("PB", 0), ("PB", 1))
            w("t", "v", t_x1)
            wdma("v", "s", tok_dbg_x1)
            for d in range(DC):
                mm(PB[:, 0:256], Jc, xs2(d), start=(d == 0), stop=(d == DC - 1),
                   sem=(d == DC - 1))
            t_mu_mm = Cn["t"]
            w("sc", "t", Cn["t"])
            for d in range(DC):
                slot = XSQ[:, 256 * (d % 2):256 * (d % 2) + 256]
                if d >= 2:
                    w("sc", "t", Cn["t"])
                op("sc", lambda e, d=d, slot=slot, tc=tc: e.activation(
                    slot, xs2(d), AF.Square))
                w("t", "sc", Cn["sc"])
                mm(PB[:, 512:768], Jc, slot, start=(d == 0), stop=(d == DC - 1),
                   sem=True)
            t_ss_mm = Cn["t"]
            w("v", "t", t_mu_mm)
            op("v", lambda e: e.tensor_copy(MUT2, PB[:, 0:256]))
            t_mucp = op("v", lambda e: e.tensor_copy(STF[:, 0:256], PB[:, 0:256]))
            w("sc", "v", t_mucp)
            t_mu2 = op("sc", lambda e: e.activation(STF[:, 256:512],
                                                    STF[:, 0:256], AF.Square))
            w("v", "t", t_ss_mm)
            w("v", "sc", t_mu2)
            t_var = op("v", lambda e: e.tensor_sub(STF[:, 0:256], PB[:, 512:768],
                                                   STF[:, 256:512]))
            free_bank(("PB", 0), "v", t_var)
            free_bank(("PB", 1), "v", t_var)
            w("sc", "v", t_var)
            wdma("sc", "s", tok_p16)
            op("sc", lambda e: e.activation(STF[:, 256:512], STF[:, 0:256],
                                            AF.Ln, bias=float(EPS_OUT)))
            t_rst2 = op("sc", lambda e: e.activation(RST2, STF[:, 256:512],
                                                     AF.Exp, scale=-0.5))
            if tc == 0:
                cut("ln2s")
            w("v", "sc", t_rst2)
            for d in range(DC):
                op("v", lambda e, d=d, tc=tc: e.tensor_sub(
                    TMP6[:, 0:256], xs2(d), MUT2))
                op("v", lambda e: e.tensor_mul(TMP6[:, 0:256], TMP6[:, 0:256],
                                               RST2))
                op("v", lambda e, d=d: e.tensor_scalar(
                    H2TC[:, 256 * d:256 * (d + 1)], TMP6[:, 0:256],
                    DERV[:, 16 + d:16 + d + 1], DERV[:, 24 + d:24 + d + 1],
                    op0=ALU.mult, op1=ALU.add))
            t_h2 = Cn["v"]
            if tc == 0 and "h2tc0" in dbg:
                w("s", "v", t_h2)
                dma("s", dbg["h2tc0"][:, :], POOL[:, 0:2048])
            w("t", "v", t_h2)
            use_bank(("PD",))
            for k in range(DC):
                mm(PD[0:32, 0:256], ASEL[:, 32 * k:32 * (k + 1)],
                   H2TC[:, 256 * k:256 * (k + 1)], start=(k == 0),
                   stop=(k == DC - 1), sem=(k == DC - 1))
            w("v", "t", Cn["t"])
            op("v", lambda e: e.tensor_copy(UB[0:32, 0:256], PD[0:32, 0:256]))
            op("v", lambda e: e.tensor_copy(UB[0:32, 256:512], PD[0:32, 0:256]))
            t_u01 = op("v", lambda e: e.tensor_scalar(
                UB[0:16, 256:512], PD[0:16, 0:256], -1.0, None, op0=ALU.mult))
            free_bank(("PD",), "v", t_u01)
            if tc == 0 and "u01" in dbg:
                w("s", "v", t_u01)
                dma("s", dbg["u01"][:, :], UB[:, :])
            if tc == 0:
                cut("u")
            w("t", "v", t_u01)
            for kc in range(8):
                wdma("t", "g", wi_tok[kc])
            for fc in range(FC):
                key = ("PA", fc % 2)
                bank = PA[:, 0:256] if fc % 2 == 0 else PA[:, 512:768]
                r0s = R0P[:, 256 * (fc % 2):256 * (fc % 2) + 256]
                r1s = R1P[:, 256 * (fc % 2):256 * (fc % 2) + 256]
                use_bank(key)
                mm(bank, ARENA[0:16, 128 * fc:128 * fc + 128], UB[0:16, 0:256],
                   start=True, stop=False)
                for k in range(DC):
                    mm(bank, wiTblk(k, fc), H2TC[:, 256 * k:256 * (k + 1)],
                       start=False, stop=False, sem=(k == DC - 1))
                t_head = Cn["t"]
                w("sc", "t", t_head)
                w("sc", "v", Cn["v"])
                op("sc", lambda e, bank=bank, r0s=r0s: e.activation(
                    r0s, bank, AF.Relu, scale=TKB[:, 0:1]))
                t_ev0 = Cn["sc"]
                w("t", "sc", t_ev0)
                mm(bank, ARENA[0:32, 128 * fc:128 * fc + 128], UB[0:32, 256:512],
                   start=False, stop=True, sem=True)
                w("sc", "t", Cn["t"])
                t_ev1 = op("sc", lambda e, bank=bank, r1s=r1s: e.activation(
                    r1s, bank, AF.Relu, scale=TKB[:, 1:2]))
                free_bank(key, "sc", t_ev1)
                w("v", "sc", t_ev1)
                if tc == 0:
                    w("v", "t", t_p3t)
                op("v", lambda e, fc=fc, r0s=r0s, r1s=r1s: e.tensor_add(
                    W0[:, 256 * fc:256 * (fc + 1)], r0s, r1s))
            t_interC = Cn["v"]
            if tc == 0 and "interC0" in dbg:
                w("s", "v", t_interC)
                dma("s", dbg["interC0"][:, :], W0[:, :])
            if tc == 0:
                cut("aphase")
            w("t", "v", t_interC)
            for fc in range(FC):
                wdma("t", "g", wo_tok[fc])
            dp_bank = [(PB, 0), (PB, 256), (PB, 512), (PB, 768),
                       (PC, 0), (PC, 256), (PC, 512), (PC, 768)]
            for dp in range(DC):
                bk, off = dp_bank[dp]
                key = ("PB", off // 512) if bk is PB else ("PC", off // 512)
                if dp % 2 == 0:
                    use_bank(key)
                for fc in range(FC):
                    mm(bk[:, off:off + 256], woTblk(fc, dp),
                       W0[:, 256 * fc:256 * (fc + 1)], start=(fc == 0),
                       stop=(fc == FC - 1), sem=(fc == FC - 1))
                if dp % 2 == 1:
                    t_dp = Cn["t"]
                    w("v", "t", t_dp)
                    for d2 in (dp - 1, dp):
                        bk2, off2 = dp_bank[d2]
                        op("v", lambda e, bk2=bk2, off2=off2, d2=d2: e.tensor_scalar(
                            TMP6[:, 256:512], bk2[:, off2:off2 + 256],
                            DERV[:, 40 + d2:40 + d2 + 1], None, op0=ALU.mult))
                        tv = op("v", lambda e, d2=d2, tc=tc: e.tensor_add(
                            SX[:, 2048 * d2 + 256 * tc:
                               2048 * d2 + 256 * tc + 256],
                            SX[:, 2048 * d2 + 256 * tc:
                               2048 * d2 + 256 * tc + 256],
                            TMP6[:, 256:512]))
                        free_bank(key, "v", tv)
            t_x2 = Cn["v"]
            if tc == 0:
                cut("tc0")
            w("g", "v", t_x2)
            dma("g",
                out_xT.rearrange("(d p) t -> p d t", p=128)[:, :,
                                                            256 * tc:256 * (tc + 1)],
                SX.ap().rearrange("p (d t) -> p d t", d=DC)[:, :,
                                                            256 * tc:256 * (tc + 1)])

        w("g", "dg", D16["g"])
        w("s", "ds", D16["s"])

    with nc.allow_low_precision("bf16 recipe calibrated by problem tolerance"), \
            nc.Block() as blk:
        def run(name, e):
            try:
                emit(name, e)
            except _Cut:
                pass

        @blk.tensor
        def _(e):
            run("t", e)

        @blk.vector
        def _(e):
            run("v", e)

        @blk.scalar
        def _(e):
            run("sc", e)

        @blk.gpsimd
        def _(e):
            run("g", e)

        @blk.sync
        def _(e):
            run("s", e)

    return nc, dbg


_NC_CACHE = {}


def _get_nc(debug=(), stop_after=None):
    key = (tuple(sorted(debug)), stop_after)
    if key not in _NC_CACHE:
        _NC_CACHE[key] = build_nc(debug=key[0], stop_after=stop_after)
    return _NC_CACHE[key]


def _host_inputs(inputs, core):
    b = core // 2
    x = np.asarray(inputs["L0_attn_x"])
    xs = x[b, 2048 * (core % 2):2048 * (core % 2 + 1), :]
    m = {}
    m["xT"] = np.ascontiguousarray(xs.T).astype(np.float32)
    for w in ("Wq", "Wk", "Wv", "Wq2", "Wk2", "Wv2", "Wo"):
        m[w] = np.ascontiguousarray(np.asarray(inputs["L0_attn_" + w],
                                               dtype=np.float32))
    vecs = [inputs["L0_attn_bq"], inputs["L0_attn_bk"], inputs["L0_attn_bv"],
            inputs["L0_attn_bv2"], inputs["L0_attn_bo"],
            inputs["L0_attn_gamma"], inputs["L0_attn_beta"]]
    m["vecs"] = np.ascontiguousarray(np.concatenate(
        [np.asarray(v, dtype=np.float32).reshape(8, 128).T for v in vecs],
        axis=1))
    m["brow"] = np.concatenate(
        [np.asarray(inputs["L0_attn_bq2"], dtype=np.float32).reshape(1, -1),
         np.asarray(inputs["L0_attn_bk2"], dtype=np.float32).reshape(1, -1)],
        axis=1)
    m["cT"] = np.ascontiguousarray(
        np.asarray(inputs["L0_c"], dtype=np.float32)[b].reshape(8, 128).T)
    m["wmod"] = np.ascontiguousarray(np.asarray(inputs["L0_wmod"],
                                                dtype=np.float32))
    m["wiT"] = np.ascontiguousarray(
        np.asarray(inputs["L0_ffn_wi_w"], dtype=np.float32).T)
    m["woT"] = np.ascontiguousarray(
        np.asarray(inputs["L0_ffn_wo_w"], dtype=np.float32).T)
    m["Aall"] = np.ascontiguousarray(
        np.asarray(inputs["L0_ffn_lora_As"], dtype=np.float32).reshape(128, 1024))
    m["BallT"] = np.ascontiguousarray(
        np.asarray(inputs["L0_ffn_lora_Bs"],
                   dtype=np.float32).transpose(0, 2, 1).reshape(128, 4096))
    m["idx"] = np.asarray(inputs["L0_ffn_top_k_indices"]).reshape(1, 2).astype(
        np.int32)
    m["tkv"] = np.asarray(inputs["L0_ffn_top_k_values"],
                          dtype=np.float32).reshape(1, 2)
    constB = np.zeros((128, 272), dtype=np.float32)
    constB[:, 0:128] = 1.0 / 1024.0
    constB[:, 128:256] = np.eye(128, dtype=np.float32)
    constB[:, 256:272] = np.tile(np.eye(16, dtype=np.float32), (8, 1))
    m["constB"] = constB
    constF = np.zeros((128, 8), dtype=np.float32)
    constF[:, 0] = np.arange(128) // 16
    constF[:, 1] = -0.5
    constF[:, 2] = 1e-5
    constF[:, 3] = 1e-3
    constF[:, 4] = 0.0
    constF[:, 5] = 1.0
    m["constF"] = constF
    sel = np.zeros((16, 1024), dtype=np.float32)
    for h in range(16):
        c, hh = h // 2, h % 2
        sel[h, 128 * c + 64 * hh:128 * c + 64 * hh + 64] = 1.0
    m["sel"] = sel
    m["ones_in"] = np.ones((1, 128), dtype=np.float32)
    return m


def kernel(**inputs):
    nc, dbg = _get_nc()
    in_maps = [_host_inputs(inputs, core) for core in range(8)]
    res = run_bass_kernel_spmd(nc, in_maps, core_ids=list(range(8)))
    outs = [np.asarray(res.results[c]["out_xT"]).T for c in range(8)]
    return np.concatenate(outs, axis=0).reshape(4, 4096, 1024).astype(np.float32)


# version 04
# speedup vs baseline: 1.3733x; 1.3733x over previous
import numpy as np
import concourse.bass as bass
import concourse.mybir as mybir
from concourse.bass_utils import run_bass_kernel_spmd

F32 = mybir.dt.float32
BF16 = mybir.dt.bfloat16
I32 = mybir.dt.int32
AF = mybir.ActivationFunctionType
ALU = mybir.AluOpType
AX = mybir.AxisListType

T = 2048
D = 1024
DC = 8
NW = 4
NT = 16
H = 16
DH = 64
FF = 4096
FC = 32
TCN = 8
EPS_IN = 1e-3
EPS_OUT = 1e-5
EPS_QK = 1e-6


class _Cut(Exception):
    pass


def build_nc(debug=(), stop_after=None):
    nc = bass.Bass("TRN2", target_bir_lowering=False)
    dbgn = set(debug)

    def din(name, shape, dtype=F32):
        return nc.dram_tensor(name, list(shape), dtype, kind="ExternalInput").ap()

    xT = din("xT", [D, T])
    WTS = {w: din(w, [D, D]) for w in ("Wq", "Wk", "Wv", "Wq2", "Wk2", "Wv2", "Wo")}
    vecs_in = din("vecs", [128, 7 * DC])
    brow_in = din("brow", [1, 2 * D])
    cT_in = din("cT", [128, DC])
    wmod_in = din("wmod", [D, 6 * D])
    wiT_in = din("wiT", [D, FF])
    woT_in = din("woT", [FF, D])
    Aall_in = din("Aall", [128, D])
    BallT_in = din("BallT", [128, FF])
    idx_in = din("idx", [1, 2], I32)
    tkv_in = din("tkv", [1, 2])
    constB_in = din("constB", [128, 272])
    constF_in = din("constF", [128, 8])
    sel_in = din("sel", [16, 1024])
    ones_in = din("ones_in", [1, 128])

    out_xT = nc.dram_tensor("out_xT", [D, T], F32, kind="ExternalOutput").ap()
    cc_in = nc.dram_tensor("cc_in", [64, H * DH], BF16, kind="Internal").ap()
    cc_out = nc.dram_tensor("cc_out", [64, H * DH], BF16, kind="Internal").ap()

    dbg = {}
    def dbgt(name, shape, dtype=F32):
        if name in dbgn:
            dbg[name] = nc.dram_tensor("dbg_" + name, list(shape), dtype,
                                       kind="ExternalOutput").ap()
    dbgt("modT", [128, 48]); dbgt("hT", [128, 16384], BF16)
    dbgt("q2", [128, 16384], BF16); dbgt("k2", [128, 16384], BF16)
    dbgt("Stot", [64, 1024], BF16); dbgt("qks", [128, 256])
    dbgt("v2", [128, 16384], BF16); dbgt("yT", [128, 16384], BF16)
    dbgt("x1T", [128, 16384], BF16); dbgt("h2tc0", [128, 2048], BF16)
    dbgt("interC0", [128, 8192], BF16); dbgt("BselT", [128, 4096], BF16)
    dbgt("u01", [64, 512], BF16); dbgt("OH", [128, 32], BF16)
    dbgt("qksrT", [128, 2048], BF16); dbgt("ASEL", [128, 256], BF16)
    dbgt("recip", [128, 256], BF16); dbgt("qks2", [128, 256])
    dbgt("DERV", [128, 64])

    A = nc.alloc_sbuf_tensor
    SX = A("SX", [128, 16384], BF16)
    SH = A("SH", [128, 16384], BF16)
    SB = A("SB", [128, 16384], BF16)
    SC = A("SC", [128, 16384], BF16)
    SAa = A("SAa", [128, 4096], BF16)
    SAb = A("SAb", [128, 4096], BF16)
    W0 = A("W0", [128, 8192], BF16)
    W1 = A("W1", [128, 8192], BF16)
    POOL = A("POOL", [128, 5120], BF16)
    CONST = A("CONSTB", [128, 272], BF16)
    P16T = A("P16T", [128, 8], F32)
    VECS = A("VECS", [128, 7 * DC], F32)
    DERV = A("DERV", [128, 8 * DC], F32)
    MODT = A("MODT", [128, 48], F32)
    CT = A("CT", [128, DC], F32)
    SIG = A("SIG", [128, DC], F32)
    SILU = A("SILU", [128, DC], BF16)
    ARENA = A("ARENA", [128, 4096], BF16)
    SEL = A("SEL", [16, 1024], BF16)
    UB = A("UB", [64, 512], BF16)
    BROW = A("BROW", [1, D], BF16)
    STF = A("STF", [128, 1024], F32)
    ASEL = A("ASEL", [128, 256], BF16)
    TKB = A("TKB", [128, 2], F32)
    IDXS = A("IDXS", [1, 2], I32)
    IDXF = A("IDXF", [1, 2], F32)
    IDXB = A("IDXB", [1, 2], BF16)
    OHE = A("OHE", [128, 2], F32)
    OH = A("OH", [128, 32], BF16)
    TKS = A("TKS", [1, 2], BF16)
    ONES1 = A("ONES1", [1, 128], BF16)

    MUT = POOL[:, 0:512]
    RST = POOL[:, 512:1024]
    TMPE = POOL[:, 1024:2048]
    TMPF = POOL[:, 2048:3072]
    QKSRT = POOL[0:16, 3072:5120]
    SPART = POOL[0:64, 1024:2048]
    ROW0 = POOL[0:1, 0:512]
    ROW1 = POOL[0:1, 512:1024]
    H2TC = POOL[:, 0:2048]
    R0P = POOL[:, 2048:2816]
    R1P = POOL[:, 2816:3584]
    MUT2 = POOL[:, 3584:3840]
    RST2 = POOL[:, 3840:4096]
    TMP6 = POOL[:, 4096:4608]
    XSQ = POOL[:, 4608:5120]
    QKS = STF[:, 768:1024]
    STOT = POOL[0:64, 3072:4096]
    STOT64 = POOL[64:128, 3072:4096]

    Jc = CONST[:, 0:128]
    I128 = CONST[:, 128:256]
    I16t = CONST[:, 256:272]

    PA = nc.alloc_psum_tensor("PA", [128, 1024], F32)
    PB = nc.alloc_psum_tensor("PB", [128, 1024], F32)
    PC = nc.alloc_psum_tensor("PC", [128, 1024], F32)
    PD = nc.alloc_psum_tensor("PD", [128, 512], F32)
    PTB = nc.alloc_psum_tensor("PTB", [128, 1024], BF16)

    sems = {}
    for name in ("t", "v", "sc", "g", "s", "dg", "ds"):
        sems[name] = nc.alloc_semaphore("sem_" + name)

    for col, val in ((1, -0.5), (2, 1e-5), (3, 1e-3), (4, 0.0), (5, 1.0)):
        nc.const_aps.register(F32, float(val), P16T[:, col:col + 1])

    def W0b(k, m):
        return W0[:, 1024 * k + 128 * m: 1024 * k + 128 * m + 128]

    def W1b(k, m):
        return W1[:, 1024 * k + 128 * m: 1024 * k + 128 * m + 128]

    def wiTblk(k, fc):
        if k < 4:
            return SB[:, 4096 * k + 128 * fc: 4096 * k + 128 * fc + 128]
        if k < 6:
            return W1[:, 4096 * (k - 4) + 128 * fc: 4096 * (k - 4) + 128 * fc + 128]
        if k == 6:
            return SAa[:, 128 * fc: 128 * fc + 128]
        return SAb[:, 128 * fc: 128 * fc + 128]

    def woTblk(fc, dp):
        if fc < 16:
            return SH[:, 1024 * fc + 128 * dp: 1024 * fc + 128 * dp + 128]
        return SC[:, 1024 * (fc - 16) + 128 * dp: 1024 * (fc - 16) + 128 * dp + 128]

    def wmod_rhs(k, n):
        if k < 6:
            base = [SB, SB, SC, SC, SH, SH][k]
            off = (k % 2) * 6144
            return base[:, off + 512 * n: off + 512 * n + 512]
        if k == 6:
            if n < 8:
                return SAa[:, 512 * n: 512 * n + 512]
            return SAb[:, 512 * (n - 8): 512 * (n - 8) + 512]
        if n < 8:
            return SB[:, 12288 + 512 * n: 12288 + 512 * n + 512]
        return SC[:, 12288 + 512 * (n - 8): 12288 + 512 * (n - 8) + 512]

    def emit(my, e):
        def cut(name):
            if stop_after == name:
                raise _Cut()
        Cn = {k: 0 for k in ("t", "v", "sc", "g", "s")}
        D16 = {"g": 0, "s": 0}
        lastw = {}
        bankfree = {}

        def op(eng2, builder, sem=True):
            if my == eng2:
                ins = builder(e)
                if sem:
                    ins.then_inc(sems[eng2], 1)
            if sem:
                Cn[eng2] += 1
            return Cn[eng2]

        def w(eng2, sem_name, val):
            if my != eng2 or val is None or val <= 0:
                return
            if lastw.get(sem_name, -1) >= val:
                return
            lastw[sem_name] = val
            e.wait_ge(sems[sem_name], val)

        def dma(q, out, in_, **kw):
            D16[q] += 16
            if my == q:
                e.dma_start(out=out, in_=in_, **kw).then_inc(
                    sems["dg" if q == "g" else "ds"], 16)
            return D16[q]

        def wdma(eng2, q, tok):
            if tok:
                w(eng2, "dg" if q == "g" else "ds", tok)

        def fence(eng2):
            w(eng2, eng2, Cn[eng2])

        def mm(out, lhsT, rhs, start, stop, sem=False):
            return op("t", lambda e: e.matmul(out, lhsT, rhs, start=start,
                                              stop=stop), sem=sem)

        def use_bank(*keys):
            for k in keys:
                if k in bankfree:
                    eng2, tok = bankfree[k]
                    w("t", eng2, tok)

        def free_bank(key, eng2, tok):
            bankfree[key] = (eng2, tok)

        tok_p16 = dma("s", P16T[:, :], constF_in[:, :])
        tok_vecs = dma("s", VECS[:, :], vecs_in[:, :])
        tok_ct = dma("s", CT[:, :], cT_in[:, :])
        tok_idx = dma("s", IDXS[:, :], idx_in[:, :])

        tok_consts = dma("g", CONST[:, :], constB_in[:, :])
        tok_ones1 = dma("g", ONES1[:, :], ones_in[:, :])
        tok_sel = dma("g", SEL[:, :], sel_in[:, :])
        tok_brow = dma("g", BROW[:, :], brow_in[0:1, 0:D])
        wmod_tok = []
        for k in range(6):
            dsts = [SB, SB, SC, SC, SH, SH][k]
            off = (k % 2) * 6144
            wmod_tok.append(dma("g", dsts[:, off:off + 6144],
                                wmod_in[128 * k:128 * (k + 1), :]))
        wmod_tok.append(dma("g", SAa[:, 0:4096], wmod_in[768:896, 0:4096]))
        wmod_tok.append(dma("g", SAb[:, 0:2048], wmod_in[768:896, 4096:6144]))
        wmod_tok.append(dma("g", SB[:, 12288:16384], wmod_in[896:1024, 0:4096]))
        wmod_tok.append(dma("g", SC[:, 12288:14336], wmod_in[896:1024, 4096:6144]))
        tok_aall = dma("g", W1[:, 0:1024], Aall_in[:, :])
        tok_ball = dma("g", ARENA[:, :], BallT_in[:, :])
        tok_tkf = dma("g", TKS[:, :], tkv_in[:, :])
        tok_x = dma("g", SX.ap().rearrange("p (d t) -> p d t", d=DC),
                    xT.rearrange("(d p) t -> p d t", p=128))

        def wload(buf, name, gate=None):
            if gate is not None:
                w("g", gate[0], gate[1])
            return dma("g", buf.ap().rearrange("p (k m j) -> p k m j", k=8, j=128),
                       WTS[name].rearrange("(k p) (m j) -> p k m j", p=128, j=128))

        tok_wq = wload(W0, "Wq")

        wdma("sc", "s", tok_ct)
        t_sig = op("sc", lambda e: e.activation(SIG[:, :], CT[:, :], AF.Sigmoid))
        w("v", "sc", t_sig)
        t_silu = op("v", lambda e: e.tensor_mul(SILU[:, :], CT[:, :], SIG[:, :]))

        rowfree = [0, 0]
        wdma("t", "g", tok_consts)
        for n in range(12):
            key = ("PA", n % 2)
            bank = PA[0:1, 0:512] if n % 2 == 0 else PA[0:1, 512:1024]
            rowslot = ROW0 if n % 2 == 0 else ROW1
            wdma("t", "g", wmod_tok[9])
            w("t", "v", t_silu)
            use_bank(key)
            for k in range(8):
                mm(bank, SILU[:, k:k + 1], wmod_rhs(k, n),
                   start=(k == 0), stop=(k == 7), sem=(k == 7))
            w("v", "t", Cn["t"])
            w("v", "t", rowfree[n % 2])
            tv = op("v", lambda e, bank=bank, rowslot=rowslot: e.tensor_copy(
                rowslot, bank))
            free_bank(key, "v", tv)
            w("t", "v", tv)
            use_bank(("PTB7",))
            for c in range(4):
                op("t", lambda e, c=c, n=n, rowslot=rowslot: e.transpose(
                    PTB[:, 8 * (n % 2) + 2 * c: 8 * (n % 2) + 2 * c + 1],
                    rowslot[0:1, 128 * c:128 * c + 128], I128[0:1, 0:1]),
                    sem=(c == 3))
            rowfree[n % 2] = Cn["t"]
            w("v", "t", Cn["t"])
            vv, cc4 = n // 2, 4 * (n % 2)
            ptb_view = PTB[:, 8 * (n % 2): 8 * (n % 2) + 8].rearrange(
                "p (c two) -> p c two", two=2)[:, :, 0:1]
            tvv = op("v", lambda e, vv=vv, cc4=cc4, ptb_view=ptb_view:
                     e.tensor_copy(
                         MODT[:, 8 * vv + cc4: 8 * vv + cc4 + 4], ptb_view))
            free_bank(("PTB7",), "v", tvv)
        t_modr = Cn["v"]

        wdma("v", "s", tok_vecs)
        fence("v")
        op("v", lambda e: e.tensor_scalar(DERV[:, 0:8], MODT[:, 8:16], 1.0, None,
                                          op0=ALU.add))
        op("v", lambda e: e.tensor_copy(DERV[:, 8:16], MODT[:, 0:8]))
        op("v", lambda e: e.tensor_scalar(DERV[:, 16:24], MODT[:, 32:40], 1.0, None,
                                          op0=ALU.add))
        op("v", lambda e: e.tensor_copy(DERV[:, 24:32], MODT[:, 24:32]))
        for (src0, dst) in ((16, 32), (40, 40)):
            op("v", lambda e, s=src0: e.tensor_scalar(
                STF[:, 8:16], MODT[:, s:s + 8], 0.0, None, op0=ALU.is_ge))
            fence("v")
            op("v", lambda e: e.tensor_scalar(
                STF[:, 8:16], STF[:, 8:16], 2.0, -1.0, op0=ALU.mult, op1=ALU.add))
            fence("v")
            op("v", lambda e, s=src0: e.tensor_mul(
                STF[:, 0:8], MODT[:, s:s + 8], STF[:, 8:16]))
            fence("v")
            op("v", lambda e: e.tensor_scalar(
                STF[:, 0:8], STF[:, 0:8], 0.25, None, op0=ALU.max))
            fence("v")
            op("v", lambda e, d=dst: e.tensor_mul(
                DERV[:, d:d + 8], STF[:, 0:8], STF[:, 8:16]))
            fence("v")
        op("v", lambda e: e.tensor_mul(DERV[:, 48:56], DERV[:, 32:40],
                                       VECS[:, 40:48]))
        t_derv = op("v", lambda e: e.tensor_mul(DERV[:, 56:64], DERV[:, 32:40],
                                                VECS[:, 48:56]))
        if "modT" in dbg:
            w("s", "v", t_derv)
            dma("s", dbg["modT"][:, :], MODT[:, :])
        if "DERV" in dbg:
            w("s", "v", t_derv)
            dma("s", dbg["DERV"][:, :], DERV[:, :])
        cut("p0a")

        wdma("v", "s", tok_idx)
        op("v", lambda e: e.tensor_copy(IDXF[:, :], IDXS[:, :]))
        fence("v")
        op("v", lambda e: e.tensor_scalar(IDXF[:, :], IDXF[:, :], 0.0, 7.0,
                                          op0=ALU.max, op1=ALU.min))
        fence("v")
        t_idxb = op("v", lambda e: e.tensor_copy(IDXB[:, :], IDXF[:, :]))
        wdma("t", "g", tok_ones1)
        w("t", "v", t_idxb)
        use_bank(("PD",))
        mm(PD[0:128, 0:2], ONES1[0:1, 0:128], IDXB[0:1, 0:2], True, True, sem=True)
        w("v", "t", Cn["t"])
        wdma("v", "s", tok_p16)
        t_ohe = op("v", lambda e: e.tensor_scalar(OHE[:, :], PD[0:128, 0:2],
                                                  P16T[:, 0:1], None,
                                                  op0=ALU.is_equal))
        t_ohz = op("g", lambda e: e.memset(OH[:, :], 0))
        wdma("v", "g", tok_consts)
        w("v", "g", t_ohz)
        fence("v")
        for k in range(2):
            op("v", lambda e, k=k: e.tensor_scalar(
                OH[:, 16 * k:16 * k + 16], I16t, OHE[:, k:k + 1], None,
                op0=ALU.mult))
        t_oh = Cn["v"]
        free_bank(("PD",), "v", t_ohe)
        wdma("t", "g", tok_tkf)
        use_bank(("PD",))
        mm(PD[0:128, 2:4], ONES1[0:1, 0:128], TKS[0:1, 0:2], True, True, sem=True)
        w("v", "t", Cn["t"])
        t_tkb = op("v", lambda e: e.tensor_copy(TKB[:, :], PD[0:128, 2:4]))
        free_bank(("PD",), "v", t_tkb)

        wdma("t", "g", tok_aall)
        w("t", "v", t_oh)
        for d in range(DC):
            key = ("PC", d % 2)
            bank = PC[:, 0:32] if d % 2 == 0 else PC[:, 512:544]
            use_bank(key)
            mm(bank, W1[:, 128 * d:128 * (d + 1)], OH[:, :], True, True, sem=True)
            w("v", "t", Cn["t"])
            tv = op("v", lambda e, d=d, bank=bank: e.tensor_copy(
                ASEL[:, 32 * d:32 * (d + 1)], bank))
            free_bank(key, "v", tv)
        wdma("t", "g", tok_ball)
        for n in range(8):
            use_bank(("PD",))
            mm(PD[0:32, 0:512], OH[:, :], ARENA[:, 512 * n:512 * (n + 1)],
               True, True, sem=True)
            w("v", "t", Cn["t"])
            tv = op("v", lambda e, n=n: e.tensor_copy(
                ARENA[0:32, 512 * n:512 * (n + 1)], PD[0:32, 0:512]))
            free_bank(("PD",), "v", tv)
        t_bsel = Cn["v"]
        t_p0b = Cn["t"]
        if "OH" in dbg:
            w("s", "v", t_bsel)
            dma("s", dbg["OH"][:, :], OH[:, :])
        if "BselT" in dbg:
            w("s", "v", t_bsel)
            dma("s", dbg["BselT"][:, :], ARENA[:, :])
        if "ASEL" in dbg:
            w("s", "v", t_bsel)
            dma("s", dbg["ASEL"][:, :], ASEL[:, :])
        cut("p0b")

        wdma("t", "g", tok_x)

        def ln_stats(src_chunk_ap, eps, sq_scratch):
            use_bank(("PA", 0))
            for d in range(DC):
                mm(PA[:, 0:512], Jc, src_chunk_ap(d), start=(d == 0),
                   stop=(d == DC - 1), sem=(d == DC - 1))
            t_mu_mm = Cn["t"]
            w("sc", "t", Cn["t"])
            for d in range(DC):
                op("sc", lambda e, d=d: e.activation(
                    sq_scratch(d), src_chunk_ap(d), AF.Square))
            t_sq = Cn["sc"]
            use_bank(("PA", 1))
            w("t", "sc", t_sq)
            for d in range(DC):
                mm(PA[:, 512:1024], Jc, sq_scratch(d), start=(d == 0),
                   stop=(d == DC - 1), sem=(d == DC - 1))
            t_ss_mm = Cn["t"]
            w("v", "t", t_mu_mm)
            op("v", lambda e: e.tensor_copy(MUT, PA[:, 0:512]))
            t_mucp = op("v", lambda e: e.tensor_copy(STF[:, 0:512], PA[:, 0:512]))
            free_bank(("PA", 0), "v", t_mucp)
            op("v", lambda e: e.tensor_mul(STF[:, 512:1024], STF[:, 0:512],
                                           STF[:, 0:512]))
            w("v", "t", t_ss_mm)
            t_var = op("v", lambda e: e.tensor_sub(STF[:, 0:512], PA[:, 512:1024],
                                                   STF[:, 512:1024]))
            free_bank(("PA", 1), "v", t_var)
            w("sc", "v", t_var)
            wdma("sc", "s", tok_p16)
            op("sc", lambda e, eps=eps: e.activation(STF[:, 512:1024],
                                                     STF[:, 0:512], AF.Ln,
                                                     bias=float(eps)))
            t_rst = op("sc", lambda e: e.activation(RST, STF[:, 512:1024],
                                                    AF.Exp, scale=-0.5))
            return t_rst

        for w_ in range(NW):
            xs = lambda d, w_=w_: SX[:, 2048 * d + 512 * w_:
                                     2048 * d + 512 * w_ + 512]
            sqs = lambda d: SAa[:, 512 * d: 512 * d + 512]
            t_rst = ln_stats(xs, EPS_OUT, sqs)
            w("v", "sc", t_rst)
            for d in range(DC):
                hs = SH[:, 2048 * d + 512 * w_: 2048 * d + 512 * w_ + 512]
                op("v", lambda e, d=d, w_=w_: e.tensor_sub(
                    TMPE[:, 0:512],
                    SX[:, 2048 * d + 512 * w_: 2048 * d + 512 * w_ + 512],
                    MUT))
                op("v", lambda e: e.tensor_mul(TMPE[:, 0:512], TMPE[:, 0:512],
                                               RST))
                op("v", lambda e, d=d, hs=hs: e.tensor_scalar(
                    hs, TMPE[:, 0:512], DERV[:, d:d + 1],
                    DERV[:, 8 + d:8 + d + 1], op0=ALU.mult, op1=ALU.add))
        t_h = Cn["v"]
        tok_dbg_h = None
        if "hT" in dbg:
            w("s", "v", t_h)
            tok_dbg_h = dma("s", dbg["hT"][:, :], SH[:, :])
        cut("p1")

        tok_wq2 = wload(W1, "Wq2", gate=("t", t_p0b))

        FMROT = [(PB, 0), (PB, 512), (PC, 0), (PC, 512)]

        def fm_proj(wblk, hoff_w, dstc, bias_col):
            for m in range(DC):
                bk, off = FMROT[m % 4]
                key = ("PB" if bk is PB else "PC", off // 512)
                bank = bk[:, off:off + 512]
                use_bank(key)
                for k in range(DC):
                    mm(bank, wblk(k, m),
                       SH[:, 2048 * k + hoff_w: 2048 * k + hoff_w + 512],
                       start=(k == 0), stop=(k == DC - 1), sem=(k == DC - 1))
                w("sc", "t", Cn["t"])
                tv = op("sc", lambda e, m=m, bank=bank: e.activation(
                    dstc(m), bank, AF.Identity,
                    bias=VECS[:, bias_col * 8 + m: bias_col * 8 + m + 1]))
                free_bank(key, "sc", tv)

        TPROT = [(PA, "PA"), (PB, "PB"), (PC, "PC")]

        def tok_proj(sax, dstsb, wbuf, base_tile):
            for tt in range(4):
                i = base_tile + tt
                PT, pnm = TPROT[i % 3]
                use_bank((pnm, 0), (pnm, 1))
                for half in range(2):
                    bank = PT[:, 0:512] if half == 0 else PT[:, 512:1024]
                    for k in range(DC):
                        mm(bank,
                           sax[:, 512 * k + 128 * tt: 512 * k + 128 * tt + 128],
                           wbuf[:, 1024 * k + 512 * half:
                                1024 * k + 512 * half + 512],
                           start=(k == 0), stop=False)
                    mm(bank, ONES1[0:1, 0:128],
                       BROW[0:1, 512 * half: 512 * half + 512],
                       start=False, stop=True, sem=(half == 1))
                t_mm = Cn["t"]
                w("sc", "t", t_mm)
                op("sc", lambda e, i=i, PT=PT: e.activation(
                    dstsb[:, 1024 * i:1024 * (i + 1)], PT[:, :], AF.Relu))
                t_relu = Cn["sc"]
                w("v", "sc", t_relu)
                t_min = op("v", lambda e, PT=PT: e.tensor_scalar(
                    TMPE, PT[:, 0:1024], 0.0, None, op0=ALU.min))
                free_bank((pnm, 0), "v", t_min)
                free_bank((pnm, 1), "v", t_min)
                w("sc", "v", t_min)
                t_exp = op("sc", lambda e: e.activation(TMPE, TMPE, AF.Exp))
                w("v", "sc", t_exp)
                op("v", lambda e, i=i: e.tensor_add(
                    dstsb[:, 1024 * i:1024 * (i + 1)],
                    dstsb[:, 1024 * i:1024 * (i + 1)], TMPE))

        wdma("t", "g", tok_wq)
        wdma("t", "g", tok_brow)
        w("t", "v", t_h)
        for w_ in range(NW):
            sax = SAa if w_ % 2 == 0 else SAb
            dstc = lambda m, sax=sax: sax[:, 512 * m: 512 * m + 512]
            fm_proj(W0b, 512 * w_, dstc, 0)
            wdma("t", "g", tok_wq2)
            w("t", "sc", Cn["sc"])
            tok_proj(sax, SB, W1, 4 * w_)
        t_qchain = Cn["t"]
        t_q2v = Cn["v"]
        cut("qchain")
        tok_wk = wload(W0, "Wk", gate=("t", t_qchain))
        w("g", "v", t_q2v)
        tok_brow2 = dma("g", BROW[:, :], brow_in[0:1, D:2 * D])

        tok_wk2 = None
        for w_ in range(NW):
            sax = SAa if w_ % 2 == 0 else SAb
            wdma("t", "g", tok_wk)
            dstc = lambda m, sax=sax: sax[:, 512 * m: 512 * m + 512]
            fm_proj(W0b, 512 * w_, dstc, 1)
            if w_ == 0:
                tok_wk2 = wload(W1, "Wk2", gate=("t", Cn["t"]))
            wdma("t", "g", tok_wk2)
            wdma("t", "g", tok_brow2)
            w("t", "sc", Cn["sc"])
            tok_proj(sax, SC, W1, 4 * w_)
            for tt in range(4):
                i = 4 * w_ + tt
                op("v", lambda e, i=i: e.tensor_mul(
                    TMPF, SB[:, 1024 * i:1024 * (i + 1)],
                    SC[:, 1024 * i:1024 * (i + 1)]))
                op("v", lambda e, i=i: e.tensor_reduce(
                    QKS[:, 16 * i:16 * (i + 1)],
                    TMPF.rearrange("p (h d) -> p h d", d=DH),
                    axis=AX.X, op=ALU.add))
        t_k2done = Cn["v"]
        t_kchain = Cn["t"]
        tok_wv = wload(W0, "Wv", gate=("t", t_kchain))
        tok_dbg_q2 = tok_dbg_k2 = tok_dbg_qks = None
        if "q2" in dbg:
            w("s", "v", t_k2done)
            tok_dbg_q2 = dma("s", dbg["q2"][:, :], SB[:, :])
        if "k2" in dbg:
            w("s", "v", t_k2done)
            tok_dbg_k2 = dma("s", dbg["k2"][:, :], SC[:, :])
        if "qks" in dbg:
            w("s", "v", t_k2done)
            tok_dbg_qks = dma("s", dbg["qks"][:, :], QKS[:, :])
        cut("kchain")

        w("t", "v", t_k2done)
        use_bank(("PA", 0), ("PA", 1))
        for h in range(H):
            for i in range(NT):
                mm(PA[0:64, 64 * h:64 * (h + 1)],
                   SC[:, 1024 * i + 64 * h: 1024 * i + 64 * h + 64],
                   SB[:, 1024 * i + 64 * h: 1024 * i + 64 * h + 64],
                   start=(i == 0), stop=(i == NT - 1),
                   sem=(i == NT - 1 and h in (7, 15)))
        t_sdone = Cn["t"]
        w("v", "t", t_sdone)
        op("v", lambda e: e.tensor_copy(SPART[:, 0:512], PA[0:64, 0:512]))
        t_sev = op("v", lambda e: e.tensor_copy(SPART[:, 512:1024],
                                                PA[0:64, 512:1024]))
        free_bank(("PA", 0), "v", t_sev)
        free_bank(("PA", 1), "v", t_sev)
        cut("sphase")
        w("g", "v", t_sev)
        tok_ccin = dma("g", cc_in[:, :], SPART[:, :])
        wdma("g", "g", tok_ccin)
        t_cc = op("g", lambda e: e.collective_compute(
            "AllReduce", ALU.add,
            replica_groups=[[0, 1], [2, 3], [4, 5], [6, 7]],
            ins=[cc_in[:, :]], outs=[cc_out[:, :]]))

        cut("cc")
        for i in range(NT):
            op("v", lambda e, i=i: e.tensor_scalar(
                STF[:, 16 * i:16 * (i + 1)], QKS[:, 16 * i:16 * (i + 1)],
                float(EPS_QK), None, op0=ALU.add))
        t_qe = Cn["v"]
        w("sc", "v", t_qe)
        op("sc", lambda e: e.activation(STF[:, 512:768], STF[:, 0:256], AF.Ln))
        fence("sc")
        t_recip = op("sc", lambda e: e.activation(
            TMPF[:, 0:256], STF[:, 512:768], AF.Exp, scale=-1.0))
        if "recip" in dbg:
            w("s", "sc", t_recip)
            dma("s", dbg["recip"][:, :], TMPF[:, 0:256])
        if "qks2" in dbg:
            w("s", "v", t_qe)
            dma("s", dbg["qks2"][:, :], QKS[:, :])
        w("t", "sc", t_recip)
        for i in range(NT):
            po = 128 * (i % 8)
            use_bank(("PTB7",))
            op("t", lambda e, i=i, po=po: e.transpose(
                PTB[0:16, po:po + 128], TMPF[:, 16 * i:16 * (i + 1)], I128),
                sem=True)
            w("v", "t", Cn["t"])
            tv = op("v", lambda e, i=i, po=po: e.tensor_copy(
                QKSRT[:, 128 * i:128 * (i + 1)], PTB[0:16, po:po + 128]))
            free_bank(("PTB7",), "v", tv)
        t_qksr = Cn["v"]
        if "qksrT" in dbg:
            w("s", "v", t_qksr)
            dma("s", dbg["qksrT"][:, :], POOL[:, 3072:5120])

        tok_wv2 = None
        for w_ in range(NW):
            sax = SAa if w_ % 2 == 0 else SAb
            wdma("t", "g", tok_wv)
            dstc = lambda m, sax=sax: sax[:, 512 * m: 512 * m + 512]
            fm_proj(W0b, 512 * w_, dstc, 2)
            if w_ == 0:
                tok_wv2 = wload(W1, "Wv2", gate=("t", Cn["t"]))
            wdma("t", "g", tok_wv2)
            w("t", "sc", Cn["sc"])
            for m in range(DC):
                bk, off = FMROT[m % 4]
                key = ("PB" if bk is PB else "PC", off // 512)
                bank = bk[:, off:off + 512]
                use_bank(key)
                for k in range(DC):
                    mm(bank, W1b(k, m), sax[:, 512 * k: 512 * k + 512],
                       start=(k == 0), stop=(k == DC - 1), sem=(k == DC - 1))
                w("sc", "t", Cn["t"])
                w("sc", "t", t_sdone)
                if tok_dbg_q2:
                    wdma("sc", "s", tok_dbg_q2)
                if tok_dbg_qks:
                    wdma("sc", "s", tok_dbg_qks)
                w("sc", "v", t_k2done)
                tv = op("sc", lambda e, m=m, w_=w_, bank=bank: e.activation(
                    SB[:, 2048 * m + 512 * w_: 2048 * m + 512 * w_ + 512], bank,
                    AF.Identity, bias=VECS[:, 3 * 8 + m: 3 * 8 + m + 1]))
                free_bank(key, "sc", tv)
        t_v2 = Cn["sc"]
        t_vchain = Cn["t"]
        tok_wo = wload(W0, "Wo", gate=("t", t_vchain))

        wdma("t", "g", tok_sel)
        w("t", "v", t_qksr)
        for c in range(DC):
            for nn in range(NW):
                j = c * NW + nn
                bk, off = FMROT[j % 4]
                key = ("PB" if bk is PB else "PC", off // 512)
                bank = bk[:, off:off + 512]
                use_bank(key)
                mm(bank, SEL[0:16, 128 * c:128 * (c + 1)],
                   QKSRT[:, 512 * nn:512 * (nn + 1)], True, True, sem=True)
                w("v", "t", Cn["t"])
                w("v", "sc", t_v2)
                tv = op("v", lambda e, c=c, nn=nn, bank=bank: e.tensor_mul(
                    SB[:, 2048 * c + 512 * nn: 2048 * c + 512 * nn + 512],
                    SB[:, 2048 * c + 512 * nn: 2048 * c + 512 * nn + 512], bank))
                free_bank(key, "v", tv)
        t_v2s = Cn["v"]
        t_bcast_t = Cn["t"]
        tok_dbg_v2 = None
        if "v2" in dbg:
            w("s", "v", t_v2s)
            tok_dbg_v2 = dma("s", dbg["v2"][:, :], SB[:, :])
        cut("v2s")

        w("g", "g", t_cc)
        w("g", "t", t_bcast_t)
        dma("g", STOT[:, :], cc_out[:, :])
        tok_stot = dma("g", STOT64[:, :], cc_out[:, :])
        if "Stot" in dbg:
            wdma("s", "g", tok_stot)
            dma("s", dbg["Stot"][:, :], STOT[:, :])

        wdma("t", "g", tok_stot)
        w("t", "v", t_v2s)
        for nn in range(NW):
            sax = SAa if nn % 2 == 0 else SAb
            for c in range(DC):
                for hh in range(2):
                    h = 2 * c + hh
                    j = c * 2 + hh
                    bk, off = FMROT[j % 4]
                    key = ("PB" if bk is PB else "PC", off // 512)
                    bank = bk[0:64, off:off + 512]
                    use_bank(key)
                    ST = STOT if hh == 0 else STOT64
                    mm(bank, ST[:, 64 * h:64 * (h + 1)],
                       SB[64 * hh:64 * (hh + 1),
                          2048 * c + 512 * nn: 2048 * c + 512 * nn + 512],
                       True, True, sem=True)
                    w("sc", "t", Cn["t"])
                    tv = op("sc", lambda e, c=c, hh=hh, bank=bank, sax=sax:
                            e.copy(
                                sax[64 * hh:64 * (hh + 1),
                                    512 * c:512 * (c + 1)], bank))
                    free_bank(key, "sc", tv)
            t_aout = Cn["sc"]
            wdma("t", "g", tok_wo)
            w("t", "sc", t_aout)
            for m in range(DC):
                bk, off = FMROT[m % 4]
                key = ("PB" if bk is PB else "PC", off // 512)
                bank = bk[:, off:off + 512]
                tslot = POOL[:, 1024 + 512 * (m % 2):1024 + 512 * (m % 2) + 512]
                use_bank(key)
                for k in range(DC):
                    mm(bank, W0b(k, m), sax[:, 512 * k: 512 * k + 512],
                       start=(k == 0), stop=(k == DC - 1), sem=(k == DC - 1))
                w("sc", "t", Cn["t"])
                wdma("sc", "g", tok_ccin)
                w("sc", "v", Cn["v"])
                op("sc", lambda e, bank=bank, tslot=tslot, m=m: e.activation(
                    tslot, bank, AF.Identity,
                    bias=VECS[:, 4 * 8 + m: 4 * 8 + m + 1]))
                t_bo = Cn["sc"]
                free_bank(key, "sc", t_bo)
                w("v", "sc", t_bo)
                w("v", "t", t_sdone)
                if tok_dbg_k2:
                    wdma("v", "s", tok_dbg_k2)
                op("v", lambda e, m=m, nn=nn, tslot=tslot: e.tensor_add(
                    SC[:, 2048 * m + 512 * nn: 2048 * m + 512 * nn + 512],
                    tslot,
                    SH[:, 2048 * m + 512 * nn: 2048 * m + 512 * nn + 512]))
        t_y = Cn["v"]
        t_p3t = Cn["t"]
        tok_dbg_y = None
        if "yT" in dbg:
            w("s", "v", t_y)
            tok_dbg_y = dma("s", dbg["yT"][:, :], SC[:, :])
        cut("p3")

        for w_ in range(NW):
            ys = lambda d, w_=w_: SC[:, 2048 * d + 512 * w_:
                                     2048 * d + 512 * w_ + 512]
            sqs = lambda d: SAa[:, 512 * d: 512 * d + 512]
            if w_ == 0:
                w("sc", "v", t_y)
                w("t", "v", t_y)
            t_rst = ln_stats(ys, EPS_IN, sqs)
            w("v", "sc", t_rst)
            for d in range(DC):
                op("v", lambda e, d=d, w_=w_: e.tensor_sub(
                    TMPE[:, 0:512],
                    SC[:, 2048 * d + 512 * w_: 2048 * d + 512 * w_ + 512],
                    MUT))
                op("v", lambda e: e.tensor_mul(TMPE[:, 0:512], TMPE[:, 0:512],
                                               RST))
                op("v", lambda e, d=d: e.tensor_scalar(
                    TMPE[:, 0:512], TMPE[:, 0:512], DERV[:, 48 + d:48 + d + 1],
                    DERV[:, 56 + d:56 + d + 1], op0=ALU.mult, op1=ALU.add))
                op("v", lambda e, d=d, w_=w_: e.tensor_add(
                    SX[:, 2048 * d + 512 * w_: 2048 * d + 512 * w_ + 512],
                    SX[:, 2048 * d + 512 * w_: 2048 * d + 512 * w_ + 512],
                    TMPE[:, 0:512]))
        t_x1 = Cn["v"]
        t_p4t = Cn["t"]
        tok_dbg_x1 = None
        if "x1T" in dbg:
            w("s", "v", t_x1)
            tok_dbg_x1 = dma("s", dbg["x1T"][:, :], SX[:, :])
        cut("p4")

        wi_tok = []
        for kc in range(8):
            if kc < 4:
                dstap = SB[:, 4096 * kc:4096 * (kc + 1)]
                gate = ("t", t_p3t)
                wdma("g", "s", tok_dbg_v2)
            elif kc < 6:
                dstap = W1[:, 4096 * (kc - 4):4096 * (kc - 4) + 4096]
                gate = ("t", t_vchain)
            elif kc == 6:
                dstap = SAa[:, :]
                gate = ("t", t_p4t)
            else:
                dstap = SAb[:, :]
                gate = ("t", t_p4t)
            w("g", gate[0], gate[1])
            wi_tok.append(dma("g", dstap, wiT_in[128 * kc:128 * (kc + 1), :]))
        wo_tok = []
        for fc in range(FC):
            if fc < 16:
                dstap = SH[:, 1024 * fc:1024 * (fc + 1)]
                w("g", "v", t_y)
                wdma("g", "s", tok_dbg_h)
            else:
                dstap = SC[:, 1024 * (fc - 16):1024 * (fc - 16) + 1024]
                w("g", "t", t_p4t)
                w("g", "v", t_x1)
                wdma("g", "s", tok_dbg_y)
            wo_tok.append(dma("g", dstap, woT_in[128 * fc:128 * (fc + 1), :]))

        h2_tok = {}
        u_tok = {}
        aend_tok = {0: None}
        interC_tok = {}

        def ln2_stage(tc, t_aend_prev):
            xs2 = lambda d: SX[:, 2048 * d + 256 * tc: 2048 * d + 256 * tc + 256]
            use_bank(("PD",))
            w("t", "v", t_x1)
            wdma("v", "s", tok_dbg_x1)
            for d in range(DC):
                mm(PD[:, 0:256], Jc, xs2(d), start=(d == 0), stop=(d == DC - 1),
                   sem=(d == DC - 1))
            t_mu_mm = Cn["t"]
            use_bank(("PB", 0))
            w("sc", "t", Cn["t"])
            for d in range(DC):
                slot = XSQ[:, 256 * (d % 2):256 * (d % 2) + 256]
                if d >= 2:
                    w("sc", "t", Cn["t"])
                op("sc", lambda e, d=d, slot=slot: e.activation(
                    slot, xs2(d), AF.Square))
                w("t", "sc", Cn["sc"])
                mm(PB[:, 0:256], Jc, slot, start=(d == 0), stop=(d == DC - 1),
                   sem=True)
            t_ss_mm = Cn["t"]
            w("v", "t", t_mu_mm)
            op("v", lambda e: e.tensor_copy(MUT2, PD[:, 0:256]))
            t_mucp = op("v", lambda e: e.tensor_copy(STF[:, 0:256], PD[:, 0:256]))
            free_bank(("PD",), "v", t_mucp)
            op("v", lambda e: e.tensor_mul(STF[:, 256:512], STF[:, 0:256],
                                           STF[:, 0:256]))
            w("v", "t", t_ss_mm)
            t_var = op("v", lambda e: e.tensor_sub(STF[:, 0:256], PB[:, 0:256],
                                                   STF[:, 256:512]))
            free_bank(("PB", 0), "v", t_var)
            w("sc", "v", t_var)
            op("sc", lambda e: e.activation(STF[:, 256:512], STF[:, 0:256],
                                            AF.Ln, bias=float(EPS_OUT)))
            t_rst2 = op("sc", lambda e: e.activation(RST2, STF[:, 256:512],
                                                     AF.Exp, scale=-0.5))
            w("v", "sc", t_rst2)
            if t_aend_prev:
                w("v", "t", t_aend_prev)
            for d in range(DC):
                op("v", lambda e, d=d: e.tensor_sub(
                    TMP6[:, 0:256], xs2(d), MUT2))
                op("v", lambda e: e.tensor_mul(TMP6[:, 0:256], TMP6[:, 0:256],
                                               RST2))
                op("v", lambda e, d=d: e.tensor_scalar(
                    H2TC[:, 256 * d:256 * (d + 1)], TMP6[:, 0:256],
                    DERV[:, 16 + d:16 + d + 1], DERV[:, 24 + d:24 + d + 1],
                    op0=ALU.mult, op1=ALU.add))
            h2_tok[tc] = Cn["v"]
            if tc == 0 and "h2tc0" in dbg:
                w("s", "v", h2_tok[tc])
                dma("s", dbg["h2tc0"][:, :], POOL[:, 0:2048])

        def u_stage(tc):
            w("t", "v", h2_tok[tc])
            use_bank(("PD",))
            for k in range(DC):
                mm(PD[0:32, 0:256], ASEL[:, 32 * k:32 * (k + 1)],
                   H2TC[:, 256 * k:256 * (k + 1)], start=(k == 0),
                   stop=(k == DC - 1), sem=(k == DC - 1))
            w("v", "t", Cn["t"])
            op("v", lambda e: e.tensor_copy(UB[0:32, 0:256], PD[0:32, 0:256]))
            op("v", lambda e: e.tensor_copy(UB[0:32, 256:512], PD[0:32, 0:256]))
            t_u01 = op("v", lambda e: e.tensor_scalar(
                UB[0:16, 256:512], PD[0:16, 0:256], -1.0, None, op0=ALU.mult))
            free_bank(("PD",), "v", t_u01)
            u_tok[tc] = t_u01
            if tc == 0 and "u01" in dbg:
                w("s", "v", t_u01)
                dma("s", dbg["u01"][:, :], UB[:, :])

        AROT = [(PA, 0, ("PA", 0)), (PA, 512, ("PA", 1)), (PD, 0, ("PD",)),
                (PB, 0, ("PB", 0)), (PB, 512, ("PB", 1)), (PC, 0, ("PC", 0))]

        def a_stage(tc, t_p3t):
            w("t", "v", u_tok[tc])
            for kc in range(8):
                wdma("t", "g", wi_tok[kc])

            def a_head(fc):
                bk, off, key = AROT[fc % 6]
                bank = bk[:, off:off + 256]
                r0s = R0P[:, 256 * (fc % 3):256 * (fc % 3) + 256]
                use_bank(key)
                mm(bank, ARENA[0:16, 128 * fc:128 * fc + 128], UB[0:16, 0:256],
                   start=True, stop=False)
                for k in range(DC):
                    mm(bank, wiTblk(k, fc), H2TC[:, 256 * k:256 * (k + 1)],
                       start=False, stop=False, sem=(k == DC - 1))
                t_head = Cn["t"]
                w("sc", "t", t_head)
                w("sc", "v", Cn["v"])
                op("sc", lambda e, bank=bank, r0s=r0s: e.activation(
                    r0s, bank, AF.Relu, scale=TKB[:, 0:1]))
                return Cn["sc"]

            def a_tail(fc, t_ev0):
                bk, off, key = AROT[fc % 6]
                bank = bk[:, off:off + 256]
                r0s = R0P[:, 256 * (fc % 3):256 * (fc % 3) + 256]
                r1s = R1P[:, 256 * (fc % 3):256 * (fc % 3) + 256]
                w("t", "sc", t_ev0)
                mm(bank, ARENA[0:32, 128 * fc:128 * fc + 128], UB[0:32, 256:512],
                   start=False, stop=True, sem=True)
                w("sc", "t", Cn["t"])
                t_ev1 = op("sc", lambda e, bank=bank, r1s=r1s: e.activation(
                    r1s, bank, AF.Relu, scale=TKB[:, 1:2]))
                free_bank(key, "sc", t_ev1)
                w("v", "sc", t_ev1)
                if tc == 0:
                    w("v", "t", t_p3t)
                op("v", lambda e, fc=fc, r0s=r0s, r1s=r1s: e.tensor_add(
                    W0[:, 256 * fc:256 * (fc + 1)], r0s, r1s))

            pend = []
            for fc in range(FC):
                ev0 = a_head(fc)
                pend.append((fc, ev0))
                if len(pend) > 2:
                    a_tail(*pend.pop(0))
            aend_tok[tc + 1] = Cn["t"]
            for it in pend:
                a_tail(*it)
            interC_tok[tc] = Cn["v"]
            if tc == 0 and "interC0" in dbg:
                w("s", "v", interC_tok[tc])
                dma("s", dbg["interC0"][:, :], W0[:, :])
            if tc == 0:
                cut("aphase")

        def mm2_stage(tc):
            w("t", "v", interC_tok[tc])
            for fc in range(FC):
                wdma("t", "g", wo_tok[fc])
            dp_bank = [(PB, 0), (PB, 256), (PB, 512), (PB, 768),
                       (PC, 0), (PC, 256), (PC, 512), (PC, 768)]
            for dp in range(DC):
                bk, off = dp_bank[dp]
                key = ("PB", off // 512) if bk is PB else ("PC", off // 512)
                if dp % 2 == 0:
                    use_bank(key)
                for fc in range(FC):
                    mm(bk[:, off:off + 256], woTblk(fc, dp),
                       W0[:, 256 * fc:256 * (fc + 1)], start=(fc == 0),
                       stop=(fc == FC - 1), sem=(fc == FC - 1))
                if dp % 2 == 1:
                    t_dp = Cn["t"]
                    w("v", "t", t_dp)
                    for d2 in (dp - 1, dp):
                        bk2, off2 = dp_bank[d2]
                        op("v", lambda e, bk2=bk2, off2=off2, d2=d2:
                           e.tensor_scalar(
                               TMP6[:, 256:512], bk2[:, off2:off2 + 256],
                               DERV[:, 40 + d2:40 + d2 + 1], None, op0=ALU.mult))
                        tv = op("v", lambda e, d2=d2, tc=tc: e.tensor_add(
                            SX[:, 2048 * d2 + 256 * tc:
                               2048 * d2 + 256 * tc + 256],
                            SX[:, 2048 * d2 + 256 * tc:
                               2048 * d2 + 256 * tc + 256],
                            TMP6[:, 256:512]))
                        free_bank(key, "v", tv)
            t_x2 = Cn["v"]
            if tc == 0:
                cut("tc0")
            w("g", "v", t_x2)
            dma("g",
                out_xT.rearrange("(d p) t -> p d t", p=128)[:, :,
                                                            256 * tc:256 * (tc + 1)],
                SX.ap().rearrange("p (d t) -> p d t", d=DC)[:, :,
                                                            256 * tc:256 * (tc + 1)])

        ln2_stage(0, None)
        if True:
            cut("ln2s")
        u_stage(0)
        cut("u")
        for tc in range(TCN):
            a_stage(tc, t_p3t)
            if tc + 1 < TCN:
                ln2_stage(tc + 1, aend_tok[tc + 1])
            mm2_stage(tc)
            if tc + 1 < TCN:
                u_stage(tc + 1)

        w("g", "dg", D16["g"])
        w("s", "ds", D16["s"])

    with nc.allow_low_precision("bf16 recipe calibrated by problem tolerance"), \
            nc.Block() as blk:
        def run(name, e):
            try:
                emit(name, e)
            except _Cut:
                pass

        @blk.tensor
        def _(e):
            run("t", e)

        @blk.vector
        def _(e):
            run("v", e)

        @blk.scalar
        def _(e):
            run("sc", e)

        @blk.gpsimd
        def _(e):
            run("g", e)

        @blk.sync
        def _(e):
            run("s", e)

    return nc, dbg


_NC_CACHE = {}


def _get_nc(debug=(), stop_after=None):
    key = (tuple(sorted(debug)), stop_after)
    if key not in _NC_CACHE:
        _NC_CACHE[key] = build_nc(debug=key[0], stop_after=stop_after)
    return _NC_CACHE[key]


def _host_inputs(inputs, core):
    b = core // 2
    x = np.asarray(inputs["L0_attn_x"])
    xs = x[b, 2048 * (core % 2):2048 * (core % 2 + 1), :]
    m = {}
    m["xT"] = np.ascontiguousarray(xs.T).astype(np.float32)
    for w in ("Wq", "Wk", "Wv", "Wq2", "Wk2", "Wv2", "Wo"):
        m[w] = np.ascontiguousarray(np.asarray(inputs["L0_attn_" + w],
                                               dtype=np.float32))
    vecs = [inputs["L0_attn_bq"], inputs["L0_attn_bk"], inputs["L0_attn_bv"],
            inputs["L0_attn_bv2"], inputs["L0_attn_bo"],
            inputs["L0_attn_gamma"], inputs["L0_attn_beta"]]
    m["vecs"] = np.ascontiguousarray(np.concatenate(
        [np.asarray(v, dtype=np.float32).reshape(8, 128).T for v in vecs],
        axis=1))
    m["brow"] = np.concatenate(
        [np.asarray(inputs["L0_attn_bq2"], dtype=np.float32).reshape(1, -1),
         np.asarray(inputs["L0_attn_bk2"], dtype=np.float32).reshape(1, -1)],
        axis=1)
    m["cT"] = np.ascontiguousarray(
        np.asarray(inputs["L0_c"], dtype=np.float32)[b].reshape(8, 128).T)
    m["wmod"] = np.ascontiguousarray(np.asarray(inputs["L0_wmod"],
                                                dtype=np.float32))
    m["wiT"] = np.ascontiguousarray(
        np.asarray(inputs["L0_ffn_wi_w"], dtype=np.float32).T)
    m["woT"] = np.ascontiguousarray(
        np.asarray(inputs["L0_ffn_wo_w"], dtype=np.float32).T)
    m["Aall"] = np.ascontiguousarray(
        np.asarray(inputs["L0_ffn_lora_As"], dtype=np.float32).reshape(128, 1024))
    m["BallT"] = np.ascontiguousarray(
        np.asarray(inputs["L0_ffn_lora_Bs"],
                   dtype=np.float32).transpose(0, 2, 1).reshape(128, 4096))
    m["idx"] = np.asarray(inputs["L0_ffn_top_k_indices"]).reshape(1, 2).astype(
        np.int32)
    m["tkv"] = np.asarray(inputs["L0_ffn_top_k_values"],
                          dtype=np.float32).reshape(1, 2)
    constB = np.zeros((128, 272), dtype=np.float32)
    constB[:, 0:128] = 1.0 / 1024.0
    constB[:, 128:256] = np.eye(128, dtype=np.float32)
    constB[:, 256:272] = np.tile(np.eye(16, dtype=np.float32), (8, 1))
    m["constB"] = constB
    constF = np.zeros((128, 8), dtype=np.float32)
    constF[:, 0] = np.arange(128) // 16
    constF[:, 1] = -0.5
    constF[:, 2] = 1e-5
    constF[:, 3] = 1e-3
    constF[:, 4] = 0.0
    constF[:, 5] = 1.0
    m["constF"] = constF
    sel = np.zeros((16, 1024), dtype=np.float32)
    for h in range(16):
        c, hh = h // 2, h % 2
        sel[h, 128 * c + 64 * hh:128 * c + 64 * hh + 64] = 1.0
    m["sel"] = sel
    m["ones_in"] = np.ones((1, 128), dtype=np.float32)
    return m


def kernel(**inputs):
    nc, dbg = _get_nc()
    in_maps = [_host_inputs(inputs, core) for core in range(8)]
    res = run_bass_kernel_spmd(nc, in_maps, core_ids=list(range(8)))
    outs = [np.asarray(res.results[c]["out_xT"]).T for c in range(8)]
    return np.concatenate(outs, axis=0).reshape(4, 4096, 1024).astype(np.float32)


# version 07
# speedup vs baseline: 1.4255x; 1.0380x over previous
import numpy as np
import concourse.bass as bass
import concourse.mybir as mybir
from concourse.bass_utils import run_bass_kernel_spmd

F32 = mybir.dt.float32
BF16 = mybir.dt.bfloat16
I32 = mybir.dt.int32
AF = mybir.ActivationFunctionType
ALU = mybir.AluOpType
AX = mybir.AxisListType

T = 2048
D = 1024
DC = 8
NW = 4
NT = 16
H = 16
DH = 64
FF = 4096
FC = 32
TCN = 8
EPS_IN = 1e-3
EPS_OUT = 1e-5
EPS_QK = 1e-6


class _Cut(Exception):
    pass


def build_nc(debug=(), stop_after=None):
    nc = bass.Bass("TRN2", target_bir_lowering=False)
    dbgn = set(debug)

    def din(name, shape, dtype=F32):
        return nc.dram_tensor(name, list(shape), dtype, kind="ExternalInput").ap()

    xT = din("xT", [D, T])
    WTS = {w: din(w, [D, D]) for w in ("Wq", "Wk", "Wv", "Wq2", "Wk2", "Wv2", "Wo")}
    vecs_in = din("vecs", [128, 7 * DC])
    brow_in = din("brow", [1, 2 * D])
    cT_in = din("cT", [128, DC])
    wmod_in = din("wmod", [D, 6 * D])
    wiT_in = din("wiT", [D, FF])
    woT_in = din("woT", [FF, D])
    Aall_in = din("Aall", [128, D])
    BallT_in = din("BallT", [128, FF])
    idx_in = din("idx", [1, 2], I32)
    tkv_in = din("tkv", [1, 2])
    constB_in = din("constB", [128, 272])
    constF_in = din("constF", [128, 8])
    sel_in = din("sel", [16, 1024])
    ones_in = din("ones_in", [1, 128])

    out_xT = nc.dram_tensor("out_xT", [D, T], F32, kind="ExternalOutput").ap()
    cc_in = nc.dram_tensor("cc_in", [64, H * DH], BF16, kind="Internal").ap()
    cc_out = nc.dram_tensor("cc_out", [64, H * DH], BF16, kind="Internal").ap()

    dbg = {}
    def dbgt(name, shape, dtype=F32):
        if name in dbgn:
            dbg[name] = nc.dram_tensor("dbg_" + name, list(shape), dtype,
                                       kind="ExternalOutput").ap()
    dbgt("modT", [128, 48]); dbgt("hT", [128, 16384], BF16)
    dbgt("q2", [128, 16384], BF16); dbgt("k2", [128, 16384], BF16)
    dbgt("Stot", [64, 1024], BF16); dbgt("qks", [128, 256])
    dbgt("v2", [128, 16384], BF16); dbgt("yT", [128, 16384], BF16)
    dbgt("x1T", [128, 16384], BF16); dbgt("h2tc0", [128, 2048], BF16)
    dbgt("interC0", [128, 8192], BF16); dbgt("BselT", [128, 4096], BF16)
    dbgt("u01", [64, 512], BF16); dbgt("OH", [128, 32], BF16)
    dbgt("qksrT", [128, 2048], BF16); dbgt("ASEL", [128, 256], BF16)
    dbgt("recip", [128, 256], BF16); dbgt("qks2", [128, 256])
    dbgt("DERV", [128, 64])

    A = nc.alloc_sbuf_tensor
    SX = A("SX", [128, 16384], BF16)
    SH = A("SH", [128, 16384], BF16)
    SB = A("SB", [128, 16384], BF16)
    SC = A("SC", [128, 16384], BF16)
    SAa = A("SAa", [128, 4096], BF16)
    SAb = A("SAb", [128, 4096], BF16)
    W0 = A("W0", [128, 8192], BF16)
    W1 = A("W1", [128, 8192], BF16)
    POOL = A("POOL", [128, 5120], BF16)
    CONST = A("CONSTB", [128, 272], BF16)
    P16T = A("P16T", [128, 8], F32)
    VECS = A("VECS", [128, 7 * DC], F32)
    DERV = A("DERV", [128, 8 * DC], F32)
    MODT = A("MODT", [128, 48], F32)
    CT = A("CT", [128, DC], F32)
    SIG = A("SIG", [128, DC], F32)
    SILU = A("SILU", [128, DC], BF16)
    ARENA = A("ARENA", [128, 4096], BF16)
    SEL = A("SEL", [16, 1024], BF16)
    UB = A("UB", [64, 512], BF16)
    BROW = A("BROW", [1, D], BF16)
    STF = A("STF", [128, 1024], F32)
    ASEL = A("ASEL", [128, 256], BF16)
    TKB = A("TKB", [128, 2], F32)
    IDXS = A("IDXS", [1, 2], I32)
    IDXF = A("IDXF", [1, 2], F32)
    IDXB = A("IDXB", [1, 2], BF16)
    OHE = A("OHE", [128, 2], F32)
    OH = A("OH", [128, 32], BF16)
    TKS = A("TKS", [1, 2], BF16)
    ONES1 = A("ONES1", [1, 128], BF16)

    MUT = POOL[:, 0:512]
    RST = POOL[:, 512:1024]
    TMPE = POOL[:, 1024:2048]
    TMPF = POOL[:, 2048:3072]
    QKSRT = POOL[0:16, 3072:5120]
    SPART = POOL[0:64, 1024:2048]
    ROW0 = POOL[0:1, 0:512]
    ROW1 = POOL[0:1, 512:1024]
    H2TC = POOL[:, 0:2048]
    R0P = POOL[:, 2048:2816]
    R1P = POOL[:, 2816:3584]
    MUT2 = POOL[:, 3584:3840]
    RST2 = POOL[:, 3840:4096]
    TMP6 = POOL[:, 4096:4608]
    XSQ = POOL[:, 4608:5120]
    QKS = STF[:, 768:1024]
    STOT = POOL[0:64, 3072:4096]
    STOT64 = POOL[64:128, 3072:4096]

    Jc = CONST[:, 0:128]
    I128 = CONST[:, 128:256]
    I16t = CONST[:, 256:272]

    PA = nc.alloc_psum_tensor("PA", [128, 1024], F32)
    PB = nc.alloc_psum_tensor("PB", [128, 1024], F32)
    PC = nc.alloc_psum_tensor("PC", [128, 1024], F32)
    PD = nc.alloc_psum_tensor("PD", [128, 512], F32)
    PTB = nc.alloc_psum_tensor("PTB", [128, 1024], BF16)

    sems = {}
    for name in ("t", "v", "sc", "g", "s", "dg", "ds"):
        sems[name] = nc.alloc_semaphore("sem_" + name)

    for col, val in ((1, -0.5), (2, 1e-5), (3, 1e-3), (4, 0.0), (5, 1.0)):
        nc.const_aps.register(F32, float(val), P16T[:, col:col + 1])

    def W0b(k, m):
        return W0[:, 1024 * k + 128 * m: 1024 * k + 128 * m + 128]

    def W1b(k, m):
        return W1[:, 1024 * k + 128 * m: 1024 * k + 128 * m + 128]

    def wiTblk(k, fc):
        if k < 4:
            return SB[:, 4096 * k + 128 * fc: 4096 * k + 128 * fc + 128]
        if k < 6:
            return W1[:, 4096 * (k - 4) + 128 * fc: 4096 * (k - 4) + 128 * fc + 128]
        if k == 6:
            return SAa[:, 128 * fc: 128 * fc + 128]
        return SAb[:, 128 * fc: 128 * fc + 128]

    def woTblk(fc, dp):
        if fc < 16:
            return SH[:, 1024 * fc + 128 * dp: 1024 * fc + 128 * dp + 128]
        return SC[:, 1024 * (fc - 16) + 128 * dp: 1024 * (fc - 16) + 128 * dp + 128]

    def wmod_rhs(k, n):
        if k < 6:
            base = [SB, SB, SC, SC, SH, SH][k]
            off = (k % 2) * 6144
            return base[:, off + 512 * n: off + 512 * n + 512]
        if k == 6:
            if n < 8:
                return SAa[:, 512 * n: 512 * n + 512]
            return SAb[:, 512 * (n - 8): 512 * (n - 8) + 512]
        if n < 8:
            return SB[:, 12288 + 512 * n: 12288 + 512 * n + 512]
        return SC[:, 12288 + 512 * (n - 8): 12288 + 512 * (n - 8) + 512]

    def emit(my, e):
        def cut(name):
            if stop_after == name:
                raise _Cut()
        Cn = {k: 0 for k in ("t", "v", "sc", "g", "s")}
        D16 = {"g": 0, "s": 0}
        lastw = {}
        bankfree = {}

        def op(eng2, builder, sem=True):
            if my == eng2:
                ins = builder(e)
                if sem:
                    ins.then_inc(sems[eng2], 1)
            if sem:
                Cn[eng2] += 1
            return Cn[eng2]

        def w(eng2, sem_name, val):
            if my != eng2 or val is None or val <= 0:
                return
            if lastw.get(sem_name, -1) >= val:
                return
            lastw[sem_name] = val
            e.wait_ge(sems[sem_name], val)

        def dma(q, out, in_, **kw):
            D16[q] += 16
            if my == q:
                e.dma_start(out=out, in_=in_, **kw).then_inc(
                    sems["dg" if q == "g" else "ds"], 16)
            return D16[q]

        def wdma(eng2, q, tok):
            if tok:
                w(eng2, "dg" if q == "g" else "ds", tok)

        def fence(eng2):
            w(eng2, eng2, Cn[eng2])

        def mm(out, lhsT, rhs, start, stop, sem=False):
            return op("t", lambda e: e.matmul(out, lhsT, rhs, start=start,
                                              stop=stop), sem=sem)

        def use_bank(*keys):
            for k in keys:
                if k in bankfree:
                    eng2, tok = bankfree[k]
                    w("t", eng2, tok)

        def free_bank(key, eng2, tok):
            bankfree[key] = (eng2, tok)

        tok_p16 = dma("s", P16T[:, :], constF_in[:, :])
        tok_vecs = dma("s", VECS[:, :], vecs_in[:, :])
        tok_ct = dma("s", CT[:, :], cT_in[:, :])
        tok_idx = dma("s", IDXS[:, :], idx_in[:, :])

        tok_consts = dma("g", CONST[:, :], constB_in[:, :])
        tok_ones1 = dma("g", ONES1[:, :], ones_in[:, :])
        tok_sel = dma("g", SEL[:, :], sel_in[:, :])
        tok_brow = dma("g", BROW[:, :], brow_in[0:1, 0:D])
        wmod_tok = []
        for k in range(6):
            dsts = [SB, SB, SC, SC, SH, SH][k]
            off = (k % 2) * 6144
            wmod_tok.append(dma("g", dsts[:, off:off + 6144],
                                wmod_in[128 * k:128 * (k + 1), :]))
        wmod_tok.append(dma("g", SAa[:, 0:4096], wmod_in[768:896, 0:4096]))
        wmod_tok.append(dma("g", SAb[:, 0:2048], wmod_in[768:896, 4096:6144]))
        wmod_tok.append(dma("g", SB[:, 12288:16384], wmod_in[896:1024, 0:4096]))
        wmod_tok.append(dma("g", SC[:, 12288:14336], wmod_in[896:1024, 4096:6144]))
        tok_aall = dma("g", W1[:, 0:1024], Aall_in[:, :])
        tok_ball = dma("g", ARENA[:, :], BallT_in[:, :])
        tok_tkf = dma("g", TKS[:, :], tkv_in[:, :])
        tok_x = dma("g", SX.ap().rearrange("p (d t) -> p d t", d=DC),
                    xT.rearrange("(d p) t -> p d t", p=128))

        def wload(buf, name, gate=None):
            if gate is not None:
                w("g", gate[0], gate[1])
            return dma("g", buf.ap().rearrange("p (k m j) -> p k m j", k=8, j=128),
                       WTS[name].rearrange("(k p) (m j) -> p k m j", p=128, j=128))

        tok_wq = wload(W0, "Wq")

        wdma("sc", "s", tok_ct)
        t_sig = op("sc", lambda e: e.activation(SIG[:, :], CT[:, :], AF.Sigmoid))
        w("v", "sc", t_sig)
        t_silu = op("v", lambda e: e.tensor_mul(SILU[:, :], CT[:, :], SIG[:, :]))

        rowfree = [0, 0]
        wdma("t", "g", tok_consts)
        for n in range(12):
            key = ("PA", n % 2)
            bank = PA[0:1, 0:512] if n % 2 == 0 else PA[0:1, 512:1024]
            rowslot = ROW0 if n % 2 == 0 else ROW1
            wdma("t", "g", wmod_tok[9])
            w("t", "v", t_silu)
            use_bank(key)
            for k in range(8):
                mm(bank, SILU[:, k:k + 1], wmod_rhs(k, n),
                   start=(k == 0), stop=(k == 7), sem=(k == 7))
            w("v", "t", Cn["t"])
            w("v", "t", rowfree[n % 2])
            tv = op("v", lambda e, bank=bank, rowslot=rowslot: e.tensor_copy(
                rowslot, bank))
            free_bank(key, "v", tv)
            w("t", "v", tv)
            use_bank(("PTB7",))
            for c in range(4):
                op("t", lambda e, c=c, n=n, rowslot=rowslot: e.transpose(
                    PTB[:, 8 * (n % 2) + 2 * c: 8 * (n % 2) + 2 * c + 1],
                    rowslot[0:1, 128 * c:128 * c + 128], I128[0:1, 0:1]),
                    sem=(c == 3))
            rowfree[n % 2] = Cn["t"]
            w("v", "t", Cn["t"])
            vv, cc4 = n // 2, 4 * (n % 2)
            ptb_view = PTB[:, 8 * (n % 2): 8 * (n % 2) + 8].rearrange(
                "p (c two) -> p c two", two=2)[:, :, 0:1]
            tvv = op("v", lambda e, vv=vv, cc4=cc4, ptb_view=ptb_view:
                     e.tensor_copy(
                         MODT[:, 8 * vv + cc4: 8 * vv + cc4 + 4], ptb_view))
            free_bank(("PTB7",), "v", tvv)
        t_modr = Cn["v"]

        wdma("v", "s", tok_vecs)
        fence("v")
        op("v", lambda e: e.tensor_scalar(DERV[:, 0:8], MODT[:, 8:16], 1.0, None,
                                          op0=ALU.add))
        op("v", lambda e: e.tensor_copy(DERV[:, 8:16], MODT[:, 0:8]))
        op("v", lambda e: e.tensor_scalar(DERV[:, 16:24], MODT[:, 32:40], 1.0, None,
                                          op0=ALU.add))
        op("v", lambda e: e.tensor_copy(DERV[:, 24:32], MODT[:, 24:32]))
        for (src0, dst) in ((16, 32), (40, 40)):
            op("v", lambda e, s=src0: e.tensor_scalar(
                STF[:, 8:16], MODT[:, s:s + 8], 0.0, None, op0=ALU.is_ge))
            fence("v")
            op("v", lambda e: e.tensor_scalar(
                STF[:, 8:16], STF[:, 8:16], 2.0, -1.0, op0=ALU.mult, op1=ALU.add))
            fence("v")
            op("v", lambda e, s=src0: e.tensor_mul(
                STF[:, 0:8], MODT[:, s:s + 8], STF[:, 8:16]))
            fence("v")
            op("v", lambda e: e.tensor_scalar(
                STF[:, 0:8], STF[:, 0:8], 0.25, None, op0=ALU.max))
            fence("v")
            op("v", lambda e, d=dst: e.tensor_mul(
                DERV[:, d:d + 8], STF[:, 0:8], STF[:, 8:16]))
            fence("v")
        op("v", lambda e: e.tensor_mul(DERV[:, 48:56], DERV[:, 32:40],
                                       VECS[:, 40:48]))
        t_derv = op("v", lambda e: e.tensor_mul(DERV[:, 56:64], DERV[:, 32:40],
                                                VECS[:, 48:56]))
        if "modT" in dbg:
            w("s", "v", t_derv)
            dma("s", dbg["modT"][:, :], MODT[:, :])
        if "DERV" in dbg:
            w("s", "v", t_derv)
            dma("s", dbg["DERV"][:, :], DERV[:, :])
        cut("p0a")

        wdma("v", "s", tok_idx)
        op("v", lambda e: e.tensor_copy(IDXF[:, :], IDXS[:, :]))
        fence("v")
        op("v", lambda e: e.tensor_scalar(IDXF[:, :], IDXF[:, :], 0.0, 7.0,
                                          op0=ALU.max, op1=ALU.min))
        fence("v")
        t_idxb = op("v", lambda e: e.tensor_copy(IDXB[:, :], IDXF[:, :]))
        wdma("t", "g", tok_ones1)
        w("t", "v", t_idxb)
        use_bank(("PD",))
        mm(PD[0:128, 0:2], ONES1[0:1, 0:128], IDXB[0:1, 0:2], True, True, sem=True)
        w("v", "t", Cn["t"])
        wdma("v", "s", tok_p16)
        t_ohe = op("v", lambda e: e.tensor_scalar(OHE[:, :], PD[0:128, 0:2],
                                                  P16T[:, 0:1], None,
                                                  op0=ALU.is_equal))
        t_ohz = op("g", lambda e: e.memset(OH[:, :], 0))
        wdma("v", "g", tok_consts)
        w("v", "g", t_ohz)
        fence("v")
        for k in range(2):
            op("v", lambda e, k=k: e.tensor_scalar(
                OH[:, 16 * k:16 * k + 16], I16t, OHE[:, k:k + 1], None,
                op0=ALU.mult))
        t_oh = Cn["v"]
        free_bank(("PD",), "v", t_ohe)
        wdma("t", "g", tok_tkf)
        use_bank(("PD",))
        mm(PD[0:128, 2:4], ONES1[0:1, 0:128], TKS[0:1, 0:2], True, True, sem=True)
        w("v", "t", Cn["t"])
        t_tkb = op("v", lambda e: e.tensor_copy(TKB[:, :], PD[0:128, 2:4]))
        free_bank(("PD",), "v", t_tkb)

        wdma("t", "g", tok_aall)
        w("t", "v", t_oh)
        for d in range(DC):
            key = ("PC", d % 2)
            bank = PC[:, 0:32] if d % 2 == 0 else PC[:, 512:544]
            use_bank(key)
            mm(bank, W1[:, 128 * d:128 * (d + 1)], OH[:, :], True, True, sem=True)
            w("v", "t", Cn["t"])
            tv = op("v", lambda e, d=d, bank=bank: e.tensor_copy(
                ASEL[:, 32 * d:32 * (d + 1)], bank))
            free_bank(key, "v", tv)
        wdma("t", "g", tok_ball)
        for n in range(8):
            use_bank(("PD",))
            mm(PD[0:32, 0:512], OH[:, :], ARENA[:, 512 * n:512 * (n + 1)],
               True, True, sem=True)
            w("v", "t", Cn["t"])
            tv = op("v", lambda e, n=n: e.tensor_copy(
                ARENA[0:32, 512 * n:512 * (n + 1)], PD[0:32, 0:512]))
            free_bank(("PD",), "v", tv)
        t_bsel = Cn["v"]
        t_p0b = Cn["t"]
        if "OH" in dbg:
            w("s", "v", t_bsel)
            dma("s", dbg["OH"][:, :], OH[:, :])
        if "BselT" in dbg:
            w("s", "v", t_bsel)
            dma("s", dbg["BselT"][:, :], ARENA[:, :])
        if "ASEL" in dbg:
            w("s", "v", t_bsel)
            dma("s", dbg["ASEL"][:, :], ASEL[:, :])
        cut("p0b")

        wdma("t", "g", tok_x)

        def ln_stats(src_chunk_ap, eps, sq_scratch):
            use_bank(("PA", 0))
            for d in range(DC):
                mm(PA[:, 0:512], Jc, src_chunk_ap(d), start=(d == 0),
                   stop=(d == DC - 1), sem=(d == DC - 1))
            t_mu_mm = Cn["t"]
            w("sc", "t", Cn["t"])
            for d in range(DC):
                op("sc", lambda e, d=d: e.activation(
                    sq_scratch(d), src_chunk_ap(d), AF.Square))
            t_sq = Cn["sc"]
            use_bank(("PA", 1))
            w("t", "sc", t_sq)
            for d in range(DC):
                mm(PA[:, 512:1024], Jc, sq_scratch(d), start=(d == 0),
                   stop=(d == DC - 1), sem=(d == DC - 1))
            t_ss_mm = Cn["t"]
            w("v", "t", t_mu_mm)
            op("v", lambda e: e.tensor_copy(MUT, PA[:, 0:512]))
            t_mucp = op("v", lambda e: e.tensor_copy(STF[:, 0:512], PA[:, 0:512]))
            free_bank(("PA", 0), "v", t_mucp)
            op("v", lambda e: e.tensor_mul(STF[:, 512:1024], STF[:, 0:512],
                                           STF[:, 0:512]))
            w("v", "t", t_ss_mm)
            t_var = op("v", lambda e: e.tensor_sub(STF[:, 0:512], PA[:, 512:1024],
                                                   STF[:, 512:1024]))
            free_bank(("PA", 1), "v", t_var)
            w("sc", "v", t_var)
            wdma("sc", "s", tok_p16)
            op("sc", lambda e, eps=eps: e.activation(STF[:, 512:1024],
                                                     STF[:, 0:512], AF.Ln,
                                                     bias=float(eps)))
            t_rst = op("sc", lambda e: e.activation(RST, STF[:, 512:1024],
                                                    AF.Exp, scale=-0.5))
            return t_rst

        for w_ in range(NW):
            xs = lambda d, w_=w_: SX[:, 2048 * d + 512 * w_:
                                     2048 * d + 512 * w_ + 512]
            sqs = lambda d: SAa[:, 512 * d: 512 * d + 512]
            t_rst = ln_stats(xs, EPS_OUT, sqs)
            w("v", "sc", t_rst)
            for d in range(DC):
                hs = SH[:, 2048 * d + 512 * w_: 2048 * d + 512 * w_ + 512]
                op("v", lambda e, d=d, w_=w_: e.tensor_sub(
                    TMPE[:, 0:512],
                    SX[:, 2048 * d + 512 * w_: 2048 * d + 512 * w_ + 512],
                    MUT))
                op("v", lambda e: e.tensor_mul(TMPE[:, 0:512], TMPE[:, 0:512],
                                               RST))
                op("v", lambda e, d=d, hs=hs: e.tensor_scalar(
                    hs, TMPE[:, 0:512], DERV[:, d:d + 1],
                    DERV[:, 8 + d:8 + d + 1], op0=ALU.mult, op1=ALU.add))
        t_h = Cn["v"]
        tok_dbg_h = None
        if "hT" in dbg:
            w("s", "v", t_h)
            tok_dbg_h = dma("s", dbg["hT"][:, :], SH[:, :])
        cut("p1")

        tok_wq2 = wload(W1, "Wq2", gate=("t", t_p0b))

        FMROT = [(PB, 0), (PB, 512), (PC, 0), (PC, 512)]

        def fm_proj(wblk, hoff_w, dstc, bias_col):
            for m in range(DC):
                bk, off = FMROT[m % 4]
                key = ("PB" if bk is PB else "PC", off // 512)
                bank = bk[:, off:off + 512]
                use_bank(key)
                for k in range(DC):
                    mm(bank, wblk(k, m),
                       SH[:, 2048 * k + hoff_w: 2048 * k + hoff_w + 512],
                       start=(k == 0), stop=(k == DC - 1), sem=(k == DC - 1))
                w("sc", "t", Cn["t"])
                tv = op("sc", lambda e, m=m, bank=bank: e.activation(
                    dstc(m), bank, AF.Identity,
                    bias=VECS[:, bias_col * 8 + m: bias_col * 8 + m + 1]))
                free_bank(key, "sc", tv)

        TPROT = [(PA, "PA"), (PB, "PB"), (PC, "PC")]

        def tok_proj(sax, dstsb, wbuf, base_tile):
            for tt in range(4):
                i = base_tile + tt
                PT, pnm = TPROT[i % 3]
                use_bank((pnm, 0), (pnm, 1))
                for half in range(2):
                    bank = PT[:, 0:512] if half == 0 else PT[:, 512:1024]
                    for k in range(DC):
                        mm(bank,
                           sax[:, 512 * k + 128 * tt: 512 * k + 128 * tt + 128],
                           wbuf[:, 1024 * k + 512 * half:
                                1024 * k + 512 * half + 512],
                           start=(k == 0), stop=False)
                    mm(bank, ONES1[0:1, 0:128],
                       BROW[0:1, 512 * half: 512 * half + 512],
                       start=False, stop=True, sem=(half == 1))
                t_mm = Cn["t"]
                w("sc", "t", t_mm)
                op("sc", lambda e, i=i, PT=PT: e.activation(
                    dstsb[:, 1024 * i:1024 * (i + 1)], PT[:, :], AF.Relu))
                t_relu = Cn["sc"]
                w("v", "sc", t_relu)
                t_min = op("v", lambda e, PT=PT: e.tensor_scalar(
                    TMPE, PT[:, 0:1024], 0.0, None, op0=ALU.min))
                free_bank((pnm, 0), "v", t_min)
                free_bank((pnm, 1), "v", t_min)
                w("sc", "v", t_min)
                t_exp = op("sc", lambda e: e.activation(TMPE, TMPE, AF.Exp))
                w("v", "sc", t_exp)
                op("v", lambda e, i=i: e.tensor_add(
                    dstsb[:, 1024 * i:1024 * (i + 1)],
                    dstsb[:, 1024 * i:1024 * (i + 1)], TMPE))

        wdma("t", "g", tok_wq)
        wdma("t", "g", tok_brow)
        w("t", "v", t_h)
        t_wq_done = None
        for w_ in range(NW):
            sax = SAa if w_ % 2 == 0 else SAb
            dstc = lambda m, sax=sax: sax[:, 512 * m: 512 * m + 512]
            fm_proj(W0b, 512 * w_, dstc, 0)
            if w_ == NW - 1:
                t_wq_done = Cn["t"]
            wdma("t", "g", tok_wq2)
            w("t", "sc", Cn["sc"])
            tok_proj(sax, SB, W1, 4 * w_)
        t_qchain = Cn["t"]
        t_q2v = Cn["v"]
        cut("qchain")
        tok_wk = wload(W0, "Wk", gate=("t", t_wq_done))
        tok_wk2 = wload(W1, "Wk2", gate=("t", t_qchain))
        w("g", "v", t_q2v)
        tok_brow2 = dma("g", BROW[:, :], brow_in[0:1, D:2 * D])

        t_wk_done = None
        for w_ in range(NW):
            sax = SAa if w_ % 2 == 0 else SAb
            wdma("t", "g", tok_wk)
            dstc = lambda m, sax=sax: sax[:, 512 * m: 512 * m + 512]
            fm_proj(W0b, 512 * w_, dstc, 1)
            if w_ == NW - 1:
                t_wk_done = Cn["t"]
            wdma("t", "g", tok_wk2)
            wdma("t", "g", tok_brow2)
            w("t", "sc", Cn["sc"])
            tok_proj(sax, SC, W1, 4 * w_)
            for tt in range(4):
                i = 4 * w_ + tt
                op("v", lambda e, i=i: e.tensor_mul(
                    TMPF, SB[:, 1024 * i:1024 * (i + 1)],
                    SC[:, 1024 * i:1024 * (i + 1)]))
                op("v", lambda e, i=i: e.tensor_reduce(
                    QKS[:, 16 * i:16 * (i + 1)],
                    TMPF.rearrange("p (h d) -> p h d", d=DH),
                    axis=AX.X, op=ALU.add))
        t_k2done = Cn["v"]
        t_kchain = Cn["t"]
        tok_wv = wload(W0, "Wv", gate=("t", t_wk_done))
        tok_wv2 = wload(W1, "Wv2", gate=("t", t_kchain))
        tok_dbg_q2 = tok_dbg_k2 = tok_dbg_qks = None
        if "q2" in dbg:
            w("s", "v", t_k2done)
            tok_dbg_q2 = dma("s", dbg["q2"][:, :], SB[:, :])
        if "k2" in dbg:
            w("s", "v", t_k2done)
            tok_dbg_k2 = dma("s", dbg["k2"][:, :], SC[:, :])
        if "qks" in dbg:
            w("s", "v", t_k2done)
            tok_dbg_qks = dma("s", dbg["qks"][:, :], QKS[:, :])
        cut("kchain")

        for i in range(NT):
            op("v", lambda e, i=i: e.tensor_scalar(
                STF[:, 16 * i:16 * (i + 1)], QKS[:, 16 * i:16 * (i + 1)],
                float(EPS_QK), None, op0=ALU.add))
        t_qe = Cn["v"]
        w("sc", "v", t_qe)
        op("sc", lambda e: e.activation(STF[:, 512:768], STF[:, 0:256], AF.Ln))
        fence("sc")
        t_recip = op("sc", lambda e: e.activation(
            TMPF[:, 0:256], STF[:, 512:768], AF.Exp, scale=-1.0))
        if "recip" in dbg:
            w("s", "sc", t_recip)
            dma("s", dbg["recip"][:, :], TMPF[:, 0:256])
        if "qks2" in dbg:
            w("s", "v", t_qe)
            dma("s", dbg["qks2"][:, :], QKS[:, :])

        w("t", "v", t_k2done)
        use_bank(("PA", 0), ("PA", 1))
        for h in range(H):
            for i in range(NT):
                mm(PA[0:64, 64 * h:64 * (h + 1)],
                   SC[:, 1024 * i + 64 * h: 1024 * i + 64 * h + 64],
                   SB[:, 1024 * i + 64 * h: 1024 * i + 64 * h + 64],
                   start=(i == 0), stop=(i == NT - 1),
                   sem=(i == NT - 1 and h in (7, 15)))
        t_sdone = Cn["t"]
        w("v", "t", t_sdone)
        op("v", lambda e: e.tensor_copy(SPART[:, 0:512], PA[0:64, 0:512]))
        t_sev = op("v", lambda e: e.tensor_copy(SPART[:, 512:1024],
                                                PA[0:64, 512:1024]))
        free_bank(("PA", 0), "v", t_sev)
        free_bank(("PA", 1), "v", t_sev)
        cut("sphase")
        w("g", "v", t_sev)
        tok_ccin = dma("g", cc_in[:, :], SPART[:, :])
        wdma("g", "g", tok_ccin)
        t_cc = op("g", lambda e: e.collective_compute(
            "AllReduce", ALU.add,
            replica_groups=[[0, 1], [2, 3], [4, 5], [6, 7]],
            ins=[cc_in[:, :]], outs=[cc_out[:, :]]))

        cut("cc")
        w("t", "sc", t_recip)
        for i in range(NT):
            po = 128 * (i % 8)
            use_bank(("PTB7",))
            op("t", lambda e, i=i, po=po: e.transpose(
                PTB[0:16, po:po + 128], TMPF[:, 16 * i:16 * (i + 1)], I128),
                sem=True)
            w("v", "t", Cn["t"])
            tv = op("v", lambda e, i=i, po=po: e.tensor_copy(
                QKSRT[:, 128 * i:128 * (i + 1)], PTB[0:16, po:po + 128]))
            free_bank(("PTB7",), "v", tv)
        t_qksr = Cn["v"]
        if "qksrT" in dbg:
            w("s", "v", t_qksr)
            dma("s", dbg["qksrT"][:, :], POOL[:, 3072:5120])

        wdma("t", "g", tok_sel)
        t_wv_done = None
        for w_ in range(NW):
            sax = SAa if w_ % 2 == 0 else SAb
            wdma("t", "g", tok_wv)
            dstc = lambda m, sax=sax: sax[:, 512 * m: 512 * m + 512]
            fm_proj(W0b, 512 * w_, dstc, 2)
            if w_ == NW - 1:
                t_wv_done = Cn["t"]
            wdma("t", "g", tok_wv2)
            w("t", "sc", Cn["sc"])
            for m in range(DC):
                bk, off = FMROT[m % 4]
                key = ("PB" if bk is PB else "PC", off // 512)
                bank = bk[:, off:off + 512]
                use_bank(key)
                for k in range(DC):
                    mm(bank, W1b(k, m), sax[:, 512 * k: 512 * k + 512],
                       start=(k == 0), stop=(k == DC - 1), sem=(k == DC - 1))
                w("sc", "t", Cn["t"])
                w("sc", "t", t_sdone)
                if tok_dbg_q2:
                    wdma("sc", "s", tok_dbg_q2)
                if tok_dbg_qks:
                    wdma("sc", "s", tok_dbg_qks)
                w("sc", "v", t_k2done)
                tv = op("sc", lambda e, m=m, w_=w_, bank=bank: e.activation(
                    SB[:, 2048 * m + 512 * w_: 2048 * m + 512 * w_ + 512], bank,
                    AF.Identity, bias=VECS[:, 3 * 8 + m: 3 * 8 + m + 1]))
                free_bank(key, "sc", tv)
            t_v2w = Cn["sc"]
            w("t", "v", t_qksr)
            for c in range(DC):
                j = c
                bk, off = FMROT[j % 4]
                key = ("PB" if bk is PB else "PC", off // 512)
                bank = bk[:, off:off + 512]
                use_bank(key)
                mm(bank, SEL[0:16, 128 * c:128 * (c + 1)],
                   QKSRT[:, 512 * w_:512 * (w_ + 1)], True, True, sem=True)
                w("v", "t", Cn["t"])
                w("v", "sc", t_v2w)
                tv = op("v", lambda e, c=c, w_=w_, bank=bank: e.tensor_mul(
                    SB[:, 2048 * c + 512 * w_: 2048 * c + 512 * w_ + 512],
                    SB[:, 2048 * c + 512 * w_: 2048 * c + 512 * w_ + 512], bank))
                free_bank(key, "v", tv)
        t_v2 = Cn["sc"]
        t_vchain = Cn["t"]
        tok_wo = wload(W0, "Wo", gate=("t", t_wv_done))
        t_v2s = Cn["v"]
        t_bcast_t = Cn["t"]
        tok_dbg_v2 = None
        if "v2" in dbg:
            w("s", "v", t_v2s)
            tok_dbg_v2 = dma("s", dbg["v2"][:, :], SB[:, :])
        cut("v2s")

        w("g", "g", t_cc)
        w("g", "t", t_bcast_t)
        dma("g", STOT[:, :], cc_out[:, :])
        tok_stot = dma("g", STOT64[:, :], cc_out[:, :])
        if "Stot" in dbg:
            wdma("s", "g", tok_stot)
            dma("s", dbg["Stot"][:, :], STOT[:, :])

        wdma("t", "g", tok_stot)
        w("t", "v", t_v2s)
        for nn in range(NW):
            sax = SAa if nn % 2 == 0 else SAb
            for c in range(DC):
                for hh in range(2):
                    h = 2 * c + hh
                    j = c * 2 + hh
                    bk, off = FMROT[j % 4]
                    key = ("PB" if bk is PB else "PC", off // 512)
                    bank = bk[0:64, off:off + 512]
                    use_bank(key)
                    ST = STOT if hh == 0 else STOT64
                    mm(bank, ST[:, 64 * h:64 * (h + 1)],
                       SB[64 * hh:64 * (hh + 1),
                          2048 * c + 512 * nn: 2048 * c + 512 * nn + 512],
                       True, True, sem=True)
                    eng2 = "sc" if j % 2 == 0 else "v"
                    w(eng2, "t", Cn["t"])
                    if eng2 == "sc":
                        tv = op("sc", lambda e, c=c, hh=hh, bank=bank, sax=sax:
                                e.copy(
                                    sax[64 * hh:64 * (hh + 1),
                                        512 * c:512 * (c + 1)], bank))
                    else:
                        tv = op("v", lambda e, c=c, hh=hh, bank=bank, sax=sax:
                                e.tensor_copy(
                                    sax[64 * hh:64 * (hh + 1),
                                        512 * c:512 * (c + 1)], bank))
                    free_bank(key, eng2, tv)
            t_aout_sc = Cn["sc"]
            t_aout_v = Cn["v"]
            t_aout = t_aout_sc
            wdma("t", "g", tok_wo)
            w("t", "sc", t_aout_sc)
            w("t", "v", t_aout_v)
            for m in range(DC):
                bk, off = FMROT[m % 4]
                key = ("PB" if bk is PB else "PC", off // 512)
                bank = bk[:, off:off + 512]
                tslot = POOL[:, 1024 + 512 * (m % 2):1024 + 512 * (m % 2) + 512]
                use_bank(key)
                for k in range(DC):
                    mm(bank, W0b(k, m), sax[:, 512 * k: 512 * k + 512],
                       start=(k == 0), stop=(k == DC - 1), sem=(k == DC - 1))
                w("sc", "t", Cn["t"])
                wdma("sc", "g", tok_ccin)
                w("sc", "v", Cn["v"])
                op("sc", lambda e, bank=bank, tslot=tslot, m=m: e.activation(
                    tslot, bank, AF.Identity,
                    bias=VECS[:, 4 * 8 + m: 4 * 8 + m + 1]))
                t_bo = Cn["sc"]
                free_bank(key, "sc", t_bo)
                w("v", "sc", t_bo)
                w("v", "t", t_sdone)
                if tok_dbg_k2:
                    wdma("v", "s", tok_dbg_k2)
                op("v", lambda e, m=m, nn=nn, tslot=tslot: e.tensor_add(
                    SC[:, 2048 * m + 512 * nn: 2048 * m + 512 * nn + 512],
                    tslot,
                    SH[:, 2048 * m + 512 * nn: 2048 * m + 512 * nn + 512]))
        t_y = Cn["v"]
        t_p3t = Cn["t"]
        tok_dbg_y = None
        if "yT" in dbg:
            w("s", "v", t_y)
            tok_dbg_y = dma("s", dbg["yT"][:, :], SC[:, :])
        cut("p3")

        for w_ in range(NW):
            ys = lambda d, w_=w_: SC[:, 2048 * d + 512 * w_:
                                     2048 * d + 512 * w_ + 512]
            sqs = lambda d: SAa[:, 512 * d: 512 * d + 512]
            if w_ == 0:
                w("sc", "v", t_y)
                w("t", "v", t_y)
            t_rst = ln_stats(ys, EPS_IN, sqs)
            w("v", "sc", t_rst)
            for d in range(DC):
                op("v", lambda e, d=d, w_=w_: e.tensor_sub(
                    TMPE[:, 0:512],
                    SC[:, 2048 * d + 512 * w_: 2048 * d + 512 * w_ + 512],
                    MUT))
                op("v", lambda e: e.tensor_mul(TMPE[:, 0:512], TMPE[:, 0:512],
                                               RST))
                op("v", lambda e, d=d: e.tensor_scalar(
                    TMPE[:, 0:512], TMPE[:, 0:512], DERV[:, 48 + d:48 + d + 1],
                    DERV[:, 56 + d:56 + d + 1], op0=ALU.mult, op1=ALU.add))
                op("v", lambda e, d=d, w_=w_: e.tensor_add(
                    SX[:, 2048 * d + 512 * w_: 2048 * d + 512 * w_ + 512],
                    SX[:, 2048 * d + 512 * w_: 2048 * d + 512 * w_ + 512],
                    TMPE[:, 0:512]))
        t_x1 = Cn["v"]
        t_p4t = Cn["t"]
        tok_dbg_x1 = None
        if "x1T" in dbg:
            w("s", "v", t_x1)
            tok_dbg_x1 = dma("s", dbg["x1T"][:, :], SX[:, :])
        cut("p4")

        wi_tok = []
        for kc in range(8):
            if kc < 4:
                dstap = SB[:, 4096 * kc:4096 * (kc + 1)]
                gate = ("t", t_p3t)
                wdma("g", "s", tok_dbg_v2)
            elif kc < 6:
                dstap = W1[:, 4096 * (kc - 4):4096 * (kc - 4) + 4096]
                gate = ("t", t_vchain)
            elif kc == 6:
                dstap = SAa[:, :]
                gate = ("t", t_p4t)
            else:
                dstap = SAb[:, :]
                gate = ("t", t_p4t)
            w("g", gate[0], gate[1])
            wi_tok.append(dma("g", dstap, wiT_in[128 * kc:128 * (kc + 1), :]))
        wo_tok = []
        for fc in range(FC):
            if fc < 16:
                dstap = SH[:, 1024 * fc:1024 * (fc + 1)]
                w("g", "v", t_y)
                wdma("g", "s", tok_dbg_h)
            else:
                dstap = SC[:, 1024 * (fc - 16):1024 * (fc - 16) + 1024]
                w("g", "t", t_p4t)
                w("g", "v", t_x1)
                wdma("g", "s", tok_dbg_y)
            wo_tok.append(dma("g", dstap, woT_in[128 * fc:128 * (fc + 1), :]))

        h2_tok = {}
        u_tok = {}
        aend_tok = {0: None}
        interC_tok = {}

        def ln2_stage(tc, t_aend_prev):
            xs2 = lambda d: SX[:, 2048 * d + 256 * tc: 2048 * d + 256 * tc + 256]
            use_bank(("PD",))
            w("t", "v", t_x1)
            wdma("v", "s", tok_dbg_x1)
            for d in range(DC):
                mm(PD[:, 0:256], Jc, xs2(d), start=(d == 0), stop=(d == DC - 1),
                   sem=(d == DC - 1))
            t_mu_mm = Cn["t"]
            use_bank(("PB", 0))
            w("sc", "t", Cn["t"])
            for d in range(DC):
                slot = XSQ[:, 256 * (d % 2):256 * (d % 2) + 256]
                if d >= 2:
                    w("sc", "t", Cn["t"])
                op("sc", lambda e, d=d, slot=slot: e.activation(
                    slot, xs2(d), AF.Square))
                w("t", "sc", Cn["sc"])
                mm(PB[:, 0:256], Jc, slot, start=(d == 0), stop=(d == DC - 1),
                   sem=True)
            t_ss_mm = Cn["t"]
            w("v", "t", t_mu_mm)
            op("v", lambda e: e.tensor_copy(MUT2, PD[:, 0:256]))
            t_mucp = op("v", lambda e: e.tensor_copy(STF[:, 0:256], PD[:, 0:256]))
            free_bank(("PD",), "v", t_mucp)
            op("v", lambda e: e.tensor_mul(STF[:, 256:512], STF[:, 0:256],
                                           STF[:, 0:256]))
            w("v", "t", t_ss_mm)
            t_var = op("v", lambda e: e.tensor_sub(STF[:, 0:256], PB[:, 0:256],
                                                   STF[:, 256:512]))
            free_bank(("PB", 0), "v", t_var)
            w("sc", "v", t_var)
            op("sc", lambda e: e.activation(STF[:, 256:512], STF[:, 0:256],
                                            AF.Ln, bias=float(EPS_OUT)))
            t_rst2 = op("sc", lambda e: e.activation(RST2, STF[:, 256:512],
                                                     AF.Exp, scale=-0.5))
            w("v", "sc", t_rst2)
            if t_aend_prev:
                w("v", "t", t_aend_prev)
            for d in range(DC):
                op("v", lambda e, d=d: e.tensor_sub(
                    TMP6[:, 0:256], xs2(d), MUT2))
                op("v", lambda e: e.tensor_mul(TMP6[:, 0:256], TMP6[:, 0:256],
                                               RST2))
                op("v", lambda e, d=d: e.tensor_scalar(
                    H2TC[:, 256 * d:256 * (d + 1)], TMP6[:, 0:256],
                    DERV[:, 16 + d:16 + d + 1], DERV[:, 24 + d:24 + d + 1],
                    op0=ALU.mult, op1=ALU.add))
            h2_tok[tc] = Cn["v"]
            if tc == 0 and "h2tc0" in dbg:
                w("s", "v", h2_tok[tc])
                dma("s", dbg["h2tc0"][:, :], POOL[:, 0:2048])

        def u_stage(tc):
            w("t", "v", h2_tok[tc])
            use_bank(("PD",))
            for k in range(DC):
                mm(PD[0:32, 0:256], ASEL[:, 32 * k:32 * (k + 1)],
                   H2TC[:, 256 * k:256 * (k + 1)], start=(k == 0),
                   stop=(k == DC - 1), sem=(k == DC - 1))
            w("v", "t", Cn["t"])
            op("v", lambda e: e.tensor_copy(UB[0:32, 0:256], PD[0:32, 0:256]))
            op("v", lambda e: e.tensor_copy(UB[0:32, 256:512], PD[0:32, 0:256]))
            t_u01 = op("v", lambda e: e.tensor_scalar(
                UB[0:16, 256:512], PD[0:16, 0:256], -1.0, None, op0=ALU.mult))
            free_bank(("PD",), "v", t_u01)
            u_tok[tc] = t_u01
            if tc == 0 and "u01" in dbg:
                w("s", "v", t_u01)
                dma("s", dbg["u01"][:, :], UB[:, :])

        AROT = [(PA, 0, ("PA", 0)), (PA, 512, ("PA", 1)), (PD, 0, ("PD",)),
                (PB, 0, ("PB", 0)), (PB, 512, ("PB", 1)), (PC, 0, ("PC", 0))]

        def a_stage(tc, t_p3t):
            w("t", "v", u_tok[tc])
            for kc in range(8):
                wdma("t", "g", wi_tok[kc])

            def a_head(fc):
                bk, off, key = AROT[fc % 6]
                bank = bk[:, off:off + 256]
                r0s = R0P[:, 256 * (fc % 3):256 * (fc % 3) + 256]
                use_bank(key)
                mm(bank, ARENA[0:16, 128 * fc:128 * fc + 128], UB[0:16, 0:256],
                   start=True, stop=False)
                for k in range(DC):
                    mm(bank, wiTblk(k, fc), H2TC[:, 256 * k:256 * (k + 1)],
                       start=False, stop=False, sem=(k == DC - 1))
                t_head = Cn["t"]
                w("sc", "t", t_head)
                w("sc", "v", Cn["v"])
                op("sc", lambda e, bank=bank, r0s=r0s: e.activation(
                    r0s, bank, AF.Relu, scale=TKB[:, 0:1]))
                return Cn["sc"]

            def a_tail(fc, t_ev0):
                bk, off, key = AROT[fc % 6]
                bank = bk[:, off:off + 256]
                r0s = R0P[:, 256 * (fc % 3):256 * (fc % 3) + 256]
                r1s = R1P[:, 256 * (fc % 3):256 * (fc % 3) + 256]
                w("t", "sc", t_ev0)
                mm(bank, ARENA[0:32, 128 * fc:128 * fc + 128], UB[0:32, 256:512],
                   start=False, stop=True, sem=True)
                w("sc", "t", Cn["t"])
                t_ev1 = op("sc", lambda e, bank=bank, r1s=r1s: e.activation(
                    r1s, bank, AF.Relu, scale=TKB[:, 1:2]))
                free_bank(key, "sc", t_ev1)
                w("v", "sc", t_ev1)
                if tc == 0:
                    w("v", "t", t_p3t)
                op("v", lambda e, fc=fc, r0s=r0s, r1s=r1s: e.tensor_add(
                    W0[:, 256 * fc:256 * (fc + 1)], r0s, r1s))

            pend = []
            for fc in range(FC):
                ev0 = a_head(fc)
                pend.append((fc, ev0))
                if len(pend) > 2:
                    a_tail(*pend.pop(0))
            aend_tok[tc + 1] = Cn["t"]
            for it in pend:
                a_tail(*it)
            interC_tok[tc] = Cn["v"]
            if tc == 0 and "interC0" in dbg:
                w("s", "v", interC_tok[tc])
                dma("s", dbg["interC0"][:, :], W0[:, :])
            if tc == 0:
                cut("aphase")

        def mm2_stage(tc):
            w("t", "v", interC_tok[tc])
            for fc in range(FC):
                wdma("t", "g", wo_tok[fc])
            dp_bank = [(PB, 0), (PB, 256), (PB, 512), (PB, 768),
                       (PC, 0), (PC, 256), (PC, 512), (PC, 768)]
            for dp in range(DC):
                bk, off = dp_bank[dp]
                key = ("PB", off // 512) if bk is PB else ("PC", off // 512)
                if dp % 2 == 0:
                    use_bank(key)
                for fc in range(FC):
                    mm(bk[:, off:off + 256], woTblk(fc, dp),
                       W0[:, 256 * fc:256 * (fc + 1)], start=(fc == 0),
                       stop=(fc == FC - 1), sem=(fc == FC - 1))
                if dp % 2 == 1:
                    t_dp = Cn["t"]
                    w("v", "t", t_dp)
                    for d2 in (dp - 1, dp):
                        bk2, off2 = dp_bank[d2]
                        op("v", lambda e, bk2=bk2, off2=off2, d2=d2:
                           e.tensor_scalar(
                               TMP6[:, 256:512], bk2[:, off2:off2 + 256],
                               DERV[:, 40 + d2:40 + d2 + 1], None, op0=ALU.mult))
                        tv = op("v", lambda e, d2=d2, tc=tc: e.tensor_add(
                            SX[:, 2048 * d2 + 256 * tc:
                               2048 * d2 + 256 * tc + 256],
                            SX[:, 2048 * d2 + 256 * tc:
                               2048 * d2 + 256 * tc + 256],
                            TMP6[:, 256:512]))
                        free_bank(key, "v", tv)
            t_x2 = Cn["v"]
            if tc == 0:
                cut("tc0")
            w("g", "v", t_x2)
            dma("g",
                out_xT.rearrange("(d p) t -> p d t", p=128)[:, :,
                                                            256 * tc:256 * (tc + 1)],
                SX.ap().rearrange("p (d t) -> p d t", d=DC)[:, :,
                                                            256 * tc:256 * (tc + 1)])

        ln2_stage(0, None)
        if True:
            cut("ln2s")
        u_stage(0)
        cut("u")
        for tc in range(TCN):
            a_stage(tc, t_p3t)
            if tc + 1 < TCN:
                ln2_stage(tc + 1, aend_tok[tc + 1])
            mm2_stage(tc)
            if tc + 1 < TCN:
                u_stage(tc + 1)

        w("g", "dg", D16["g"])
        w("s", "ds", D16["s"])

    with nc.allow_low_precision("bf16 recipe calibrated by problem tolerance"), \
            nc.Block() as blk:
        def run(name, e):
            try:
                emit(name, e)
            except _Cut:
                pass

        @blk.tensor
        def _(e):
            run("t", e)

        @blk.vector
        def _(e):
            run("v", e)

        @blk.scalar
        def _(e):
            run("sc", e)

        @blk.gpsimd
        def _(e):
            run("g", e)

        @blk.sync
        def _(e):
            run("s", e)

    return nc, dbg


_NC_CACHE = {}


def _get_nc(debug=(), stop_after=None):
    key = (tuple(sorted(debug)), stop_after)
    if key not in _NC_CACHE:
        _NC_CACHE[key] = build_nc(debug=key[0], stop_after=stop_after)
    return _NC_CACHE[key]


def _host_inputs(inputs, core):
    b = core // 2
    x = np.asarray(inputs["L0_attn_x"])
    xs = x[b, 2048 * (core % 2):2048 * (core % 2 + 1), :]
    m = {}
    m["xT"] = np.ascontiguousarray(xs.T).astype(np.float32)
    for w in ("Wq", "Wk", "Wv", "Wq2", "Wk2", "Wv2", "Wo"):
        m[w] = np.ascontiguousarray(np.asarray(inputs["L0_attn_" + w],
                                               dtype=np.float32))
    vecs = [inputs["L0_attn_bq"], inputs["L0_attn_bk"], inputs["L0_attn_bv"],
            inputs["L0_attn_bv2"], inputs["L0_attn_bo"],
            inputs["L0_attn_gamma"], inputs["L0_attn_beta"]]
    m["vecs"] = np.ascontiguousarray(np.concatenate(
        [np.asarray(v, dtype=np.float32).reshape(8, 128).T for v in vecs],
        axis=1))
    m["brow"] = np.concatenate(
        [np.asarray(inputs["L0_attn_bq2"], dtype=np.float32).reshape(1, -1),
         np.asarray(inputs["L0_attn_bk2"], dtype=np.float32).reshape(1, -1)],
        axis=1)
    m["cT"] = np.ascontiguousarray(
        np.asarray(inputs["L0_c"], dtype=np.float32)[b].reshape(8, 128).T)
    m["wmod"] = np.ascontiguousarray(np.asarray(inputs["L0_wmod"],
                                                dtype=np.float32))
    m["wiT"] = np.ascontiguousarray(
        np.asarray(inputs["L0_ffn_wi_w"], dtype=np.float32).T)
    m["woT"] = np.ascontiguousarray(
        np.asarray(inputs["L0_ffn_wo_w"], dtype=np.float32).T)
    m["Aall"] = np.ascontiguousarray(
        np.asarray(inputs["L0_ffn_lora_As"], dtype=np.float32).reshape(128, 1024))
    m["BallT"] = np.ascontiguousarray(
        np.asarray(inputs["L0_ffn_lora_Bs"],
                   dtype=np.float32).transpose(0, 2, 1).reshape(128, 4096))
    m["idx"] = np.asarray(inputs["L0_ffn_top_k_indices"]).reshape(1, 2).astype(
        np.int32)
    m["tkv"] = np.asarray(inputs["L0_ffn_top_k_values"],
                          dtype=np.float32).reshape(1, 2)
    constB = np.zeros((128, 272), dtype=np.float32)
    constB[:, 0:128] = 1.0 / 1024.0
    constB[:, 128:256] = np.eye(128, dtype=np.float32)
    constB[:, 256:272] = np.tile(np.eye(16, dtype=np.float32), (8, 1))
    m["constB"] = constB
    constF = np.zeros((128, 8), dtype=np.float32)
    constF[:, 0] = np.arange(128) // 16
    constF[:, 1] = -0.5
    constF[:, 2] = 1e-5
    constF[:, 3] = 1e-3
    constF[:, 4] = 0.0
    constF[:, 5] = 1.0
    m["constF"] = constF
    sel = np.zeros((16, 1024), dtype=np.float32)
    for h in range(16):
        c, hh = h // 2, h % 2
        sel[h, 128 * c + 64 * hh:128 * c + 64 * hh + 64] = 1.0
    m["sel"] = sel
    m["ones_in"] = np.ones((1, 128), dtype=np.float32)
    return m


def kernel(**inputs):
    nc, dbg = _get_nc()
    in_maps = [_host_inputs(inputs, core) for core in range(8)]
    res = run_bass_kernel_spmd(nc, in_maps, core_ids=list(range(8)))
    outs = [np.asarray(res.results[c]["out_xT"]).T for c in range(8)]
    return np.concatenate(outs, axis=0).reshape(4, 4096, 1024).astype(np.float32)


# version 09
# speedup vs baseline: 1.4672x; 1.0293x over previous
import numpy as np
import concourse.bass as bass
import concourse.mybir as mybir
from concourse.bass_utils import run_bass_kernel_spmd

F32 = mybir.dt.float32
BF16 = mybir.dt.bfloat16
I32 = mybir.dt.int32
AF = mybir.ActivationFunctionType
ALU = mybir.AluOpType
AX = mybir.AxisListType

T = 2048
D = 1024
DC = 8
NW = 4
NT = 16
H = 16
DH = 64
FF = 4096
FC = 32
TCN = 8
EPS_IN = 1e-3
EPS_OUT = 1e-5
EPS_QK = 1e-6


class _Cut(Exception):
    pass


def build_nc(debug=(), stop_after=None):
    nc = bass.Bass("TRN2", target_bir_lowering=False)
    dbgn = set(debug)

    def din(name, shape, dtype=F32):
        return nc.dram_tensor(name, list(shape), dtype, kind="ExternalInput").ap()

    xT = din("xT", [D, T])
    WTS = {w: din(w, [D, D]) for w in ("Wq", "Wk", "Wv", "Wq2", "Wk2", "Wv2", "Wo")}
    vecs_in = din("vecs", [128, 7 * DC])
    brow_in = din("brow", [1, 2 * D])
    cT_in = din("cT", [128, DC])
    wmod_in = din("wmod", [D, 6 * D])
    wiT_in = din("wiT", [D, FF])
    woT_in = din("woT", [FF, D])
    Aall_in = din("Aall", [128, D])
    BallT_in = din("BallT", [128, FF])
    idx_in = din("idx", [1, 2], I32)
    tkv_in = din("tkv", [1, 2])
    constB_in = din("constB", [128, 272])
    constF_in = din("constF", [128, 8])
    sel_in = din("sel", [16, 1024])
    ones_in = din("ones_in", [1, 128])

    out_xT = nc.dram_tensor("out_xT", [D, T], F32, kind="ExternalOutput").ap()
    cc_in = nc.dram_tensor("cc_in", [64, H * DH], BF16, kind="Internal").ap()
    cc_out = nc.dram_tensor("cc_out", [64, H * DH], BF16, kind="Internal").ap()

    dbg = {}
    def dbgt(name, shape, dtype=F32):
        if name in dbgn:
            dbg[name] = nc.dram_tensor("dbg_" + name, list(shape), dtype,
                                       kind="ExternalOutput").ap()
    dbgt("modT", [128, 48]); dbgt("hT", [128, 16384], BF16)
    dbgt("q2", [128, 16384], BF16); dbgt("k2", [128, 16384], BF16)
    dbgt("Stot", [64, 1024], BF16); dbgt("qks", [128, 256])
    dbgt("v2", [128, 16384], BF16); dbgt("yT", [128, 16384], BF16)
    dbgt("x1T", [128, 16384], BF16); dbgt("h2tc0", [128, 2048], BF16)
    dbgt("interC0", [128, 8192], BF16); dbgt("BselT", [128, 4096], BF16)
    dbgt("u01", [64, 512], BF16); dbgt("OH", [128, 32], BF16)
    dbgt("qksrT", [128, 2048], BF16); dbgt("ASEL", [128, 256], BF16)
    dbgt("recip", [128, 256], BF16); dbgt("qks2", [128, 256])
    dbgt("DERV", [128, 64])

    A = nc.alloc_sbuf_tensor
    SX = A("SX", [128, 16384], BF16)
    SH = A("SH", [128, 16384], BF16)
    SB = A("SB", [128, 16384], BF16)
    SC = A("SC", [128, 16384], BF16)
    SAa = A("SAa", [128, 4096], BF16)
    SAb = A("SAb", [128, 4096], BF16)
    W0 = A("W0", [128, 8192], BF16)
    W1 = A("W1", [128, 8192], BF16)
    POOL = A("POOL", [128, 5120], BF16)
    CONST = A("CONSTB", [128, 272], BF16)
    P16T = A("P16T", [128, 8], F32)
    VECS = A("VECS", [128, 7 * DC], F32)
    DERV = A("DERV", [128, 8 * DC], F32)
    MODT = A("MODT", [128, 48], F32)
    CT = A("CT", [128, DC], F32)
    SIG = A("SIG", [128, DC], F32)
    SILU = A("SILU", [128, DC], BF16)
    ARENA = A("ARENA", [128, 4096], BF16)
    SEL = A("SEL", [16, 1024], BF16)
    UB = A("UB", [64, 512], BF16)
    BROW = A("BROW", [1, D], BF16)
    STF = A("STF", [128, 1024], F32)
    ASEL = A("ASEL", [128, 256], BF16)
    TKB = A("TKB", [128, 2], F32)
    IDXS = A("IDXS", [1, 2], I32)
    IDXF = A("IDXF", [1, 2], F32)
    IDXB = A("IDXB", [1, 2], BF16)
    OHE = A("OHE", [128, 2], F32)
    OH = A("OH", [128, 32], BF16)
    TKS = A("TKS", [1, 2], BF16)
    ONES1 = A("ONES1", [1, 128], BF16)

    MUT = POOL[:, 0:512]
    RST = POOL[:, 512:1024]
    TMPE = POOL[:, 1024:2048]
    TMPF = POOL[:, 2048:3072]
    QKSRT = POOL[0:16, 3072:5120]
    SPART = POOL[0:64, 1024:2048]
    ROW0 = POOL[0:1, 3072:3584]
    ROW1 = POOL[0:1, 3584:4096]
    H2TC = POOL[:, 0:2048]
    R0P = POOL[:, 2048:2816]
    R1P = POOL[:, 2816:3584]
    MUT2 = POOL[:, 3584:3840]
    RST2 = POOL[:, 3840:4096]
    TMP6 = POOL[:, 4096:4608]
    XSQ = POOL[:, 4608:5120]
    QKS = STF[:, 768:1024]
    STOT = POOL[0:64, 3072:4096]
    STOT64 = POOL[64:128, 3072:4096]

    Jc = CONST[:, 0:128]
    I128 = CONST[:, 128:256]
    I16t = CONST[:, 256:272]

    PA = nc.alloc_psum_tensor("PA", [128, 1024], F32)
    PB = nc.alloc_psum_tensor("PB", [128, 1024], F32)
    PC = nc.alloc_psum_tensor("PC", [128, 1024], F32)
    PD = nc.alloc_psum_tensor("PD", [128, 512], F32)
    PTB = nc.alloc_psum_tensor("PTB", [128, 1024], BF16)

    sems = {}
    for name in ("t", "v", "sc", "g", "s", "dg", "ds"):
        sems[name] = nc.alloc_semaphore("sem_" + name)

    for col, val in ((1, -0.5), (2, 1e-5), (3, 1e-3), (4, 0.0), (5, 1.0)):
        nc.const_aps.register(F32, float(val), P16T[:, col:col + 1])

    def W0b(k, m):
        return W0[:, 1024 * k + 128 * m: 1024 * k + 128 * m + 128]

    def W1b(k, m):
        return W1[:, 1024 * k + 128 * m: 1024 * k + 128 * m + 128]

    def wiTblk(k, fc):
        if k < 4:
            return SB[:, 4096 * k + 128 * fc: 4096 * k + 128 * fc + 128]
        if k < 6:
            return W1[:, 4096 * (k - 4) + 128 * fc: 4096 * (k - 4) + 128 * fc + 128]
        if k == 6:
            return SAa[:, 128 * fc: 128 * fc + 128]
        return SAb[:, 128 * fc: 128 * fc + 128]

    def woTblk(fc, dp):
        if fc < 16:
            return SH[:, 1024 * fc + 128 * dp: 1024 * fc + 128 * dp + 128]
        return SC[:, 1024 * (fc - 16) + 128 * dp: 1024 * (fc - 16) + 128 * dp + 128]

    def wmod_rhs(k, n):
        if k < 4:
            base = [SB, SB, SC, SC][k]
            off = (k % 2) * 6144
            return base[:, off + 512 * n: off + 512 * n + 512]
        if k == 4:
            if n < 4:
                return SC[:, 14336 + 512 * n: 14336 + 512 * n + 512]
            if n < 8:
                return SAb[:, 2048 + 512 * (n - 4): 2048 + 512 * (n - 4) + 512]
            return POOL[:, 1024 + 512 * (n - 8): 1024 + 512 * (n - 8) + 512]
        if k == 5:
            return W1[:, 512 * n: 512 * n + 512]
        if k == 6:
            if n < 8:
                return SAa[:, 512 * n: 512 * n + 512]
            return SAb[:, 512 * (n - 8): 512 * (n - 8) + 512]
        if n < 8:
            return SB[:, 12288 + 512 * n: 12288 + 512 * n + 512]
        return SC[:, 12288 + 512 * (n - 8): 12288 + 512 * (n - 8) + 512]

    def emit(my, e):
        def cut(name):
            if stop_after == name:
                raise _Cut()
        Cn = {k: 0 for k in ("t", "v", "sc", "g", "s")}
        D16 = {"g": 0, "s": 0}
        lastw = {}
        bankfree = {}

        def op(eng2, builder, sem=True):
            if my == eng2:
                ins = builder(e)
                if sem:
                    ins.then_inc(sems[eng2], 1)
            if sem:
                Cn[eng2] += 1
            return Cn[eng2]

        def w(eng2, sem_name, val):
            if my != eng2 or val is None or val <= 0:
                return
            if lastw.get(sem_name, -1) >= val:
                return
            lastw[sem_name] = val
            e.wait_ge(sems[sem_name], val)

        def dma(q, out, in_, **kw):
            D16[q] += 16
            if my == q:
                e.dma_start(out=out, in_=in_, **kw).then_inc(
                    sems["dg" if q == "g" else "ds"], 16)
            return D16[q]

        def wdma(eng2, q, tok):
            if tok:
                w(eng2, "dg" if q == "g" else "ds", tok)

        def fence(eng2):
            w(eng2, eng2, Cn[eng2])

        def mm(out, lhsT, rhs, start, stop, sem=False):
            return op("t", lambda e: e.matmul(out, lhsT, rhs, start=start,
                                              stop=stop), sem=sem)

        def use_bank(*keys):
            for k in keys:
                if k in bankfree:
                    eng2, tok = bankfree[k]
                    w("t", eng2, tok)

        def free_bank(key, eng2, tok):
            bankfree[key] = (eng2, tok)

        tok_p16 = dma("s", P16T[:, :], constF_in[:, :])
        tok_vecs = dma("s", VECS[:, :], vecs_in[:, :])
        tok_ct = dma("s", CT[:, :], cT_in[:, :])
        tok_idx = dma("s", IDXS[:, :], idx_in[:, :])

        tok_consts = dma("g", CONST[:, :], constB_in[:, :])
        tok_ones1 = dma("g", ONES1[:, :], ones_in[:, :])
        tok_sel = dma("g", SEL[:, :], sel_in[:, :])
        tok_brow = dma("g", BROW[:, :], brow_in[0:1, 0:D])
        tok_x = dma("g", SX.ap().rearrange("p (d t) -> p d t", d=DC),
                    xT.rearrange("(d p) t -> p d t", p=128))
        wmod_tok = []
        for k in range(4):
            dsts = [SB, SB, SC, SC][k]
            off = (k % 2) * 6144
            wmod_tok.append(dma("g", dsts[:, off:off + 6144],
                                wmod_in[128 * k:128 * (k + 1), :]))
        wmod_tok.append(dma("g", SC[:, 14336:16384], wmod_in[512:640, 0:2048]))
        wmod_tok.append(dma("g", SAb[:, 2048:4096], wmod_in[512:640, 2048:4096]))
        wmod_tok.append(dma("g", POOL[:, 1024:3072], wmod_in[512:640, 4096:6144]))
        wmod_tok.append(dma("g", W1[:, 0:6144], wmod_in[640:768, :]))
        wmod_tok.append(dma("g", SAa[:, 0:4096], wmod_in[768:896, 0:4096]))
        wmod_tok.append(dma("g", SAb[:, 0:2048], wmod_in[768:896, 4096:6144]))
        wmod_tok.append(dma("g", SB[:, 12288:16384], wmod_in[896:1024, 0:4096]))
        wmod_tok.append(dma("g", SC[:, 12288:14336], wmod_in[896:1024, 4096:6144]))
        tok_aall = dma("g", W1[:, 6144:7168], Aall_in[:, :])
        tok_ball = dma("g", ARENA[:, :], BallT_in[:, :])
        tok_tkf = dma("g", TKS[:, :], tkv_in[:, :])

        def wload(buf, name, gate=None):
            if gate is not None:
                w("g", gate[0], gate[1])
            return dma("g", buf.ap().rearrange("p (k m j) -> p k m j", k=8, j=128),
                       WTS[name].rearrange("(k p) (m j) -> p k m j", p=128, j=128))

        tok_wq = wload(W0, "Wq")

        wdma("t", "g", tok_consts)
        wdma("t", "g", tok_x)

        def ln_stats(src_chunk_ap, eps, sq_scratch):
            use_bank(("PA", 0))
            for d in range(DC):
                mm(PA[:, 0:512], Jc, src_chunk_ap(d), start=(d == 0),
                   stop=(d == DC - 1), sem=(d == DC - 1))
            t_mu_mm = Cn["t"]
            w("sc", "t", Cn["t"])
            for d in range(DC):
                op("sc", lambda e, d=d: e.activation(
                    sq_scratch(d), src_chunk_ap(d), AF.Square))
            t_sq = Cn["sc"]
            use_bank(("PA", 1))
            w("t", "sc", t_sq)
            for d in range(DC):
                mm(PA[:, 512:1024], Jc, sq_scratch(d), start=(d == 0),
                   stop=(d == DC - 1), sem=(d == DC - 1))
            t_ss_mm = Cn["t"]
            w("v", "t", t_mu_mm)
            op("v", lambda e: e.tensor_copy(MUT, PA[:, 0:512]))
            t_mucp = op("v", lambda e: e.tensor_copy(STF[:, 0:512], PA[:, 0:512]))
            free_bank(("PA", 0), "v", t_mucp)
            op("v", lambda e: e.tensor_mul(STF[:, 512:1024], STF[:, 0:512],
                                           STF[:, 0:512]))
            w("v", "t", t_ss_mm)
            t_var = op("v", lambda e: e.tensor_sub(STF[:, 0:512], PA[:, 512:1024],
                                                   STF[:, 512:1024]))
            free_bank(("PA", 1), "v", t_var)
            w("sc", "v", t_var)
            wdma("sc", "s", tok_p16)
            op("sc", lambda e, eps=eps: e.activation(STF[:, 512:1024],
                                                     STF[:, 0:512], AF.Ln,
                                                     bias=float(eps)))
            t_rst = op("sc", lambda e: e.activation(RST, STF[:, 512:1024],
                                                    AF.Exp, scale=-0.5))
            return t_rst

        for w_ in range(NW):
            xs = lambda d, w_=w_: SX[:, 2048 * d + 512 * w_:
                                     2048 * d + 512 * w_ + 512]
            sqs = lambda d, w_=w_: SH[:, 2048 * d + 512 * w_:
                                      2048 * d + 512 * w_ + 512]
            t_rst = ln_stats(xs, EPS_OUT, sqs)
            w("v", "sc", t_rst)
            for d in range(DC):
                hs = SH[:, 2048 * d + 512 * w_: 2048 * d + 512 * w_ + 512]
                op("v", lambda e, d=d, w_=w_, hs=hs: e.tensor_sub(
                    hs, SX[:, 2048 * d + 512 * w_: 2048 * d + 512 * w_ + 512],
                    MUT))
                op("v", lambda e, hs=hs: e.tensor_mul(hs, hs, RST))
        t_hn = Cn["v"]

        wdma("sc", "s", tok_ct)
        t_sig = op("sc", lambda e: e.activation(SIG[:, :], CT[:, :], AF.Sigmoid))
        w("v", "sc", t_sig)
        t_silu = op("v", lambda e: e.tensor_mul(SILU[:, :], CT[:, :], SIG[:, :]))

        rowfree = [0, 0]
        wdma("t", "g", tok_consts)
        for n in range(12):
            key = ("PA", n % 2)
            bank = PA[0:1, 0:512] if n % 2 == 0 else PA[0:1, 512:1024]
            rowslot = ROW0 if n % 2 == 0 else ROW1
            wdma("t", "g", wmod_tok[11])
            w("t", "v", t_silu)
            use_bank(key)
            for k in range(8):
                mm(bank, SILU[:, k:k + 1], wmod_rhs(k, n),
                   start=(k == 0), stop=(k == 7), sem=(k == 7))
            w("v", "t", Cn["t"])
            w("v", "t", rowfree[n % 2])
            tv = op("v", lambda e, bank=bank, rowslot=rowslot: e.tensor_copy(
                rowslot, bank))
            free_bank(key, "v", tv)
            w("t", "v", tv)
            use_bank(("PTB7",))
            for c in range(4):
                op("t", lambda e, c=c, n=n, rowslot=rowslot: e.transpose(
                    PTB[:, 8 * (n % 2) + 2 * c: 8 * (n % 2) + 2 * c + 1],
                    rowslot[0:1, 128 * c:128 * c + 128], I128[0:1, 0:1]),
                    sem=(c == 3))
            rowfree[n % 2] = Cn["t"]
            w("v", "t", Cn["t"])
            vv, cc4 = n // 2, 4 * (n % 2)
            ptb_view = PTB[:, 8 * (n % 2): 8 * (n % 2) + 8].rearrange(
                "p (c two) -> p c two", two=2)[:, :, 0:1]
            tvv = op("v", lambda e, vv=vv, cc4=cc4, ptb_view=ptb_view:
                     e.tensor_copy(
                         MODT[:, 8 * vv + cc4: 8 * vv + cc4 + 4], ptb_view))
            free_bank(("PTB7",), "v", tvv)
        t_modr = Cn["v"]

        wdma("v", "s", tok_vecs)
        fence("v")
        op("v", lambda e: e.tensor_scalar(DERV[:, 0:8], MODT[:, 8:16], 1.0, None,
                                          op0=ALU.add))
        op("v", lambda e: e.tensor_copy(DERV[:, 8:16], MODT[:, 0:8]))
        op("v", lambda e: e.tensor_scalar(DERV[:, 16:24], MODT[:, 32:40], 1.0, None,
                                          op0=ALU.add))
        op("v", lambda e: e.tensor_copy(DERV[:, 24:32], MODT[:, 24:32]))
        for (src0, dst) in ((16, 32), (40, 40)):
            op("v", lambda e, s=src0: e.tensor_scalar(
                STF[:, 8:16], MODT[:, s:s + 8], 0.0, None, op0=ALU.is_ge))
            fence("v")
            op("v", lambda e: e.tensor_scalar(
                STF[:, 8:16], STF[:, 8:16], 2.0, -1.0, op0=ALU.mult, op1=ALU.add))
            fence("v")
            op("v", lambda e, s=src0: e.tensor_mul(
                STF[:, 0:8], MODT[:, s:s + 8], STF[:, 8:16]))
            fence("v")
            op("v", lambda e: e.tensor_scalar(
                STF[:, 0:8], STF[:, 0:8], 0.25, None, op0=ALU.max))
            fence("v")
            op("v", lambda e, d=dst: e.tensor_mul(
                DERV[:, d:d + 8], STF[:, 0:8], STF[:, 8:16]))
            fence("v")
        op("v", lambda e: e.tensor_mul(DERV[:, 48:56], DERV[:, 32:40],
                                       VECS[:, 40:48]))
        t_derv = op("v", lambda e: e.tensor_mul(DERV[:, 56:64], DERV[:, 32:40],
                                                VECS[:, 48:56]))
        if "modT" in dbg:
            w("s", "v", t_derv)
            dma("s", dbg["modT"][:, :], MODT[:, :])
        if "DERV" in dbg:
            w("s", "v", t_derv)
            dma("s", dbg["DERV"][:, :], DERV[:, :])
        for d in range(DC):
            op("v", lambda e, d=d: e.tensor_scalar(
                SH[:, 2048 * d:2048 * (d + 1)], SH[:, 2048 * d:2048 * (d + 1)],
                DERV[:, d:d + 1], DERV[:, 8 + d:8 + d + 1],
                op0=ALU.mult, op1=ALU.add))
        t_h = Cn["v"]
        tok_dbg_h = None
        if "hT" in dbg:
            w("s", "v", t_h)
            tok_dbg_h = dma("s", dbg["hT"][:, :], SH[:, :])
        cut("p1")

        cut("p0a")

        wdma("v", "s", tok_idx)
        op("v", lambda e: e.tensor_copy(IDXF[:, :], IDXS[:, :]))
        fence("v")
        op("v", lambda e: e.tensor_scalar(IDXF[:, :], IDXF[:, :], 0.0, 7.0,
                                          op0=ALU.max, op1=ALU.min))
        fence("v")
        t_idxb = op("v", lambda e: e.tensor_copy(IDXB[:, :], IDXF[:, :]))
        wdma("t", "g", tok_ones1)
        w("t", "v", t_idxb)
        use_bank(("PD",))
        mm(PD[0:128, 0:2], ONES1[0:1, 0:128], IDXB[0:1, 0:2], True, True, sem=True)
        w("v", "t", Cn["t"])
        wdma("v", "s", tok_p16)
        t_ohe = op("v", lambda e: e.tensor_scalar(OHE[:, :], PD[0:128, 0:2],
                                                  P16T[:, 0:1], None,
                                                  op0=ALU.is_equal))
        t_ohz = op("g", lambda e: e.memset(OH[:, :], 0))
        wdma("v", "g", tok_consts)
        w("v", "g", t_ohz)
        fence("v")
        for k in range(2):
            op("v", lambda e, k=k: e.tensor_scalar(
                OH[:, 16 * k:16 * k + 16], I16t, OHE[:, k:k + 1], None,
                op0=ALU.mult))
        t_oh = Cn["v"]
        free_bank(("PD",), "v", t_ohe)
        wdma("t", "g", tok_tkf)
        use_bank(("PD",))
        mm(PD[0:128, 2:4], ONES1[0:1, 0:128], TKS[0:1, 0:2], True, True, sem=True)
        w("v", "t", Cn["t"])
        t_tkb = op("v", lambda e: e.tensor_copy(TKB[:, :], PD[0:128, 2:4]))
        free_bank(("PD",), "v", t_tkb)

        wdma("t", "g", tok_aall)
        w("t", "v", t_oh)
        for d in range(DC):
            key = ("PC", d % 2)
            bank = PC[:, 0:32] if d % 2 == 0 else PC[:, 512:544]
            use_bank(key)
            mm(bank, W1[:, 6144 + 128 * d:6144 + 128 * (d + 1)], OH[:, :],
               True, True, sem=True)
            w("v", "t", Cn["t"])
            tv = op("v", lambda e, d=d, bank=bank: e.tensor_copy(
                ASEL[:, 32 * d:32 * (d + 1)], bank))
            free_bank(key, "v", tv)
        wdma("t", "g", tok_ball)
        for n in range(8):
            use_bank(("PD",))
            mm(PD[0:32, 0:512], OH[:, :], ARENA[:, 512 * n:512 * (n + 1)],
               True, True, sem=True)
            w("v", "t", Cn["t"])
            tv = op("v", lambda e, n=n: e.tensor_copy(
                ARENA[0:32, 512 * n:512 * (n + 1)], PD[0:32, 0:512]))
            free_bank(("PD",), "v", tv)
        t_bsel = Cn["v"]
        t_p0b = Cn["t"]
        if "OH" in dbg:
            w("s", "v", t_bsel)
            dma("s", dbg["OH"][:, :], OH[:, :])
        if "BselT" in dbg:
            w("s", "v", t_bsel)
            dma("s", dbg["BselT"][:, :], ARENA[:, :])
        if "ASEL" in dbg:
            w("s", "v", t_bsel)
            dma("s", dbg["ASEL"][:, :], ASEL[:, :])
        cut("p0b")

        tok_wq2 = wload(W1, "Wq2", gate=("t", t_p0b))

        FMROT = [(PB, 0), (PB, 512), (PC, 0), (PC, 512)]

        def fm_proj(wblk, hoff_w, dstc, bias_col):
            for m in range(DC):
                bk, off = FMROT[m % 4]
                key = ("PB" if bk is PB else "PC", off // 512)
                bank = bk[:, off:off + 512]
                use_bank(key)
                for k in range(DC):
                    mm(bank, wblk(k, m),
                       SH[:, 2048 * k + hoff_w: 2048 * k + hoff_w + 512],
                       start=(k == 0), stop=(k == DC - 1), sem=(k == DC - 1))
                w("sc", "t", Cn["t"])
                tv = op("sc", lambda e, m=m, bank=bank: e.activation(
                    dstc(m), bank, AF.Identity,
                    bias=VECS[:, bias_col * 8 + m: bias_col * 8 + m + 1]))
                free_bank(key, "sc", tv)

        TPROT = [(PA, "PA"), (PB, "PB"), (PC, "PC")]

        def tok_proj(sax, dstsb, wbuf, base_tile, toks=None):
            for tt in range(4):
                i = base_tile + tt
                PT, pnm = TPROT[i % 3]
                use_bank((pnm, 0), (pnm, 1))
                for half in range(2):
                    bank = PT[:, 0:512] if half == 0 else PT[:, 512:1024]
                    for k in range(DC):
                        mm(bank,
                           sax[:, 512 * k + 128 * tt: 512 * k + 128 * tt + 128],
                           wbuf[:, 1024 * k + 512 * half:
                                1024 * k + 512 * half + 512],
                           start=(k == 0), stop=False)
                    mm(bank, ONES1[0:1, 0:128],
                       BROW[0:1, 512 * half: 512 * half + 512],
                       start=False, stop=True, sem=(half == 1))
                t_mm = Cn["t"]
                w("sc", "t", t_mm)
                op("sc", lambda e, i=i, PT=PT: e.activation(
                    dstsb[:, 1024 * i:1024 * (i + 1)], PT[:, :], AF.Relu))
                t_relu = Cn["sc"]
                w("v", "sc", t_relu)
                t_min = op("v", lambda e, PT=PT: e.tensor_scalar(
                    TMPE, PT[:, 0:1024], 0.0, None, op0=ALU.min))
                free_bank((pnm, 0), "v", t_min)
                free_bank((pnm, 1), "v", t_min)
                w("sc", "v", t_min)
                t_exp = op("sc", lambda e: e.activation(TMPE, TMPE, AF.Exp))
                w("v", "sc", t_exp)
                tadd = op("v", lambda e, i=i: e.tensor_add(
                    dstsb[:, 1024 * i:1024 * (i + 1)],
                    dstsb[:, 1024 * i:1024 * (i + 1)], TMPE))
                if toks is not None:
                    toks.append(tadd)

        wdma("t", "g", tok_wq)
        wdma("t", "g", tok_brow)
        w("t", "v", t_h)
        t_wq_done = None
        for w_ in range(NW):
            sax = SAa if w_ % 2 == 0 else SAb
            dstc = lambda m, sax=sax: sax[:, 512 * m: 512 * m + 512]
            fm_proj(W0b, 512 * w_, dstc, 0)
            if w_ == NW - 1:
                t_wq_done = Cn["t"]
            wdma("t", "g", tok_wq2)
            w("t", "sc", Cn["sc"])
            tok_proj(sax, SB, W1, 4 * w_)
        t_qchain = Cn["t"]
        t_q2v = Cn["v"]
        cut("qchain")
        tok_wk = wload(W0, "Wk", gate=("t", t_wq_done))
        tok_wk2 = wload(W1, "Wk2", gate=("t", t_qchain))
        w("g", "v", t_q2v)
        tok_brow2 = dma("g", BROW[:, :], brow_in[0:1, D:2 * D])

        elu_k = []
        t_wk_done = None
        for w_ in range(NW):
            sax = SAa if w_ % 2 == 0 else SAb
            wdma("t", "g", tok_wk)
            dstc = lambda m, sax=sax: sax[:, 512 * m: 512 * m + 512]
            fm_proj(W0b, 512 * w_, dstc, 1)
            if w_ == NW - 1:
                t_wk_done = Cn["t"]
            wdma("t", "g", tok_wk2)
            wdma("t", "g", tok_brow2)
            w("t", "sc", Cn["sc"])
            tok_proj(sax, SC, W1, 4 * w_, toks=elu_k)
            for tt in range(4):
                i = 4 * w_ + tt
                op("v", lambda e, i=i: e.tensor_mul(
                    TMPF, SB[:, 1024 * i:1024 * (i + 1)],
                    SC[:, 1024 * i:1024 * (i + 1)]))
                op("v", lambda e, i=i: e.tensor_reduce(
                    QKS[:, 16 * i:16 * (i + 1)],
                    TMPF.rearrange("p (h d) -> p h d", d=DH),
                    axis=AX.X, op=ALU.add))
        t_k2done = Cn["v"]
        t_kchain = Cn["t"]
        tok_wv = wload(W0, "Wv", gate=("t", t_wk_done))
        tok_wv2 = wload(W1, "Wv2", gate=("t", t_kchain))
        tok_dbg_q2 = tok_dbg_k2 = tok_dbg_qks = None
        if "q2" in dbg:
            w("s", "v", t_k2done)
            tok_dbg_q2 = dma("s", dbg["q2"][:, :], SB[:, :])
        if "k2" in dbg:
            w("s", "v", t_k2done)
            tok_dbg_k2 = dma("s", dbg["k2"][:, :], SC[:, :])
        if "qks" in dbg:
            w("s", "v", t_k2done)
            tok_dbg_qks = dma("s", dbg["qks"][:, :], QKS[:, :])
        cut("kchain")

        for i in range(NT):
            op("v", lambda e, i=i: e.tensor_scalar(
                STF[:, 16 * i:16 * (i + 1)], QKS[:, 16 * i:16 * (i + 1)],
                float(EPS_QK), None, op0=ALU.add))
        t_qe = Cn["v"]
        w("sc", "v", t_qe)
        op("sc", lambda e: e.activation(STF[:, 512:768], STF[:, 0:256], AF.Ln))
        fence("sc")
        t_recip = op("sc", lambda e: e.activation(
            TMPF[:, 0:256], STF[:, 512:768], AF.Exp, scale=-1.0))
        if "recip" in dbg:
            w("s", "sc", t_recip)
            dma("s", dbg["recip"][:, :], TMPF[:, 0:256])
        if "qks2" in dbg:
            w("s", "v", t_qe)
            dma("s", dbg["qks2"][:, :], QKS[:, :])

        use_bank(("PA", 0), ("PA", 1))
        for h in range(H):
            for i in range(NT):
                w("t", "v", elu_k[i])
                mm(PA[0:64, 64 * h:64 * (h + 1)],
                   SC[:, 1024 * i + 64 * h: 1024 * i + 64 * h + 64],
                   SB[:, 1024 * i + 64 * h: 1024 * i + 64 * h + 64],
                   start=(i == 0), stop=(i == NT - 1),
                   sem=(i == NT - 1 and h in (7, 15)))
        t_sdone = Cn["t"]
        w("v", "t", t_sdone)
        op("v", lambda e: e.tensor_copy(SPART[:, 0:512], PA[0:64, 0:512]))
        t_sev = op("v", lambda e: e.tensor_copy(SPART[:, 512:1024],
                                                PA[0:64, 512:1024]))
        free_bank(("PA", 0), "v", t_sev)
        free_bank(("PA", 1), "v", t_sev)
        cut("sphase")
        w("g", "v", t_sev)
        tok_ccin = dma("g", cc_in[:, :], SPART[:, :])
        wdma("g", "g", tok_ccin)
        t_cc = op("g", lambda e: e.collective_compute(
            "AllReduce", ALU.add,
            replica_groups=[[0, 1], [2, 3], [4, 5], [6, 7]],
            ins=[cc_in[:, :]], outs=[cc_out[:, :]]))

        cut("cc")
        w("t", "sc", t_recip)
        for i in range(NT):
            po = 128 * (i % 8)
            use_bank(("PTB7",))
            op("t", lambda e, i=i, po=po: e.transpose(
                PTB[0:16, po:po + 128], TMPF[:, 16 * i:16 * (i + 1)], I128),
                sem=True)
            w("v", "t", Cn["t"])
            tv = op("v", lambda e, i=i, po=po: e.tensor_copy(
                QKSRT[:, 128 * i:128 * (i + 1)], PTB[0:16, po:po + 128]))
            free_bank(("PTB7",), "v", tv)
        t_qksr = Cn["v"]
        if "qksrT" in dbg:
            w("s", "v", t_qksr)
            dma("s", dbg["qksrT"][:, :], POOL[:, 3072:5120])

        wdma("t", "g", tok_sel)
        t_wv_done = None
        for w_ in range(NW):
            sax = SAa if w_ % 2 == 0 else SAb
            wdma("t", "g", tok_wv)
            dstc = lambda m, sax=sax: sax[:, 512 * m: 512 * m + 512]
            fm_proj(W0b, 512 * w_, dstc, 2)
            if w_ == NW - 1:
                t_wv_done = Cn["t"]
            wdma("t", "g", tok_wv2)
            w("t", "sc", Cn["sc"])
            for m in range(DC):
                bk, off = FMROT[m % 4]
                key = ("PB" if bk is PB else "PC", off // 512)
                bank = bk[:, off:off + 512]
                use_bank(key)
                for k in range(DC):
                    mm(bank, W1b(k, m), sax[:, 512 * k: 512 * k + 512],
                       start=(k == 0), stop=(k == DC - 1), sem=(k == DC - 1))
                w("sc", "t", Cn["t"])
                w("sc", "t", t_sdone)
                if tok_dbg_q2:
                    wdma("sc", "s", tok_dbg_q2)
                if tok_dbg_qks:
                    wdma("sc", "s", tok_dbg_qks)
                w("sc", "v", t_k2done)
                tv = op("sc", lambda e, m=m, w_=w_, bank=bank: e.activation(
                    SB[:, 2048 * m + 512 * w_: 2048 * m + 512 * w_ + 512], bank,
                    AF.Identity, bias=VECS[:, 3 * 8 + m: 3 * 8 + m + 1]))
                free_bank(key, "sc", tv)
            t_v2w = Cn["sc"]
            w("t", "v", t_qksr)
            for c in range(DC):
                j = c
                bk, off = FMROT[j % 4]
                key = ("PB" if bk is PB else "PC", off // 512)
                bank = bk[:, off:off + 512]
                use_bank(key)
                mm(bank, SEL[0:16, 128 * c:128 * (c + 1)],
                   QKSRT[:, 512 * w_:512 * (w_ + 1)], True, True, sem=True)
                w("v", "t", Cn["t"])
                w("v", "sc", t_v2w)
                tv = op("v", lambda e, c=c, w_=w_, bank=bank: e.tensor_mul(
                    SB[:, 2048 * c + 512 * w_: 2048 * c + 512 * w_ + 512],
                    SB[:, 2048 * c + 512 * w_: 2048 * c + 512 * w_ + 512], bank))
                free_bank(key, "v", tv)
        t_v2 = Cn["sc"]
        t_vchain = Cn["t"]
        tok_wo = wload(W0, "Wo", gate=("t", t_wv_done))
        t_v2s = Cn["v"]
        t_bcast_t = Cn["t"]
        tok_dbg_v2 = None
        if "v2" in dbg:
            w("s", "v", t_v2s)
            tok_dbg_v2 = dma("s", dbg["v2"][:, :], SB[:, :])
        cut("v2s")

        w("g", "g", t_cc)
        w("g", "t", t_bcast_t)
        dma("g", STOT[:, :], cc_out[:, :])
        tok_stot = dma("g", STOT64[:, :], cc_out[:, :])
        if "Stot" in dbg:
            wdma("s", "g", tok_stot)
            dma("s", dbg["Stot"][:, :], STOT[:, :])

        wdma("t", "g", tok_stot)
        w("t", "v", t_v2s)
        aout_sc = {}
        aout_v = {}
        wo_end = {}

        def out_units(nn):
            sax = SAa if nn % 2 == 0 else SAb
            for c in range(DC):
                for hh in range(2):
                    h = 2 * c + hh
                    j = c * 2 + hh
                    bk, off = FMROT[j % 4]
                    key = ("PB" if bk is PB else "PC", off // 512)
                    bank = bk[0:64, off:off + 512]
                    use_bank(key)
                    ST = STOT if hh == 0 else STOT64
                    mm(bank, ST[:, 64 * h:64 * (h + 1)],
                       SB[64 * hh:64 * (hh + 1),
                          2048 * c + 512 * nn: 2048 * c + 512 * nn + 512],
                       True, True, sem=True)
                    eng2 = "sc" if j % 2 == 0 else "v"
                    w(eng2, "t", Cn["t"])
                    if nn >= 2:
                        w(eng2, "t", wo_end[nn - 2])
                    if eng2 == "sc":
                        tv = op("sc", lambda e, c=c, hh=hh, bank=bank, sax=sax:
                                e.copy(
                                    sax[64 * hh:64 * (hh + 1),
                                        512 * c:512 * (c + 1)], bank))
                    else:
                        tv = op("v", lambda e, c=c, hh=hh, bank=bank, sax=sax:
                                e.tensor_copy(
                                    sax[64 * hh:64 * (hh + 1),
                                        512 * c:512 * (c + 1)], bank))
                    free_bank(key, eng2, tv)
            aout_sc[nn] = Cn["sc"]
            aout_v[nn] = Cn["v"]

        def wo_units(nn):
            sax = SAa if nn % 2 == 0 else SAb
            wdma("t", "g", tok_wo)
            w("t", "sc", aout_sc[nn])
            w("t", "v", aout_v[nn])
            for m in range(DC):
                bk, off = FMROT[m % 4]
                key = ("PB" if bk is PB else "PC", off // 512)
                bank = bk[:, off:off + 512]
                tslot = POOL[:, 1024 + 512 * (m % 2):1024 + 512 * (m % 2) + 512]
                use_bank(key)
                for k in range(DC):
                    mm(bank, W0b(k, m), sax[:, 512 * k: 512 * k + 512],
                       start=(k == 0), stop=(k == DC - 1), sem=(k == DC - 1))
                w("sc", "t", Cn["t"])
                wdma("sc", "g", tok_ccin)
                w("sc", "v", Cn["v"])
                op("sc", lambda e, bank=bank, tslot=tslot, m=m: e.activation(
                    tslot, bank, AF.Identity,
                    bias=VECS[:, 4 * 8 + m: 4 * 8 + m + 1]))
                t_bo = Cn["sc"]
                free_bank(key, "sc", t_bo)
                w("v", "sc", t_bo)
                w("v", "t", t_sdone)
                if tok_dbg_k2:
                    wdma("v", "s", tok_dbg_k2)
                op("v", lambda e, m=m, nn=nn, tslot=tslot: e.tensor_add(
                    SC[:, 2048 * m + 512 * nn: 2048 * m + 512 * nn + 512],
                    tslot,
                    SH[:, 2048 * m + 512 * nn: 2048 * m + 512 * nn + 512]))
            wo_end[nn] = Cn["t"]

        out_units(0)
        out_units(1)
        wo_units(0)
        out_units(2)
        wo_units(1)
        out_units(3)
        wo_units(2)
        wo_units(3)
        t_y = Cn["v"]
        t_p3t = Cn["t"]
        tok_dbg_y = None
        if "yT" in dbg:
            w("s", "v", t_y)
            tok_dbg_y = dma("s", dbg["yT"][:, :], SC[:, :])
        cut("p3")

        for w_ in range(NW):
            ys = lambda d, w_=w_: SC[:, 2048 * d + 512 * w_:
                                     2048 * d + 512 * w_ + 512]
            sqs = lambda d: SAa[:, 512 * d: 512 * d + 512]
            if w_ == 0:
                w("sc", "v", t_y)
                w("t", "v", t_y)
            t_rst = ln_stats(ys, EPS_IN, sqs)
            w("v", "sc", t_rst)
            for d in range(DC):
                op("v", lambda e, d=d, w_=w_: e.tensor_sub(
                    TMPE[:, 0:512],
                    SC[:, 2048 * d + 512 * w_: 2048 * d + 512 * w_ + 512],
                    MUT))
                op("v", lambda e: e.tensor_mul(TMPE[:, 0:512], TMPE[:, 0:512],
                                               RST))
                op("v", lambda e, d=d: e.tensor_scalar(
                    TMPE[:, 0:512], TMPE[:, 0:512], DERV[:, 48 + d:48 + d + 1],
                    DERV[:, 56 + d:56 + d + 1], op0=ALU.mult, op1=ALU.add))
                op("v", lambda e, d=d, w_=w_: e.tensor_add(
                    SX[:, 2048 * d + 512 * w_: 2048 * d + 512 * w_ + 512],
                    SX[:, 2048 * d + 512 * w_: 2048 * d + 512 * w_ + 512],
                    TMPE[:, 0:512]))
        t_x1 = Cn["v"]
        t_p4t = Cn["t"]
        tok_dbg_x1 = None
        if "x1T" in dbg:
            w("s", "v", t_x1)
            tok_dbg_x1 = dma("s", dbg["x1T"][:, :], SX[:, :])
        cut("p4")

        wi_tok = []
        for kc in range(8):
            if kc < 4:
                dstap = SB[:, 4096 * kc:4096 * (kc + 1)]
                gate = ("t", t_p3t)
                wdma("g", "s", tok_dbg_v2)
            elif kc < 6:
                dstap = W1[:, 4096 * (kc - 4):4096 * (kc - 4) + 4096]
                gate = ("t", t_vchain)
            elif kc == 6:
                dstap = SAa[:, :]
                gate = ("t", t_p4t)
            else:
                dstap = SAb[:, :]
                gate = ("t", t_p4t)
            w("g", gate[0], gate[1])
            wi_tok.append(dma("g", dstap, wiT_in[128 * kc:128 * (kc + 1), :]))
        wo_tok = []
        for fc in range(FC):
            if fc < 16:
                dstap = SH[:, 1024 * fc:1024 * (fc + 1)]
                w("g", "v", t_y)
                wdma("g", "s", tok_dbg_h)
            else:
                dstap = SC[:, 1024 * (fc - 16):1024 * (fc - 16) + 1024]
                w("g", "t", t_p4t)
                w("g", "v", t_x1)
                wdma("g", "s", tok_dbg_y)
            wo_tok.append(dma("g", dstap, woT_in[128 * fc:128 * (fc + 1), :]))

        h2_tok = {}
        u_tok = {}
        aend_tok = {0: None}
        interC_tok = {}

        def ln2_stage(tc, t_aend_prev):
            xs2 = lambda d: SX[:, 2048 * d + 256 * tc: 2048 * d + 256 * tc + 256]
            use_bank(("PD",))
            w("t", "v", t_x1)
            wdma("v", "s", tok_dbg_x1)
            for d in range(DC):
                mm(PD[:, 0:256], Jc, xs2(d), start=(d == 0), stop=(d == DC - 1),
                   sem=(d == DC - 1))
            t_mu_mm = Cn["t"]
            use_bank(("PB", 0))
            w("sc", "t", Cn["t"])
            for d in range(DC):
                slot = XSQ[:, 256 * (d % 2):256 * (d % 2) + 256]
                if d >= 2:
                    w("sc", "t", Cn["t"])
                op("sc", lambda e, d=d, slot=slot: e.activation(
                    slot, xs2(d), AF.Square))
                w("t", "sc", Cn["sc"])
                mm(PB[:, 0:256], Jc, slot, start=(d == 0), stop=(d == DC - 1),
                   sem=True)
            t_ss_mm = Cn["t"]
            w("v", "t", t_mu_mm)
            op("v", lambda e: e.tensor_copy(MUT2, PD[:, 0:256]))
            t_mucp = op("v", lambda e: e.tensor_copy(STF[:, 0:256], PD[:, 0:256]))
            free_bank(("PD",), "v", t_mucp)
            op("v", lambda e: e.tensor_mul(STF[:, 256:512], STF[:, 0:256],
                                           STF[:, 0:256]))
            w("v", "t", t_ss_mm)
            t_var = op("v", lambda e: e.tensor_sub(STF[:, 0:256], PB[:, 0:256],
                                                   STF[:, 256:512]))
            free_bank(("PB", 0), "v", t_var)
            w("sc", "v", t_var)
            op("sc", lambda e: e.activation(STF[:, 256:512], STF[:, 0:256],
                                            AF.Ln, bias=float(EPS_OUT)))
            t_rst2 = op("sc", lambda e: e.activation(RST2, STF[:, 256:512],
                                                     AF.Exp, scale=-0.5))
            w("v", "sc", t_rst2)
            if t_aend_prev:
                w("v", "t", t_aend_prev)
            for d in range(DC):
                op("v", lambda e, d=d: e.tensor_sub(
                    TMP6[:, 0:256], xs2(d), MUT2))
                op("v", lambda e: e.tensor_mul(TMP6[:, 0:256], TMP6[:, 0:256],
                                               RST2))
                op("v", lambda e, d=d: e.tensor_scalar(
                    H2TC[:, 256 * d:256 * (d + 1)], TMP6[:, 0:256],
                    DERV[:, 16 + d:16 + d + 1], DERV[:, 24 + d:24 + d + 1],
                    op0=ALU.mult, op1=ALU.add))
            h2_tok[tc] = Cn["v"]
            if tc == 0 and "h2tc0" in dbg:
                w("s", "v", h2_tok[tc])
                dma("s", dbg["h2tc0"][:, :], POOL[:, 0:2048])

        def u_stage(tc):
            w("t", "v", h2_tok[tc])
            use_bank(("PD",))
            for k in range(DC):
                mm(PD[0:32, 0:256], ASEL[:, 32 * k:32 * (k + 1)],
                   H2TC[:, 256 * k:256 * (k + 1)], start=(k == 0),
                   stop=(k == DC - 1), sem=(k == DC - 1))
            w("v", "t", Cn["t"])
            op("v", lambda e: e.tensor_copy(UB[0:32, 0:256], PD[0:32, 0:256]))
            op("v", lambda e: e.tensor_copy(UB[0:32, 256:512], PD[0:32, 0:256]))
            t_u01 = op("v", lambda e: e.tensor_scalar(
                UB[0:16, 256:512], PD[0:16, 0:256], -1.0, None, op0=ALU.mult))
            free_bank(("PD",), "v", t_u01)
            u_tok[tc] = t_u01
            if tc == 0 and "u01" in dbg:
                w("s", "v", t_u01)
                dma("s", dbg["u01"][:, :], UB[:, :])

        AROT = [(PA, 0, ("PA", 0)), (PA, 512, ("PA", 1)), (PD, 0, ("PD",)),
                (PB, 0, ("PB", 0)), (PB, 512, ("PB", 1)), (PC, 0, ("PC", 0))]

        def a_stage(tc, t_p3t):
            w("t", "v", u_tok[tc])
            for kc in range(8):
                wdma("t", "g", wi_tok[kc])

            def a_head(fc):
                bk, off, key = AROT[fc % 6]
                bank = bk[:, off:off + 256]
                r0s = R0P[:, 256 * (fc % 3):256 * (fc % 3) + 256]
                use_bank(key)
                mm(bank, ARENA[0:16, 128 * fc:128 * fc + 128], UB[0:16, 0:256],
                   start=True, stop=False)
                for k in range(DC):
                    mm(bank, wiTblk(k, fc), H2TC[:, 256 * k:256 * (k + 1)],
                       start=False, stop=False, sem=(k == DC - 1))
                t_head = Cn["t"]
                w("sc", "t", t_head)
                w("sc", "v", Cn["v"])
                op("sc", lambda e, bank=bank, r0s=r0s: e.activation(
                    r0s, bank, AF.Relu, scale=TKB[:, 0:1]))
                return Cn["sc"]

            def a_tail(fc, t_ev0):
                bk, off, key = AROT[fc % 6]
                bank = bk[:, off:off + 256]
                r0s = R0P[:, 256 * (fc % 3):256 * (fc % 3) + 256]
                r1s = R1P[:, 256 * (fc % 3):256 * (fc % 3) + 256]
                w("t", "sc", t_ev0)
                mm(bank, ARENA[0:32, 128 * fc:128 * fc + 128], UB[0:32, 256:512],
                   start=False, stop=True, sem=True)
                w("sc", "t", Cn["t"])
                t_ev1 = op("sc", lambda e, bank=bank, r1s=r1s: e.activation(
                    r1s, bank, AF.Relu, scale=TKB[:, 1:2]))
                free_bank(key, "sc", t_ev1)
                w("v", "sc", t_ev1)
                if tc == 0:
                    w("v", "t", t_p3t)
                op("v", lambda e, fc=fc, r0s=r0s, r1s=r1s: e.tensor_add(
                    W0[:, 256 * fc:256 * (fc + 1)], r0s, r1s))

            pend = []
            for fc in range(FC):
                ev0 = a_head(fc)
                pend.append((fc, ev0))
                if len(pend) > 2:
                    a_tail(*pend.pop(0))
            aend_tok[tc + 1] = Cn["t"]
            for it in pend:
                a_tail(*it)
            interC_tok[tc] = Cn["v"]
            if tc == 0 and "interC0" in dbg:
                w("s", "v", interC_tok[tc])
                dma("s", dbg["interC0"][:, :], W0[:, :])
            if tc == 0:
                cut("aphase")

        def mm2_stage(tc):
            w("t", "v", interC_tok[tc])
            for fc in range(FC):
                wdma("t", "g", wo_tok[fc])
            dp_bank = [(PB, 0), (PB, 256), (PB, 512), (PB, 768),
                       (PC, 0), (PC, 256), (PC, 512), (PC, 768)]
            for dp in range(DC):
                bk, off = dp_bank[dp]
                key = ("PB", off // 512) if bk is PB else ("PC", off // 512)
                if dp % 2 == 0:
                    use_bank(key)
                for fc in range(FC):
                    mm(bk[:, off:off + 256], woTblk(fc, dp),
                       W0[:, 256 * fc:256 * (fc + 1)], start=(fc == 0),
                       stop=(fc == FC - 1), sem=(fc == FC - 1))
                if dp % 2 == 1:
                    t_dp = Cn["t"]
                    w("v", "t", t_dp)
                    for d2 in (dp - 1, dp):
                        bk2, off2 = dp_bank[d2]
                        op("v", lambda e, bk2=bk2, off2=off2, d2=d2:
                           e.tensor_scalar(
                               TMP6[:, 256:512], bk2[:, off2:off2 + 256],
                               DERV[:, 40 + d2:40 + d2 + 1], None, op0=ALU.mult))
                        tv = op("v", lambda e, d2=d2, tc=tc: e.tensor_add(
                            SX[:, 2048 * d2 + 256 * tc:
                               2048 * d2 + 256 * tc + 256],
                            SX[:, 2048 * d2 + 256 * tc:
                               2048 * d2 + 256 * tc + 256],
                            TMP6[:, 256:512]))
                        free_bank(key, "v", tv)
            t_x2 = Cn["v"]
            if tc == 0:
                cut("tc0")
            w("g", "v", t_x2)
            dma("g",
                out_xT.rearrange("(d p) t -> p d t", p=128)[:, :,
                                                            256 * tc:256 * (tc + 1)],
                SX.ap().rearrange("p (d t) -> p d t", d=DC)[:, :,
                                                            256 * tc:256 * (tc + 1)])

        ln2_stage(0, None)
        if True:
            cut("ln2s")
        u_stage(0)
        cut("u")
        for tc in range(TCN):
            a_stage(tc, t_p3t)
            if tc + 1 < TCN:
                ln2_stage(tc + 1, aend_tok[tc + 1])
            mm2_stage(tc)
            if tc + 1 < TCN:
                u_stage(tc + 1)

        w("g", "dg", D16["g"])
        w("s", "ds", D16["s"])

    with nc.allow_low_precision("bf16 recipe calibrated by problem tolerance"), \
            nc.Block() as blk:
        def run(name, e):
            try:
                emit(name, e)
            except _Cut:
                pass

        @blk.tensor
        def _(e):
            run("t", e)

        @blk.vector
        def _(e):
            run("v", e)

        @blk.scalar
        def _(e):
            run("sc", e)

        @blk.gpsimd
        def _(e):
            run("g", e)

        @blk.sync
        def _(e):
            run("s", e)

    return nc, dbg


_NC_CACHE = {}


def _get_nc(debug=(), stop_after=None):
    key = (tuple(sorted(debug)), stop_after)
    if key not in _NC_CACHE:
        _NC_CACHE[key] = build_nc(debug=key[0], stop_after=stop_after)
    return _NC_CACHE[key]


def _host_inputs(inputs, core):
    b = core // 2
    x = np.asarray(inputs["L0_attn_x"])
    xs = x[b, 2048 * (core % 2):2048 * (core % 2 + 1), :]
    m = {}
    m["xT"] = np.ascontiguousarray(xs.T).astype(np.float32)
    for w in ("Wq", "Wk", "Wv", "Wq2", "Wk2", "Wv2", "Wo"):
        m[w] = np.ascontiguousarray(np.asarray(inputs["L0_attn_" + w],
                                               dtype=np.float32))
    vecs = [inputs["L0_attn_bq"], inputs["L0_attn_bk"], inputs["L0_attn_bv"],
            inputs["L0_attn_bv2"], inputs["L0_attn_bo"],
            inputs["L0_attn_gamma"], inputs["L0_attn_beta"]]
    m["vecs"] = np.ascontiguousarray(np.concatenate(
        [np.asarray(v, dtype=np.float32).reshape(8, 128).T for v in vecs],
        axis=1))
    m["brow"] = np.concatenate(
        [np.asarray(inputs["L0_attn_bq2"], dtype=np.float32).reshape(1, -1),
         np.asarray(inputs["L0_attn_bk2"], dtype=np.float32).reshape(1, -1)],
        axis=1)
    m["cT"] = np.ascontiguousarray(
        np.asarray(inputs["L0_c"], dtype=np.float32)[b].reshape(8, 128).T)
    m["wmod"] = np.ascontiguousarray(np.asarray(inputs["L0_wmod"],
                                                dtype=np.float32))
    m["wiT"] = np.ascontiguousarray(
        np.asarray(inputs["L0_ffn_wi_w"], dtype=np.float32).T)
    m["woT"] = np.ascontiguousarray(
        np.asarray(inputs["L0_ffn_wo_w"], dtype=np.float32).T)
    m["Aall"] = np.ascontiguousarray(
        np.asarray(inputs["L0_ffn_lora_As"], dtype=np.float32).reshape(128, 1024))
    m["BallT"] = np.ascontiguousarray(
        np.asarray(inputs["L0_ffn_lora_Bs"],
                   dtype=np.float32).transpose(0, 2, 1).reshape(128, 4096))
    m["idx"] = np.asarray(inputs["L0_ffn_top_k_indices"]).reshape(1, 2).astype(
        np.int32)
    m["tkv"] = np.asarray(inputs["L0_ffn_top_k_values"],
                          dtype=np.float32).reshape(1, 2)
    constB = np.zeros((128, 272), dtype=np.float32)
    constB[:, 0:128] = 1.0 / 1024.0
    constB[:, 128:256] = np.eye(128, dtype=np.float32)
    constB[:, 256:272] = np.tile(np.eye(16, dtype=np.float32), (8, 1))
    m["constB"] = constB
    constF = np.zeros((128, 8), dtype=np.float32)
    constF[:, 0] = np.arange(128) // 16
    constF[:, 1] = -0.5
    constF[:, 2] = 1e-5
    constF[:, 3] = 1e-3
    constF[:, 4] = 0.0
    constF[:, 5] = 1.0
    m["constF"] = constF
    sel = np.zeros((16, 1024), dtype=np.float32)
    for h in range(16):
        c, hh = h // 2, h % 2
        sel[h, 128 * c + 64 * hh:128 * c + 64 * hh + 64] = 1.0
    m["sel"] = sel
    m["ones_in"] = np.ones((1, 128), dtype=np.float32)
    return m


def kernel(**inputs):
    nc, dbg = _get_nc()
    in_maps = [_host_inputs(inputs, core) for core in range(8)]
    res = run_bass_kernel_spmd(nc, in_maps, core_ids=list(range(8)))
    outs = [np.asarray(res.results[c]["out_xT"]).T for c in range(8)]
    return np.concatenate(outs, axis=0).reshape(4, 4096, 1024).astype(np.float32)


# version 10
# speedup vs baseline: 1.5236x; 1.0385x over previous
import numpy as np
import concourse.bass as bass
import concourse.mybir as mybir
from concourse.bass_utils import run_bass_kernel_spmd

F32 = mybir.dt.float32
BF16 = mybir.dt.bfloat16
I32 = mybir.dt.int32
AF = mybir.ActivationFunctionType
ALU = mybir.AluOpType
AX = mybir.AxisListType

T = 2048
D = 1024
DC = 8
NW = 4
NT = 16
H = 16
DH = 64
FF = 4096
FC = 32
TCN = 8
EPS_IN = 1e-3
EPS_OUT = 1e-5
EPS_QK = 1e-6


class _Cut(Exception):
    pass


def build_nc(debug=(), stop_after=None):
    nc = bass.Bass("TRN2", target_bir_lowering=False)
    dbgn = set(debug)

    def din(name, shape, dtype=F32):
        return nc.dram_tensor(name, list(shape), dtype, kind="ExternalInput").ap()

    xT = din("xT", [D, T])
    WTS = {w: din(w, [D, D]) for w in ("Wq", "Wk", "Wv", "Wq2", "Wk2", "Wv2", "Wo")}
    vecs_in = din("vecs", [128, 7 * DC])
    brow_in = din("brow", [1, 2 * D])
    cT_in = din("cT", [128, DC])
    wmod_in = din("wmod", [D, 6 * D])
    wiT_in = din("wiT", [D, FF])
    woT_in = din("woT", [FF, D])
    Aall_in = din("Aall", [128, D])
    BallT_in = din("BallT", [128, FF])
    idx_in = din("idx", [1, 2], I32)
    tkv_in = din("tkv", [1, 2])
    constB_in = din("constB", [128, 272])
    constF_in = din("constF", [128, 8])
    sel_in = din("sel", [16, 1024])
    ones_in = din("ones_in", [1, 128])

    out_xT = nc.dram_tensor("out_xT", [D, T], F32, kind="ExternalOutput").ap()
    cc_in = nc.dram_tensor("cc_in", [64, H * DH], BF16, kind="Internal").ap()
    cc_out = nc.dram_tensor("cc_out", [64, H * DH], BF16, kind="Internal").ap()

    dbg = {}
    def dbgt(name, shape, dtype=F32):
        if name in dbgn:
            dbg[name] = nc.dram_tensor("dbg_" + name, list(shape), dtype,
                                       kind="ExternalOutput").ap()
    dbgt("modT", [128, 48]); dbgt("hT", [128, 16384], BF16)
    dbgt("q2", [128, 16384], BF16); dbgt("k2", [128, 16384], BF16)
    dbgt("Stot", [64, 1024], BF16); dbgt("qks", [128, 256])
    dbgt("v2", [128, 16384], BF16); dbgt("yT", [128, 16384], BF16)
    dbgt("x1T", [128, 16384], BF16); dbgt("h2tc0", [128, 2048], BF16)
    dbgt("interC0", [128, 8192], BF16); dbgt("BselT", [128, 4096], BF16)
    dbgt("u01", [64, 512], BF16); dbgt("OH", [128, 32], BF16)
    dbgt("qksrT", [128, 2048], BF16); dbgt("ASEL", [128, 256], BF16)
    dbgt("recip", [128, 256], BF16); dbgt("qks2", [128, 256])
    dbgt("DERV", [128, 64])

    A = nc.alloc_sbuf_tensor
    SX = A("SX", [128, 16384], BF16)
    SH = A("SH", [128, 16384], BF16)
    SB = A("SB", [128, 16384], BF16)
    SC = A("SC", [128, 16384], BF16)
    SAa = A("SAa", [128, 4096], BF16)
    SAb = A("SAb", [128, 4096], BF16)
    W0 = A("W0", [128, 8192], BF16)
    W1 = A("W1", [128, 8192], BF16)
    POOL = A("POOL", [128, 5120], BF16)
    CONST = A("CONSTB", [128, 272], BF16)
    P16T = A("P16T", [128, 8], F32)
    VECS = A("VECS", [128, 7 * DC], F32)
    DERV = A("DERV", [128, 8 * DC], F32)
    MODT = A("MODT", [128, 48], F32)
    CT = A("CT", [128, DC], F32)
    SIG = A("SIG", [128, DC], F32)
    SILU = A("SILU", [128, DC], BF16)
    ARENA = A("ARENA", [128, 4096], BF16)
    SEL = A("SEL", [16, 1024], BF16)
    UB = A("UB", [64, 512], BF16)
    BROW = A("BROW", [1, D], BF16)
    STF = A("STF", [128, 1024], F32)
    ASEL = A("ASEL", [128, 256], BF16)
    TKB = A("TKB", [128, 2], F32)
    IDXS = A("IDXS", [1, 2], I32)
    IDXF = A("IDXF", [1, 2], F32)
    IDXB = A("IDXB", [1, 2], BF16)
    OHE = A("OHE", [128, 2], F32)
    OH = A("OH", [128, 32], BF16)
    TKS = A("TKS", [1, 2], BF16)
    ONES1 = A("ONES1", [1, 128], BF16)

    MUT = POOL[:, 0:512]
    RST = POOL[:, 512:1024]
    TMPE = POOL[:, 1024:2048]
    TMPF = POOL[:, 2048:3072]
    QKSRT = POOL[0:16, 3072:5120]
    SPART = POOL[0:64, 1024:2048]
    ROW0 = POOL[0:1, 3072:3584]
    ROW1 = POOL[0:1, 3584:4096]
    H2TC = POOL[:, 0:2048]
    R0P = POOL[:, 2048:2816]
    R1P = POOL[:, 2816:3584]
    MUT2 = POOL[:, 3584:3840]
    RST2 = POOL[:, 3840:4096]
    TMP6 = POOL[:, 4096:4608]
    XSQ = POOL[:, 4608:5120]
    QKS = STF[:, 768:1024]
    STOT = POOL[0:64, 3072:4096]
    STOT64 = POOL[64:128, 3072:4096]

    Jc = CONST[:, 0:128]
    I128 = CONST[:, 128:256]
    I16t = CONST[:, 256:272]

    PA = nc.alloc_psum_tensor("PA", [128, 1024], F32)
    PB = nc.alloc_psum_tensor("PB", [128, 1024], F32)
    PC = nc.alloc_psum_tensor("PC", [128, 1024], F32)
    PD = nc.alloc_psum_tensor("PD", [128, 512], F32)
    PTB = nc.alloc_psum_tensor("PTB", [128, 1024], BF16)

    sems = {}
    for name in ("t", "v", "sc", "g", "s", "dg", "ds"):
        sems[name] = nc.alloc_semaphore("sem_" + name)

    for col, val in ((1, -0.5), (2, 1e-5), (3, 1e-3), (4, 0.0), (5, 1.0)):
        nc.const_aps.register(F32, float(val), P16T[:, col:col + 1])

    def W0b(k, m):
        return W0[:, 1024 * k + 128 * m: 1024 * k + 128 * m + 128]

    def W1b(k, m):
        return W1[:, 1024 * k + 128 * m: 1024 * k + 128 * m + 128]

    def wiTblk(k, fc):
        if k < 4:
            return SB[:, 4096 * k + 128 * fc: 4096 * k + 128 * fc + 128]
        if k < 6:
            return W1[:, 4096 * (k - 4) + 128 * fc: 4096 * (k - 4) + 128 * fc + 128]
        if k == 6:
            return SAa[:, 128 * fc: 128 * fc + 128]
        return SAb[:, 128 * fc: 128 * fc + 128]

    def woTblk(fc, dp):
        if fc < 16:
            return SH[:, 1024 * fc + 128 * dp: 1024 * fc + 128 * dp + 128]
        return SC[:, 1024 * (fc - 16) + 128 * dp: 1024 * (fc - 16) + 128 * dp + 128]

    def wmod_rhs(k, n):
        if k < 4:
            base = [SB, SB, SC, SC][k]
            off = (k % 2) * 6144
            return base[:, off + 512 * n: off + 512 * n + 512]
        if k == 4:
            if n < 4:
                return SC[:, 14336 + 512 * n: 14336 + 512 * n + 512]
            if n < 8:
                return SAb[:, 2048 + 512 * (n - 4): 2048 + 512 * (n - 4) + 512]
            return POOL[:, 1024 + 512 * (n - 8): 1024 + 512 * (n - 8) + 512]
        if k == 5:
            return W1[:, 512 * n: 512 * n + 512]
        if k == 6:
            if n < 8:
                return SAa[:, 512 * n: 512 * n + 512]
            return SAb[:, 512 * (n - 8): 512 * (n - 8) + 512]
        if n < 8:
            return SB[:, 12288 + 512 * n: 12288 + 512 * n + 512]
        return SC[:, 12288 + 512 * (n - 8): 12288 + 512 * (n - 8) + 512]

    def emit(my, e):
        def cut(name):
            if stop_after == name:
                raise _Cut()
        Cn = {k: 0 for k in ("t", "v", "sc", "g", "s")}
        D16 = {"g": 0, "s": 0}
        lastw = {}
        bankfree = {}

        def op(eng2, builder, sem=True):
            if my == eng2:
                ins = builder(e)
                if sem:
                    ins.then_inc(sems[eng2], 1)
            if sem:
                Cn[eng2] += 1
            return Cn[eng2]

        def w(eng2, sem_name, val):
            if my != eng2 or val is None or val <= 0:
                return
            if lastw.get(sem_name, -1) >= val:
                return
            lastw[sem_name] = val
            e.wait_ge(sems[sem_name], val)

        def dma(q, out, in_, **kw):
            D16[q] += 16
            if my == q:
                e.dma_start(out=out, in_=in_, **kw).then_inc(
                    sems["dg" if q == "g" else "ds"], 16)
            return D16[q]

        def wdma(eng2, q, tok):
            if tok:
                w(eng2, "dg" if q == "g" else "ds", tok)

        def fence(eng2):
            w(eng2, eng2, Cn[eng2])

        def mm(out, lhsT, rhs, start, stop, sem=False):
            return op("t", lambda e: e.matmul(out, lhsT, rhs, start=start,
                                              stop=stop), sem=sem)

        def use_bank(*keys):
            for k in keys:
                if k in bankfree:
                    eng2, tok = bankfree[k]
                    w("t", eng2, tok)

        def free_bank(key, eng2, tok):
            bankfree[key] = (eng2, tok)

        tok_p16 = dma("s", P16T[:, :], constF_in[:, :])
        tok_vecs = dma("s", VECS[:, :], vecs_in[:, :])
        tok_ct = dma("s", CT[:, :], cT_in[:, :])
        tok_idx = dma("s", IDXS[:, :], idx_in[:, :])

        tok_consts = dma("g", CONST[:, :], constB_in[:, :])
        tok_ones1 = dma("g", ONES1[:, :], ones_in[:, :])
        tok_sel = dma("g", SEL[:, :], sel_in[:, :])
        tok_brow = dma("g", BROW[:, :], brow_in[0:1, 0:D])
        tok_x = dma("g", SX.ap().rearrange("p (d t) -> p d t", d=DC),
                    xT.rearrange("(d p) t -> p d t", p=128))
        wmod_tok = []
        for k in range(4):
            dsts = [SB, SB, SC, SC][k]
            off = (k % 2) * 6144
            wmod_tok.append(dma("g", dsts[:, off:off + 6144],
                                wmod_in[128 * k:128 * (k + 1), :]))
        wmod_tok.append(dma("g", SC[:, 14336:16384], wmod_in[512:640, 0:2048]))
        wmod_tok.append(dma("g", SAb[:, 2048:4096], wmod_in[512:640, 2048:4096]))
        wmod_tok.append(dma("g", POOL[:, 1024:3072], wmod_in[512:640, 4096:6144]))
        wmod_tok.append(dma("g", W1[:, 0:6144], wmod_in[640:768, :]))
        wmod_tok.append(dma("g", SAa[:, 0:4096], wmod_in[768:896, 0:4096]))
        wmod_tok.append(dma("g", SAb[:, 0:2048], wmod_in[768:896, 4096:6144]))
        wmod_tok.append(dma("g", SB[:, 12288:16384], wmod_in[896:1024, 0:4096]))
        wmod_tok.append(dma("g", SC[:, 12288:14336], wmod_in[896:1024, 4096:6144]))
        tok_aall = dma("g", W1[:, 6144:7168], Aall_in[:, :])
        tok_ball = dma("g", ARENA[:, :], BallT_in[:, :])
        tok_tkf = dma("g", TKS[:, :], tkv_in[:, :])

        def wload(buf, name, gate=None):
            if gate is not None:
                w("g", gate[0], gate[1])
            return dma("g", buf.ap().rearrange("p (k m j) -> p k m j", k=8, j=128),
                       WTS[name].rearrange("(k p) (m j) -> p k m j", p=128, j=128))

        tok_wq = wload(W0, "Wq")

        wdma("t", "g", tok_consts)
        wdma("t", "g", tok_x)

        def ln_stats(src_chunk_ap, eps, sq_scratch):
            use_bank(("PA", 0))
            for d in range(DC):
                mm(PA[:, 0:512], Jc, src_chunk_ap(d), start=(d == 0),
                   stop=(d == DC - 1), sem=(d == DC - 1))
            t_mu_mm = Cn["t"]
            w("sc", "t", Cn["t"])
            for d in range(DC):
                op("sc", lambda e, d=d: e.activation(
                    sq_scratch(d), src_chunk_ap(d), AF.Square))
            t_sq = Cn["sc"]
            use_bank(("PA", 1))
            w("t", "sc", t_sq)
            for d in range(DC):
                mm(PA[:, 512:1024], Jc, sq_scratch(d), start=(d == 0),
                   stop=(d == DC - 1), sem=(d == DC - 1))
            t_ss_mm = Cn["t"]
            w("v", "t", t_mu_mm)
            op("v", lambda e: e.tensor_copy(MUT, PA[:, 0:512]))
            t_mucp = op("v", lambda e: e.tensor_copy(STF[:, 0:512], PA[:, 0:512]))
            free_bank(("PA", 0), "v", t_mucp)
            op("v", lambda e: e.tensor_mul(STF[:, 512:1024], STF[:, 0:512],
                                           STF[:, 0:512]))
            w("v", "t", t_ss_mm)
            t_var = op("v", lambda e: e.tensor_sub(STF[:, 0:512], PA[:, 512:1024],
                                                   STF[:, 512:1024]))
            free_bank(("PA", 1), "v", t_var)
            w("sc", "v", t_var)
            wdma("sc", "s", tok_p16)
            op("sc", lambda e, eps=eps: e.activation(STF[:, 512:1024],
                                                     STF[:, 0:512], AF.Ln,
                                                     bias=float(eps)))
            t_rst = op("sc", lambda e: e.activation(RST, STF[:, 512:1024],
                                                    AF.Exp, scale=-0.5))
            return t_rst

        for w_ in range(NW):
            xs = lambda d, w_=w_: SX[:, 2048 * d + 512 * w_:
                                     2048 * d + 512 * w_ + 512]
            sqs = lambda d, w_=w_: SH[:, 2048 * d + 512 * w_:
                                      2048 * d + 512 * w_ + 512]
            t_rst = ln_stats(xs, EPS_OUT, sqs)
            w("v", "sc", t_rst)
            for d in range(DC):
                hs = SH[:, 2048 * d + 512 * w_: 2048 * d + 512 * w_ + 512]
                op("v", lambda e, d=d, w_=w_, hs=hs: e.tensor_sub(
                    hs, SX[:, 2048 * d + 512 * w_: 2048 * d + 512 * w_ + 512],
                    MUT))
                op("v", lambda e, hs=hs: e.tensor_mul(hs, hs, RST))
        t_hn = Cn["v"]

        wdma("sc", "s", tok_ct)
        t_sig = op("sc", lambda e: e.activation(SIG[:, :], CT[:, :], AF.Sigmoid))
        w("v", "sc", t_sig)
        t_silu = op("v", lambda e: e.tensor_mul(SILU[:, :], CT[:, :], SIG[:, :]))

        rowfree = [0, 0]
        wdma("t", "g", tok_consts)
        for n in range(12):
            key = ("PA", n % 2)
            bank = PA[0:1, 0:512] if n % 2 == 0 else PA[0:1, 512:1024]
            rowslot = ROW0 if n % 2 == 0 else ROW1
            wdma("t", "g", wmod_tok[11])
            w("t", "v", t_silu)
            use_bank(key)
            for k in range(8):
                mm(bank, SILU[:, k:k + 1], wmod_rhs(k, n),
                   start=(k == 0), stop=(k == 7), sem=(k == 7))
            w("v", "t", Cn["t"])
            w("v", "t", rowfree[n % 2])
            tv = op("v", lambda e, bank=bank, rowslot=rowslot: e.tensor_copy(
                rowslot, bank))
            free_bank(key, "v", tv)
            w("t", "v", tv)
            use_bank(("PTB7",))
            for c in range(4):
                op("t", lambda e, c=c, n=n, rowslot=rowslot: e.transpose(
                    PTB[:, 8 * (n % 2) + 2 * c: 8 * (n % 2) + 2 * c + 1],
                    rowslot[0:1, 128 * c:128 * c + 128], I128[0:1, 0:1]),
                    sem=(c == 3))
            rowfree[n % 2] = Cn["t"]
            w("v", "t", Cn["t"])
            vv, cc4 = n // 2, 4 * (n % 2)
            ptb_view = PTB[:, 8 * (n % 2): 8 * (n % 2) + 8].rearrange(
                "p (c two) -> p c two", two=2)[:, :, 0:1]
            tvv = op("v", lambda e, vv=vv, cc4=cc4, ptb_view=ptb_view:
                     e.tensor_copy(
                         MODT[:, 8 * vv + cc4: 8 * vv + cc4 + 4], ptb_view))
            free_bank(("PTB7",), "v", tvv)
        t_modr = Cn["v"]

        wdma("v", "s", tok_vecs)
        fence("v")
        op("v", lambda e: e.tensor_scalar(DERV[:, 0:8], MODT[:, 8:16], 1.0, None,
                                          op0=ALU.add))
        op("v", lambda e: e.tensor_copy(DERV[:, 8:16], MODT[:, 0:8]))
        op("v", lambda e: e.tensor_scalar(DERV[:, 16:24], MODT[:, 32:40], 1.0, None,
                                          op0=ALU.add))
        op("v", lambda e: e.tensor_copy(DERV[:, 24:32], MODT[:, 24:32]))
        for (src0, dst) in ((16, 32), (40, 40)):
            op("v", lambda e, s=src0: e.tensor_scalar(
                STF[:, 8:16], MODT[:, s:s + 8], 0.0, None, op0=ALU.is_ge))
            fence("v")
            op("v", lambda e: e.tensor_scalar(
                STF[:, 8:16], STF[:, 8:16], 2.0, -1.0, op0=ALU.mult, op1=ALU.add))
            fence("v")
            op("v", lambda e, s=src0: e.tensor_mul(
                STF[:, 0:8], MODT[:, s:s + 8], STF[:, 8:16]))
            fence("v")
            op("v", lambda e: e.tensor_scalar(
                STF[:, 0:8], STF[:, 0:8], 0.25, None, op0=ALU.max))
            fence("v")
            op("v", lambda e, d=dst: e.tensor_mul(
                DERV[:, d:d + 8], STF[:, 0:8], STF[:, 8:16]))
            fence("v")
        op("v", lambda e: e.tensor_mul(DERV[:, 48:56], DERV[:, 32:40],
                                       VECS[:, 40:48]))
        t_derv = op("v", lambda e: e.tensor_mul(DERV[:, 56:64], DERV[:, 32:40],
                                                VECS[:, 48:56]))
        if "modT" in dbg:
            w("s", "v", t_derv)
            dma("s", dbg["modT"][:, :], MODT[:, :])
        if "DERV" in dbg:
            w("s", "v", t_derv)
            dma("s", dbg["DERV"][:, :], DERV[:, :])
        for d in range(DC):
            op("v", lambda e, d=d: e.tensor_scalar(
                SH[:, 2048 * d:2048 * (d + 1)], SH[:, 2048 * d:2048 * (d + 1)],
                DERV[:, d:d + 1], DERV[:, 8 + d:8 + d + 1],
                op0=ALU.mult, op1=ALU.add))
        t_h = Cn["v"]
        tok_dbg_h = None
        if "hT" in dbg:
            w("s", "v", t_h)
            tok_dbg_h = dma("s", dbg["hT"][:, :], SH[:, :])
        cut("p1")

        cut("p0a")

        wdma("v", "s", tok_idx)
        op("v", lambda e: e.tensor_copy(IDXF[:, :], IDXS[:, :]))
        fence("v")
        op("v", lambda e: e.tensor_scalar(IDXF[:, :], IDXF[:, :], 0.0, 7.0,
                                          op0=ALU.max, op1=ALU.min))
        fence("v")
        t_idxb = op("v", lambda e: e.tensor_copy(IDXB[:, :], IDXF[:, :]))
        wdma("t", "g", tok_ones1)
        w("t", "v", t_idxb)
        use_bank(("PD",))
        mm(PD[0:128, 0:2], ONES1[0:1, 0:128], IDXB[0:1, 0:2], True, True, sem=True)
        w("v", "t", Cn["t"])
        wdma("v", "s", tok_p16)
        t_ohe = op("v", lambda e: e.tensor_scalar(OHE[:, :], PD[0:128, 0:2],
                                                  P16T[:, 0:1], None,
                                                  op0=ALU.is_equal))
        t_ohz = op("g", lambda e: e.memset(OH[:, :], 0))
        wdma("v", "g", tok_consts)
        w("v", "g", t_ohz)
        fence("v")
        for k in range(2):
            op("v", lambda e, k=k: e.tensor_scalar(
                OH[:, 16 * k:16 * k + 16], I16t, OHE[:, k:k + 1], None,
                op0=ALU.mult))
        t_oh = Cn["v"]
        free_bank(("PD",), "v", t_ohe)
        wdma("t", "g", tok_tkf)
        use_bank(("PD",))
        mm(PD[0:128, 2:4], ONES1[0:1, 0:128], TKS[0:1, 0:2], True, True, sem=True)
        w("v", "t", Cn["t"])
        t_tkb = op("v", lambda e: e.tensor_copy(TKB[:, :], PD[0:128, 2:4]))
        free_bank(("PD",), "v", t_tkb)

        wdma("t", "g", tok_aall)
        w("t", "v", t_oh)
        for d in range(DC):
            key = ("PC", d % 2)
            bank = PC[:, 0:32] if d % 2 == 0 else PC[:, 512:544]
            use_bank(key)
            mm(bank, W1[:, 6144 + 128 * d:6144 + 128 * (d + 1)], OH[:, :],
               True, True, sem=True)
            w("v", "t", Cn["t"])
            tv = op("v", lambda e, d=d, bank=bank: e.tensor_copy(
                ASEL[:, 32 * d:32 * (d + 1)], bank))
            free_bank(key, "v", tv)
        wdma("t", "g", tok_ball)
        for n in range(8):
            use_bank(("PD",))
            mm(PD[0:32, 0:512], OH[:, :], ARENA[:, 512 * n:512 * (n + 1)],
               True, True, sem=True)
            w("v", "t", Cn["t"])
            tv = op("v", lambda e, n=n: e.tensor_copy(
                ARENA[0:32, 512 * n:512 * (n + 1)], PD[0:32, 0:512]))
            free_bank(("PD",), "v", tv)
        t_bsel = Cn["v"]
        t_p0b = Cn["t"]
        if "OH" in dbg:
            w("s", "v", t_bsel)
            dma("s", dbg["OH"][:, :], OH[:, :])
        if "BselT" in dbg:
            w("s", "v", t_bsel)
            dma("s", dbg["BselT"][:, :], ARENA[:, :])
        if "ASEL" in dbg:
            w("s", "v", t_bsel)
            dma("s", dbg["ASEL"][:, :], ASEL[:, :])
        cut("p0b")

        tok_wq2 = wload(W1, "Wq2", gate=("t", t_p0b))

        FMROT = [(PB, 0), (PB, 512), (PC, 0), (PC, 512)]

        def fm_proj(wblk, hoff_w, dstc, bias_col):
            for m in range(DC):
                bk, off = FMROT[m % 4]
                key = ("PB" if bk is PB else "PC", off // 512)
                bank = bk[:, off:off + 512]
                use_bank(key)
                for k in range(DC):
                    mm(bank, wblk(k, m),
                       SH[:, 2048 * k + hoff_w: 2048 * k + hoff_w + 512],
                       start=(k == 0), stop=(k == DC - 1), sem=(k == DC - 1))
                w("sc", "t", Cn["t"])
                tv = op("sc", lambda e, m=m, bank=bank: e.activation(
                    dstc(m), bank, AF.Identity,
                    bias=VECS[:, bias_col * 8 + m: bias_col * 8 + m + 1]))
                free_bank(key, "sc", tv)

        TPROT = [(PA, "PA"), (PB, "PB"), (PC, "PC")]

        def tok_proj(sax, dstsb, wbuf, base_tile, toks=None):
            for tt in range(4):
                i = base_tile + tt
                PT, pnm = TPROT[i % 3]
                use_bank((pnm, 0), (pnm, 1))
                for half in range(2):
                    bank = PT[:, 0:512] if half == 0 else PT[:, 512:1024]
                    for k in range(DC):
                        mm(bank,
                           sax[:, 512 * k + 128 * tt: 512 * k + 128 * tt + 128],
                           wbuf[:, 1024 * k + 512 * half:
                                1024 * k + 512 * half + 512],
                           start=(k == 0), stop=False)
                    mm(bank, ONES1[0:1, 0:128],
                       BROW[0:1, 512 * half: 512 * half + 512],
                       start=False, stop=True, sem=(half == 1))
                t_mm = Cn["t"]
                w("sc", "t", t_mm)
                op("sc", lambda e, i=i, PT=PT: e.activation(
                    dstsb[:, 1024 * i:1024 * (i + 1)], PT[:, :], AF.Relu))
                t_relu = Cn["sc"]
                w("v", "sc", t_relu)
                t_min = op("v", lambda e, PT=PT: e.tensor_scalar(
                    TMPE, PT[:, 0:1024], 0.0, None, op0=ALU.min))
                free_bank((pnm, 0), "v", t_min)
                free_bank((pnm, 1), "v", t_min)
                w("sc", "v", t_min)
                t_exp = op("sc", lambda e: e.activation(TMPE, TMPE, AF.Exp))
                w("v", "sc", t_exp)
                tadd = op("v", lambda e, i=i: e.tensor_add(
                    dstsb[:, 1024 * i:1024 * (i + 1)],
                    dstsb[:, 1024 * i:1024 * (i + 1)], TMPE))
                if toks is not None:
                    toks.append(tadd)

        wdma("t", "g", tok_wq)
        wdma("t", "g", tok_brow)
        w("t", "v", t_h)
        t_wq_done = None
        for w_ in range(NW):
            sax = SAa if w_ % 2 == 0 else SAb
            dstc = lambda m, sax=sax: sax[:, 512 * m: 512 * m + 512]
            fm_proj(W0b, 512 * w_, dstc, 0)
            if w_ == NW - 1:
                t_wq_done = Cn["t"]
            wdma("t", "g", tok_wq2)
            w("t", "sc", Cn["sc"])
            tok_proj(sax, SB, W1, 4 * w_)
        t_qchain = Cn["t"]
        t_q2v = Cn["v"]
        cut("qchain")
        tok_wk = wload(W0, "Wk", gate=("t", t_wq_done))
        tok_wk2 = wload(W1, "Wk2", gate=("t", t_qchain))
        w("g", "v", t_q2v)
        tok_brow2 = dma("g", BROW[:, :], brow_in[0:1, D:2 * D])

        elu_k = []
        t_wk_done = None
        for w_ in range(NW):
            sax = SAa if w_ % 2 == 0 else SAb
            wdma("t", "g", tok_wk)
            dstc = lambda m, sax=sax: sax[:, 512 * m: 512 * m + 512]
            fm_proj(W0b, 512 * w_, dstc, 1)
            if w_ == NW - 1:
                t_wk_done = Cn["t"]
            wdma("t", "g", tok_wk2)
            wdma("t", "g", tok_brow2)
            w("t", "sc", Cn["sc"])
            tok_proj(sax, SC, W1, 4 * w_, toks=elu_k)
            for tt in range(4):
                i = 4 * w_ + tt
                op("v", lambda e, i=i: e.tensor_mul(
                    TMPF, SB[:, 1024 * i:1024 * (i + 1)],
                    SC[:, 1024 * i:1024 * (i + 1)]))
                op("v", lambda e, i=i: e.tensor_reduce(
                    QKS[:, 16 * i:16 * (i + 1)],
                    TMPF.rearrange("p (h d) -> p h d", d=DH),
                    axis=AX.X, op=ALU.add))
        t_k2done = Cn["v"]
        t_kchain = Cn["t"]
        tok_wv = wload(W0, "Wv", gate=("t", t_wk_done))
        tok_wv2 = wload(W1, "Wv2", gate=("t", t_kchain))
        tok_dbg_q2 = tok_dbg_k2 = tok_dbg_qks = None
        if "q2" in dbg:
            w("s", "v", t_k2done)
            tok_dbg_q2 = dma("s", dbg["q2"][:, :], SB[:, :])
        if "k2" in dbg:
            w("s", "v", t_k2done)
            tok_dbg_k2 = dma("s", dbg["k2"][:, :], SC[:, :])
        if "qks" in dbg:
            w("s", "v", t_k2done)
            tok_dbg_qks = dma("s", dbg["qks"][:, :], QKS[:, :])
        cut("kchain")

        for i in range(NT):
            op("v", lambda e, i=i: e.tensor_scalar(
                STF[:, 16 * i:16 * (i + 1)], QKS[:, 16 * i:16 * (i + 1)],
                float(EPS_QK), None, op0=ALU.add))
        t_qe = Cn["v"]
        w("sc", "v", t_qe)
        op("sc", lambda e: e.activation(STF[:, 512:768], STF[:, 0:256], AF.Ln))
        fence("sc")
        t_recip = op("sc", lambda e: e.activation(
            TMPF[:, 0:256], STF[:, 512:768], AF.Exp, scale=-1.0))
        if "recip" in dbg:
            w("s", "sc", t_recip)
            dma("s", dbg["recip"][:, :], TMPF[:, 0:256])
        if "qks2" in dbg:
            w("s", "v", t_qe)
            dma("s", dbg["qks2"][:, :], QKS[:, :])

        use_bank(("PA", 0), ("PA", 1))
        for h in range(H):
            for i in range(NT):
                w("t", "v", elu_k[i])
                mm(PA[0:64, 64 * h:64 * (h + 1)],
                   SC[:, 1024 * i + 64 * h: 1024 * i + 64 * h + 64],
                   SB[:, 1024 * i + 64 * h: 1024 * i + 64 * h + 64],
                   start=(i == 0), stop=(i == NT - 1),
                   sem=(i == NT - 1 and h in (7, 15)))
        t_sdone = Cn["t"]
        w("v", "t", t_sdone)
        op("v", lambda e: e.tensor_copy(SPART[:, 0:512], PA[0:64, 0:512]))
        t_sev = op("v", lambda e: e.tensor_copy(SPART[:, 512:1024],
                                                PA[0:64, 512:1024]))
        free_bank(("PA", 0), "v", t_sev)
        free_bank(("PA", 1), "v", t_sev)
        cut("sphase")
        w("g", "v", t_sev)
        tok_ccin = dma("g", cc_in[:, :], SPART[:, :])
        wdma("g", "g", tok_ccin)
        t_cc = op("g", lambda e: e.collective_compute(
            "AllReduce", ALU.add,
            replica_groups=[[0, 1], [2, 3], [4, 5], [6, 7]],
            ins=[cc_in[:, :]], outs=[cc_out[:, :]]))

        cut("cc")
        w("t", "sc", t_recip)
        for i in range(NT):
            po = 128 * (i % 8)
            use_bank(("PTB7",))
            op("t", lambda e, i=i, po=po: e.transpose(
                PTB[0:16, po:po + 128], TMPF[:, 16 * i:16 * (i + 1)], I128),
                sem=True)
            w("v", "t", Cn["t"])
            tv = op("v", lambda e, i=i, po=po: e.tensor_copy(
                QKSRT[:, 128 * i:128 * (i + 1)], PTB[0:16, po:po + 128]))
            free_bank(("PTB7",), "v", tv)
        t_qksr = Cn["v"]
        if "qksrT" in dbg:
            w("s", "v", t_qksr)
            dma("s", dbg["qksrT"][:, :], POOL[:, 3072:5120])

        wdma("t", "g", tok_sel)
        t_wv_done = None
        for w_ in range(NW):
            sax = SAa if w_ % 2 == 0 else SAb
            wdma("t", "g", tok_wv)
            dstc = lambda m, sax=sax: sax[:, 512 * m: 512 * m + 512]
            fm_proj(W0b, 512 * w_, dstc, 2)
            if w_ == NW - 1:
                t_wv_done = Cn["t"]
            wdma("t", "g", tok_wv2)
            w("t", "sc", Cn["sc"])
            for m in range(DC):
                bk, off = FMROT[m % 4]
                key = ("PB" if bk is PB else "PC", off // 512)
                bank = bk[:, off:off + 512]
                use_bank(key)
                for k in range(DC):
                    mm(bank, W1b(k, m), sax[:, 512 * k: 512 * k + 512],
                       start=(k == 0), stop=(k == DC - 1), sem=(k == DC - 1))
                w("sc", "t", Cn["t"])
                w("sc", "t", t_sdone)
                if tok_dbg_q2:
                    wdma("sc", "s", tok_dbg_q2)
                if tok_dbg_qks:
                    wdma("sc", "s", tok_dbg_qks)
                w("sc", "v", t_k2done)
                tv = op("sc", lambda e, m=m, w_=w_, bank=bank: e.activation(
                    SB[:, 2048 * m + 512 * w_: 2048 * m + 512 * w_ + 512], bank,
                    AF.Identity, bias=VECS[:, 3 * 8 + m: 3 * 8 + m + 1]))
                free_bank(key, "sc", tv)
            t_v2w = Cn["sc"]
            w("t", "v", t_qksr)
            for c in range(DC):
                key = ("PA", c % 2)
                bank = PA[:, 0:512] if c % 2 == 0 else PA[:, 512:1024]
                use_bank(key)
                mm(bank, SEL[0:16, 128 * c:128 * (c + 1)],
                   QKSRT[:, 512 * w_:512 * (w_ + 1)], True, True, sem=True)
                w("v", "t", Cn["t"])
                w("v", "sc", t_v2w)
                tv = op("v", lambda e, c=c, w_=w_, bank=bank: e.tensor_mul(
                    SB[:, 2048 * c + 512 * w_: 2048 * c + 512 * w_ + 512],
                    SB[:, 2048 * c + 512 * w_: 2048 * c + 512 * w_ + 512], bank))
                free_bank(key, "v", tv)
        t_v2 = Cn["sc"]
        t_vchain = Cn["t"]
        tok_wo = wload(W0, "Wo", gate=("t", t_wv_done))
        t_v2s = Cn["v"]
        t_bcast_t = Cn["t"]
        tok_dbg_v2 = None
        if "v2" in dbg:
            w("s", "v", t_v2s)
            tok_dbg_v2 = dma("s", dbg["v2"][:, :], SB[:, :])
        cut("v2s")

        w("g", "g", t_cc)
        w("g", "t", t_bcast_t)
        dma("g", STOT[:, :], cc_out[:, :])
        tok_stot = dma("g", STOT64[:, :], cc_out[:, :])
        if "Stot" in dbg:
            wdma("s", "g", tok_stot)
            dma("s", dbg["Stot"][:, :], STOT[:, :])

        wdma("t", "g", tok_stot)
        w("t", "v", t_v2s)
        aout_sc = {}
        aout_v = {}
        wo_end = {}
        y_tok = {}

        def out_units(nn):
            sax = SAa if nn % 2 == 0 else SAb
            for c in range(DC):
                for hh in range(2):
                    h = 2 * c + hh
                    j = c * 2 + hh
                    bk, off = FMROT[j % 4]
                    key = ("PB" if bk is PB else "PC", off // 512)
                    bank = bk[0:64, off:off + 512]
                    use_bank(key)
                    ST = STOT if hh == 0 else STOT64
                    mm(bank, ST[:, 64 * h:64 * (h + 1)],
                       SB[64 * hh:64 * (hh + 1),
                          2048 * c + 512 * nn: 2048 * c + 512 * nn + 512],
                       True, True, sem=True)
                    eng2 = "sc" if j % 2 == 0 else "v"
                    w(eng2, "t", Cn["t"])
                    if nn >= 2:
                        w(eng2, "t", wo_end[nn - 2])
                    if eng2 == "sc":
                        tv = op("sc", lambda e, c=c, hh=hh, bank=bank, sax=sax:
                                e.copy(
                                    sax[64 * hh:64 * (hh + 1),
                                        512 * c:512 * (c + 1)], bank))
                    else:
                        tv = op("v", lambda e, c=c, hh=hh, bank=bank, sax=sax:
                                e.tensor_copy(
                                    sax[64 * hh:64 * (hh + 1),
                                        512 * c:512 * (c + 1)], bank))
                    free_bank(key, eng2, tv)
            aout_sc[nn] = Cn["sc"]
            aout_v[nn] = Cn["v"]

        def wo_units(nn):
            sax = SAa if nn % 2 == 0 else SAb
            wdma("t", "g", tok_wo)
            w("t", "sc", aout_sc[nn])
            w("t", "v", aout_v[nn])
            for m in range(DC):
                bk, off = FMROT[m % 4]
                key = ("PB" if bk is PB else "PC", off // 512)
                bank = bk[:, off:off + 512]
                tslot = POOL[:, 1024 + 512 * (m % 2):1024 + 512 * (m % 2) + 512]
                use_bank(key)
                for k in range(DC):
                    mm(bank, W0b(k, m), sax[:, 512 * k: 512 * k + 512],
                       start=(k == 0), stop=(k == DC - 1), sem=(k == DC - 1))
                w("sc", "t", Cn["t"])
                wdma("sc", "g", tok_ccin)
                w("sc", "v", Cn["v"])
                op("sc", lambda e, bank=bank, tslot=tslot, m=m: e.activation(
                    tslot, bank, AF.Identity,
                    bias=VECS[:, 4 * 8 + m: 4 * 8 + m + 1]))
                t_bo = Cn["sc"]
                free_bank(key, "sc", t_bo)
                w("v", "sc", t_bo)
                w("v", "t", t_sdone)
                if tok_dbg_k2:
                    wdma("v", "s", tok_dbg_k2)
                op("v", lambda e, m=m, nn=nn, tslot=tslot: e.tensor_add(
                    SC[:, 2048 * m + 512 * nn: 2048 * m + 512 * nn + 512],
                    tslot,
                    SH[:, 2048 * m + 512 * nn: 2048 * m + 512 * nn + 512]))
            wo_end[nn] = Cn["t"]
            y_tok[nn] = Cn["v"]

        out_units(0)
        out_units(1)
        wo_units(0)
        out_units(2)
        wo_units(1)
        out_units(3)
        wo_units(2)
        wo_units(3)
        t_y = Cn["v"]
        t_p3t = Cn["t"]
        tok_dbg_y = None
        if "yT" in dbg:
            w("s", "v", t_y)
            tok_dbg_y = dma("s", dbg["yT"][:, :], SC[:, :])
        cut("p3")

        for w_ in range(NW):
            ys = lambda d, w_=w_: SC[:, 2048 * d + 512 * w_:
                                     2048 * d + 512 * w_ + 512]
            sqs = lambda d: SAa[:, 512 * d: 512 * d + 512]
            w("sc", "v", y_tok[w_])
            w("t", "v", y_tok[w_])
            t_rst = ln_stats(ys, EPS_IN, sqs)
            w("v", "sc", t_rst)
            for d in range(DC):
                op("v", lambda e, d=d, w_=w_: e.tensor_sub(
                    TMPE[:, 0:512],
                    SC[:, 2048 * d + 512 * w_: 2048 * d + 512 * w_ + 512],
                    MUT))
                op("v", lambda e: e.tensor_mul(TMPE[:, 0:512], TMPE[:, 0:512],
                                               RST))
                op("v", lambda e, d=d: e.tensor_scalar(
                    TMPE[:, 0:512], TMPE[:, 0:512], DERV[:, 48 + d:48 + d + 1],
                    DERV[:, 56 + d:56 + d + 1], op0=ALU.mult, op1=ALU.add))
                op("v", lambda e, d=d, w_=w_: e.tensor_add(
                    SX[:, 2048 * d + 512 * w_: 2048 * d + 512 * w_ + 512],
                    SX[:, 2048 * d + 512 * w_: 2048 * d + 512 * w_ + 512],
                    TMPE[:, 0:512]))
        t_x1 = Cn["v"]
        t_p4t = Cn["t"]
        tok_dbg_x1 = None
        if "x1T" in dbg:
            w("s", "v", t_x1)
            tok_dbg_x1 = dma("s", dbg["x1T"][:, :], SX[:, :])
        cut("p4")

        wi_tok = []
        for kc in range(8):
            if kc < 4:
                dstap = SB[:, 4096 * kc:4096 * (kc + 1)]
                gate = ("t", t_p3t)
                wdma("g", "s", tok_dbg_v2)
            elif kc < 6:
                dstap = W1[:, 4096 * (kc - 4):4096 * (kc - 4) + 4096]
                gate = ("t", t_vchain)
            elif kc == 6:
                dstap = SAa[:, :]
                gate = ("t", t_p4t)
            else:
                dstap = SAb[:, :]
                gate = ("t", t_p4t)
            w("g", gate[0], gate[1])
            wi_tok.append(dma("g", dstap, wiT_in[128 * kc:128 * (kc + 1), :]))
        wo_tok = []
        for fc in range(FC):
            if fc < 16:
                dstap = SH[:, 1024 * fc:1024 * (fc + 1)]
                w("g", "v", t_y)
                wdma("g", "s", tok_dbg_h)
            else:
                dstap = SC[:, 1024 * (fc - 16):1024 * (fc - 16) + 1024]
                w("g", "t", t_p4t)
                w("g", "v", t_x1)
                wdma("g", "s", tok_dbg_y)
            wo_tok.append(dma("g", dstap, woT_in[128 * fc:128 * (fc + 1), :]))

        h2_tok = {}
        u_tok = {}
        aend_tok = {0: None}
        interC_tok = {}

        def ln2_stage(tc, t_aend_prev):
            xs2 = lambda d: SX[:, 2048 * d + 256 * tc: 2048 * d + 256 * tc + 256]
            use_bank(("PD",))
            w("t", "v", t_x1)
            wdma("v", "s", tok_dbg_x1)
            for d in range(DC):
                mm(PD[:, 0:256], Jc, xs2(d), start=(d == 0), stop=(d == DC - 1),
                   sem=(d == DC - 1))
            t_mu_mm = Cn["t"]
            use_bank(("PB", 0))
            w("sc", "t", Cn["t"])
            for d in range(DC):
                slot = XSQ[:, 256 * (d % 2):256 * (d % 2) + 256]
                if d >= 2:
                    w("sc", "t", Cn["t"])
                op("sc", lambda e, d=d, slot=slot: e.activation(
                    slot, xs2(d), AF.Square))
                w("t", "sc", Cn["sc"])
                mm(PB[:, 0:256], Jc, slot, start=(d == 0), stop=(d == DC - 1),
                   sem=True)
            t_ss_mm = Cn["t"]
            w("v", "t", t_mu_mm)
            op("v", lambda e: e.tensor_copy(MUT2, PD[:, 0:256]))
            t_mucp = op("v", lambda e: e.tensor_copy(STF[:, 0:256], PD[:, 0:256]))
            free_bank(("PD",), "v", t_mucp)
            op("v", lambda e: e.tensor_mul(STF[:, 256:512], STF[:, 0:256],
                                           STF[:, 0:256]))
            w("v", "t", t_ss_mm)
            t_var = op("v", lambda e: e.tensor_sub(STF[:, 0:256], PB[:, 0:256],
                                                   STF[:, 256:512]))
            free_bank(("PB", 0), "v", t_var)
            w("sc", "v", t_var)
            op("sc", lambda e: e.activation(STF[:, 256:512], STF[:, 0:256],
                                            AF.Ln, bias=float(EPS_OUT)))
            t_rst2 = op("sc", lambda e: e.activation(RST2, STF[:, 256:512],
                                                     AF.Exp, scale=-0.5))
            w("v", "sc", t_rst2)
            if t_aend_prev:
                w("v", "t", t_aend_prev)
            for d in range(DC):
                op("v", lambda e, d=d: e.tensor_sub(
                    TMP6[:, 0:256], xs2(d), MUT2))
                op("v", lambda e: e.tensor_mul(TMP6[:, 0:256], TMP6[:, 0:256],
                                               RST2))
                op("v", lambda e, d=d: e.tensor_scalar(
                    H2TC[:, 256 * d:256 * (d + 1)], TMP6[:, 0:256],
                    DERV[:, 16 + d:16 + d + 1], DERV[:, 24 + d:24 + d + 1],
                    op0=ALU.mult, op1=ALU.add))
            h2_tok[tc] = Cn["v"]
            if tc == 0 and "h2tc0" in dbg:
                w("s", "v", h2_tok[tc])
                dma("s", dbg["h2tc0"][:, :], POOL[:, 0:2048])

        def u_stage(tc):
            w("t", "v", h2_tok[tc])
            use_bank(("PD",))
            for k in range(DC):
                mm(PD[0:32, 0:256], ASEL[:, 32 * k:32 * (k + 1)],
                   H2TC[:, 256 * k:256 * (k + 1)], start=(k == 0),
                   stop=(k == DC - 1), sem=(k == DC - 1))
            w("v", "t", Cn["t"])
            op("v", lambda e: e.tensor_copy(UB[0:32, 0:256], PD[0:32, 0:256]))
            op("v", lambda e: e.tensor_copy(UB[0:32, 256:512], PD[0:32, 0:256]))
            t_u01 = op("v", lambda e: e.tensor_scalar(
                UB[0:16, 256:512], PD[0:16, 0:256], -1.0, None, op0=ALU.mult))
            free_bank(("PD",), "v", t_u01)
            u_tok[tc] = t_u01
            if tc == 0 and "u01" in dbg:
                w("s", "v", t_u01)
                dma("s", dbg["u01"][:, :], UB[:, :])

        AROT = [(PA, 0, ("PA", 0)), (PA, 512, ("PA", 1)), (PD, 0, ("PD",)),
                (PB, 0, ("PB", 0)), (PB, 512, ("PB", 1)), (PC, 0, ("PC", 0))]

        def a_stage(tc, t_p3t):
            w("t", "v", u_tok[tc])
            for kc in range(8):
                wdma("t", "g", wi_tok[kc])

            def a_head(fc):
                bk, off, key = AROT[fc % 6]
                bank = bk[:, off:off + 256]
                r0s = R0P[:, 256 * (fc % 3):256 * (fc % 3) + 256]
                use_bank(key)
                mm(bank, ARENA[0:16, 128 * fc:128 * fc + 128], UB[0:16, 0:256],
                   start=True, stop=False)
                for k in range(DC):
                    mm(bank, wiTblk(k, fc), H2TC[:, 256 * k:256 * (k + 1)],
                       start=False, stop=False, sem=(k == DC - 1))
                t_head = Cn["t"]
                w("sc", "t", t_head)
                w("sc", "v", Cn["v"])
                op("sc", lambda e, bank=bank, r0s=r0s: e.activation(
                    r0s, bank, AF.Relu, scale=TKB[:, 0:1]))
                return Cn["sc"]

            def a_tail(fc, t_ev0):
                bk, off, key = AROT[fc % 6]
                bank = bk[:, off:off + 256]
                r0s = R0P[:, 256 * (fc % 3):256 * (fc % 3) + 256]
                r1s = R1P[:, 256 * (fc % 3):256 * (fc % 3) + 256]
                w("t", "sc", t_ev0)
                mm(bank, ARENA[0:32, 128 * fc:128 * fc + 128], UB[0:32, 256:512],
                   start=False, stop=True, sem=True)
                w("v", "t", Cn["t"])
                t_ev1 = op("v", lambda e, bank=bank, r1s=r1s: e.tensor_scalar(
                    r1s, bank, 0.0, TKB[:, 1:2], op0=ALU.max, op1=ALU.mult))
                free_bank(key, "v", t_ev1)
                if tc == 0:
                    w("v", "t", t_p3t)
                op("v", lambda e, fc=fc, r0s=r0s, r1s=r1s: e.tensor_add(
                    W0[:, 256 * fc:256 * (fc + 1)], r0s, r1s))

            pend = []
            for fc in range(FC):
                ev0 = a_head(fc)
                pend.append((fc, ev0))
                if len(pend) > 2:
                    a_tail(*pend.pop(0))
            aend_tok[tc + 1] = Cn["t"]
            for it in pend:
                a_tail(*it)
            interC_tok[tc] = Cn["v"]
            if tc == 0 and "interC0" in dbg:
                w("s", "v", interC_tok[tc])
                dma("s", dbg["interC0"][:, :], W0[:, :])
            if tc == 0:
                cut("aphase")

        def mm2_stage(tc):
            w("t", "v", interC_tok[tc])
            for fc in range(FC):
                wdma("t", "g", wo_tok[fc])
            dp_bank = [(PB, 0), (PB, 256), (PB, 512), (PB, 768),
                       (PC, 0), (PC, 256), (PC, 512), (PC, 768)]
            for dp in range(DC):
                bk, off = dp_bank[dp]
                key = ("PB", off // 512) if bk is PB else ("PC", off // 512)
                if dp % 2 == 0:
                    use_bank(key)
                for fc in range(FC):
                    mm(bk[:, off:off + 256], woTblk(fc, dp),
                       W0[:, 256 * fc:256 * (fc + 1)], start=(fc == 0),
                       stop=(fc == FC - 1), sem=(fc == FC - 1))
                if dp % 2 == 1:
                    t_dp = Cn["t"]
                    w("v", "t", t_dp)
                    for d2 in (dp - 1, dp):
                        bk2, off2 = dp_bank[d2]
                        op("v", lambda e, bk2=bk2, off2=off2, d2=d2:
                           e.tensor_scalar(
                               TMP6[:, 256:512], bk2[:, off2:off2 + 256],
                               DERV[:, 40 + d2:40 + d2 + 1], None, op0=ALU.mult))
                        tv = op("v", lambda e, d2=d2, tc=tc: e.tensor_add(
                            SX[:, 2048 * d2 + 256 * tc:
                               2048 * d2 + 256 * tc + 256],
                            SX[:, 2048 * d2 + 256 * tc:
                               2048 * d2 + 256 * tc + 256],
                            TMP6[:, 256:512]))
                        free_bank(key, "v", tv)
            t_x2 = Cn["v"]
            if tc == 0:
                cut("tc0")
            w("g", "v", t_x2)
            dma("g",
                out_xT.rearrange("(d p) t -> p d t", p=128)[:, :,
                                                            256 * tc:256 * (tc + 1)],
                SX.ap().rearrange("p (d t) -> p d t", d=DC)[:, :,
                                                            256 * tc:256 * (tc + 1)])

        ln2_stage(0, None)
        if True:
            cut("ln2s")
        u_stage(0)
        cut("u")
        for tc in range(TCN):
            a_stage(tc, t_p3t)
            if tc + 1 < TCN:
                ln2_stage(tc + 1, aend_tok[tc + 1])
            mm2_stage(tc)
            if tc + 1 < TCN:
                u_stage(tc + 1)

        w("g", "dg", D16["g"])
        w("s", "ds", D16["s"])

    with nc.allow_low_precision("bf16 recipe calibrated by problem tolerance"), \
            nc.Block() as blk:
        def run(name, e):
            try:
                emit(name, e)
            except _Cut:
                pass

        @blk.tensor
        def _(e):
            run("t", e)

        @blk.vector
        def _(e):
            run("v", e)

        @blk.scalar
        def _(e):
            run("sc", e)

        @blk.gpsimd
        def _(e):
            run("g", e)

        @blk.sync
        def _(e):
            run("s", e)

    return nc, dbg


_NC_CACHE = {}


def _get_nc(debug=(), stop_after=None):
    key = (tuple(sorted(debug)), stop_after)
    if key not in _NC_CACHE:
        _NC_CACHE[key] = build_nc(debug=key[0], stop_after=stop_after)
    return _NC_CACHE[key]


def _host_inputs(inputs, core):
    b = core // 2
    x = np.asarray(inputs["L0_attn_x"])
    xs = x[b, 2048 * (core % 2):2048 * (core % 2 + 1), :]
    m = {}
    m["xT"] = np.ascontiguousarray(xs.T).astype(np.float32)
    for w in ("Wq", "Wk", "Wv", "Wq2", "Wk2", "Wv2", "Wo"):
        m[w] = np.ascontiguousarray(np.asarray(inputs["L0_attn_" + w],
                                               dtype=np.float32))
    vecs = [inputs["L0_attn_bq"], inputs["L0_attn_bk"], inputs["L0_attn_bv"],
            inputs["L0_attn_bv2"], inputs["L0_attn_bo"],
            inputs["L0_attn_gamma"], inputs["L0_attn_beta"]]
    m["vecs"] = np.ascontiguousarray(np.concatenate(
        [np.asarray(v, dtype=np.float32).reshape(8, 128).T for v in vecs],
        axis=1))
    m["brow"] = np.concatenate(
        [np.asarray(inputs["L0_attn_bq2"], dtype=np.float32).reshape(1, -1),
         np.asarray(inputs["L0_attn_bk2"], dtype=np.float32).reshape(1, -1)],
        axis=1)
    m["cT"] = np.ascontiguousarray(
        np.asarray(inputs["L0_c"], dtype=np.float32)[b].reshape(8, 128).T)
    m["wmod"] = np.ascontiguousarray(np.asarray(inputs["L0_wmod"],
                                                dtype=np.float32))
    m["wiT"] = np.ascontiguousarray(
        np.asarray(inputs["L0_ffn_wi_w"], dtype=np.float32).T)
    m["woT"] = np.ascontiguousarray(
        np.asarray(inputs["L0_ffn_wo_w"], dtype=np.float32).T)
    m["Aall"] = np.ascontiguousarray(
        np.asarray(inputs["L0_ffn_lora_As"], dtype=np.float32).reshape(128, 1024))
    m["BallT"] = np.ascontiguousarray(
        np.asarray(inputs["L0_ffn_lora_Bs"],
                   dtype=np.float32).transpose(0, 2, 1).reshape(128, 4096))
    m["idx"] = np.asarray(inputs["L0_ffn_top_k_indices"]).reshape(1, 2).astype(
        np.int32)
    m["tkv"] = np.asarray(inputs["L0_ffn_top_k_values"],
                          dtype=np.float32).reshape(1, 2)
    constB = np.zeros((128, 272), dtype=np.float32)
    constB[:, 0:128] = 1.0 / 1024.0
    constB[:, 128:256] = np.eye(128, dtype=np.float32)
    constB[:, 256:272] = np.tile(np.eye(16, dtype=np.float32), (8, 1))
    m["constB"] = constB
    constF = np.zeros((128, 8), dtype=np.float32)
    constF[:, 0] = np.arange(128) // 16
    constF[:, 1] = -0.5
    constF[:, 2] = 1e-5
    constF[:, 3] = 1e-3
    constF[:, 4] = 0.0
    constF[:, 5] = 1.0
    m["constF"] = constF
    sel = np.zeros((16, 1024), dtype=np.float32)
    for h in range(16):
        c, hh = h // 2, h % 2
        sel[h, 128 * c + 64 * hh:128 * c + 64 * hh + 64] = 1.0
    m["sel"] = sel
    m["ones_in"] = np.ones((1, 128), dtype=np.float32)
    return m


def kernel(**inputs):
    nc, dbg = _get_nc()
    in_maps = [_host_inputs(inputs, core) for core in range(8)]
    res = run_bass_kernel_spmd(nc, in_maps, core_ids=list(range(8)))
    outs = [np.asarray(res.results[c]["out_xT"]).T for c in range(8)]
    return np.concatenate(outs, axis=0).reshape(4, 4096, 1024).astype(np.float32)


# version 12
# speedup vs baseline: 1.5348x; 1.0073x over previous
import numpy as np
import concourse.bass as bass
import concourse.mybir as mybir
from concourse.bass_utils import run_bass_kernel_spmd

F32 = mybir.dt.float32
BF16 = mybir.dt.bfloat16
I32 = mybir.dt.int32
AF = mybir.ActivationFunctionType
ALU = mybir.AluOpType
AX = mybir.AxisListType

T = 2048
D = 1024
DC = 8
NW = 4
NT = 16
H = 16
DH = 64
FF = 4096
FC = 32
TCN = 8
EPS_IN = 1e-3
EPS_OUT = 1e-5
EPS_QK = 1e-6


class _Cut(Exception):
    pass


def build_nc(debug=(), stop_after=None):
    nc = bass.Bass("TRN2", target_bir_lowering=False)
    dbgn = set(debug)

    def din(name, shape, dtype=F32):
        return nc.dram_tensor(name, list(shape), dtype, kind="ExternalInput").ap()

    xT = din("xT", [D, T])
    WTS = {w: din(w, [D, D]) for w in ("Wq", "Wk", "Wv", "Wq2", "Wk2", "Wv2", "Wo")}
    vecs_in = din("vecs", [128, 7 * DC])
    brow_in = din("brow", [1, 2 * D])
    cT_in = din("cT", [128, DC])
    wmod_in = din("wmod", [D, 6 * D])
    wiT_in = din("wiT", [D, FF])
    woT_in = din("woT", [FF, D])
    Aall_in = din("Aall", [128, D])
    BallT_in = din("BallT", [128, FF])
    idx_in = din("idx", [1, 2], I32)
    tkv_in = din("tkv", [1, 2])
    constB_in = din("constB", [128, 272])
    constF_in = din("constF", [128, 8])
    sel_in = din("sel", [16, 1024])
    ones_in = din("ones_in", [1, 128])

    out_xT = nc.dram_tensor("out_xT", [D, T], F32, kind="ExternalOutput").ap()
    cc_in = nc.dram_tensor("cc_in", [64, H * DH], BF16, kind="Internal").ap()
    cc_out = nc.dram_tensor("cc_out", [64, H * DH], BF16, kind="Internal").ap()

    dbg = {}
    def dbgt(name, shape, dtype=F32):
        if name in dbgn:
            dbg[name] = nc.dram_tensor("dbg_" + name, list(shape), dtype,
                                       kind="ExternalOutput").ap()
    dbgt("modT", [128, 48]); dbgt("hT", [128, 16384], BF16)
    dbgt("q2", [128, 16384], BF16); dbgt("k2", [128, 16384], BF16)
    dbgt("Stot", [64, 1024], BF16); dbgt("qks", [128, 256])
    dbgt("v2", [128, 16384], BF16); dbgt("yT", [128, 16384], BF16)
    dbgt("x1T", [128, 16384], BF16); dbgt("h2tc0", [128, 2048], BF16)
    dbgt("interC0", [128, 8192], BF16); dbgt("BselT", [128, 4096], BF16)
    dbgt("u01", [64, 512], BF16); dbgt("OH", [128, 32], BF16)
    dbgt("qksrT", [128, 2048], BF16); dbgt("ASEL", [128, 256], BF16)
    dbgt("recip", [128, 256], BF16); dbgt("qks2", [128, 256])
    dbgt("DERV", [128, 64])

    A = nc.alloc_sbuf_tensor
    SX = A("SX", [128, 16384], BF16)
    SH = A("SH", [128, 16384], BF16)
    SB = A("SB", [128, 16384], BF16)
    SC = A("SC", [128, 16384], BF16)
    SAa = A("SAa", [128, 4096], BF16)
    SAb = A("SAb", [128, 4096], BF16)
    W0 = A("W0", [128, 8192], BF16)
    W1 = A("W1", [128, 8192], BF16)
    POOL = A("POOL", [128, 5120], BF16)
    CONST = A("CONSTB", [128, 272], BF16)
    P16T = A("P16T", [128, 8], F32)
    VECS = A("VECS", [128, 7 * DC], F32)
    DERV = A("DERV", [128, 8 * DC], F32)
    MODT = A("MODT", [128, 48], F32)
    CT = A("CT", [128, DC], F32)
    SIG = A("SIG", [128, DC], F32)
    SILU = A("SILU", [128, DC], BF16)
    ARENA = A("ARENA", [128, 4096], BF16)
    SEL = A("SEL", [16, 1024], BF16)
    UB = A("UB", [64, 512], BF16)
    BROW = A("BROW", [1, D], BF16)
    STF = A("STF", [128, 1024], F32)
    ASEL = A("ASEL", [128, 256], BF16)
    TKB = A("TKB", [128, 2], F32)
    IDXS = A("IDXS", [1, 2], I32)
    IDXF = A("IDXF", [1, 2], F32)
    IDXB = A("IDXB", [1, 2], BF16)
    OHE = A("OHE", [128, 2], F32)
    OH = A("OH", [128, 32], BF16)
    TKS = A("TKS", [1, 2], BF16)
    ONES1 = A("ONES1", [1, 128], BF16)

    MUT = POOL[:, 0:512]
    RST = POOL[:, 512:1024]
    TMPE = POOL[:, 1024:2048]
    TMPF = POOL[:, 2048:3072]
    QKSRT = POOL[0:16, 3072:5120]
    SPART = POOL[0:64, 1024:2048]
    ROW0 = POOL[0:1, 3072:3584]
    ROW1 = POOL[0:1, 3584:4096]
    H2TC = POOL[:, 0:2048]
    R0P = POOL[:, 2048:2816]
    R1P = POOL[:, 2816:3584]
    MUT2 = POOL[:, 3584:3840]
    RST2 = POOL[:, 3840:4096]
    TMP6 = POOL[:, 4096:4608]
    XSQ = POOL[:, 4608:5120]
    QKS = STF[:, 768:1024]
    STOT = POOL[0:64, 3072:4096]
    STOT64 = POOL[64:128, 3072:4096]

    Jc = CONST[:, 0:128]
    I128 = CONST[:, 128:256]
    I16t = CONST[:, 256:272]

    PA = nc.alloc_psum_tensor("PA", [128, 1024], F32)
    PB = nc.alloc_psum_tensor("PB", [128, 1024], F32)
    PC = nc.alloc_psum_tensor("PC", [128, 1024], F32)
    PD = nc.alloc_psum_tensor("PD", [128, 512], F32)
    PTB = nc.alloc_psum_tensor("PTB", [128, 1024], BF16)

    sems = {}
    for name in ("t", "v", "sc", "g", "s", "dg", "ds"):
        sems[name] = nc.alloc_semaphore("sem_" + name)

    for col, val in ((1, -0.5), (2, 1e-5), (3, 1e-3), (4, 0.0), (5, 1.0)):
        nc.const_aps.register(F32, float(val), P16T[:, col:col + 1])

    def W0b(k, m):
        return W0[:, 1024 * k + 128 * m: 1024 * k + 128 * m + 128]

    def W1b(k, m):
        return W1[:, 1024 * k + 128 * m: 1024 * k + 128 * m + 128]

    def wiTblk(k, fc):
        if k < 4:
            return SB[:, 4096 * k + 128 * fc: 4096 * k + 128 * fc + 128]
        if k < 6:
            return W1[:, 4096 * (k - 4) + 128 * fc: 4096 * (k - 4) + 128 * fc + 128]
        if k == 6:
            return SAa[:, 128 * fc: 128 * fc + 128]
        return SAb[:, 128 * fc: 128 * fc + 128]

    def woTblk(fc, dp):
        if fc < 16:
            return SH[:, 1024 * fc + 128 * dp: 1024 * fc + 128 * dp + 128]
        return SC[:, 1024 * (fc - 16) + 128 * dp: 1024 * (fc - 16) + 128 * dp + 128]

    def wmod_rhs(k, n):
        if k < 4:
            base = [SB, SB, SC, SC][k]
            off = (k % 2) * 6144
            return base[:, off + 512 * n: off + 512 * n + 512]
        if k == 4:
            if n < 4:
                return SC[:, 14336 + 512 * n: 14336 + 512 * n + 512]
            if n < 8:
                return SAb[:, 2048 + 512 * (n - 4): 2048 + 512 * (n - 4) + 512]
            return POOL[:, 1024 + 512 * (n - 8): 1024 + 512 * (n - 8) + 512]
        if k == 5:
            return W1[:, 512 * n: 512 * n + 512]
        if k == 6:
            if n < 8:
                return SAa[:, 512 * n: 512 * n + 512]
            return SAb[:, 512 * (n - 8): 512 * (n - 8) + 512]
        if n < 8:
            return SB[:, 12288 + 512 * n: 12288 + 512 * n + 512]
        return SC[:, 12288 + 512 * (n - 8): 12288 + 512 * (n - 8) + 512]

    def emit(my, e):
        def cut(name):
            if stop_after == name:
                raise _Cut()
        Cn = {k: 0 for k in ("t", "v", "sc", "g", "s")}
        D16 = {"g": 0, "s": 0}
        lastw = {}
        bankfree = {}

        def op(eng2, builder, sem=True):
            if my == eng2:
                ins = builder(e)
                if sem:
                    ins.then_inc(sems[eng2], 1)
            if sem:
                Cn[eng2] += 1
            return Cn[eng2]

        def w(eng2, sem_name, val):
            if my != eng2 or val is None or val <= 0:
                return
            if lastw.get(sem_name, -1) >= val:
                return
            lastw[sem_name] = val
            e.wait_ge(sems[sem_name], val)

        def dma(q, out, in_, **kw):
            D16[q] += 16
            if my == q:
                e.dma_start(out=out, in_=in_, **kw).then_inc(
                    sems["dg" if q == "g" else "ds"], 16)
            return D16[q]

        def wdma(eng2, q, tok):
            if tok:
                w(eng2, "dg" if q == "g" else "ds", tok)

        def fence(eng2):
            w(eng2, eng2, Cn[eng2])

        def mm(out, lhsT, rhs, start, stop, sem=False):
            return op("t", lambda e: e.matmul(out, lhsT, rhs, start=start,
                                              stop=stop), sem=sem)

        def use_bank(*keys):
            for k in keys:
                if k in bankfree:
                    eng2, tok = bankfree[k]
                    w("t", eng2, tok)

        def free_bank(key, eng2, tok):
            bankfree[key] = (eng2, tok)

        tok_p16 = dma("s", P16T[:, :], constF_in[:, :])
        tok_vecs = dma("s", VECS[:, :], vecs_in[:, :])
        tok_ct = dma("s", CT[:, :], cT_in[:, :])
        tok_idx = dma("s", IDXS[:, :], idx_in[:, :])

        tok_consts = dma("g", CONST[:, :], constB_in[:, :])
        tok_ones1 = dma("g", ONES1[:, :], ones_in[:, :])
        tok_sel = dma("g", SEL[:, :], sel_in[:, :])
        tok_brow = dma("g", BROW[:, :], brow_in[0:1, 0:D])
        tok_x = dma("g", SX.ap().rearrange("p (d t) -> p d t", d=DC),
                    xT.rearrange("(d p) t -> p d t", p=128))
        wmod_tok = []
        for k in range(4):
            dsts = [SB, SB, SC, SC][k]
            off = (k % 2) * 6144
            wmod_tok.append(dma("g", dsts[:, off:off + 6144],
                                wmod_in[128 * k:128 * (k + 1), :]))
        wmod_tok.append(dma("g", SC[:, 14336:16384], wmod_in[512:640, 0:2048]))
        wmod_tok.append(dma("g", SAb[:, 2048:4096], wmod_in[512:640, 2048:4096]))
        wmod_tok.append(dma("g", POOL[:, 1024:3072], wmod_in[512:640, 4096:6144]))
        wmod_tok.append(dma("g", W1[:, 0:6144], wmod_in[640:768, :]))
        wmod_tok.append(dma("g", SAa[:, 0:4096], wmod_in[768:896, 0:4096]))
        wmod_tok.append(dma("g", SAb[:, 0:2048], wmod_in[768:896, 4096:6144]))
        wmod_tok.append(dma("g", SB[:, 12288:16384], wmod_in[896:1024, 0:4096]))
        wmod_tok.append(dma("g", SC[:, 12288:14336], wmod_in[896:1024, 4096:6144]))
        tok_aall = dma("g", W1[:, 6144:7168], Aall_in[:, :])
        tok_ball = dma("g", ARENA[:, :], BallT_in[:, :])
        tok_tkf = dma("g", TKS[:, :], tkv_in[:, :])

        def wload(buf, name, gate=None):
            if gate is not None:
                w("g", gate[0], gate[1])
            return dma("g", buf.ap().rearrange("p (k m j) -> p k m j", k=8, j=128),
                       WTS[name].rearrange("(k p) (m j) -> p k m j", p=128, j=128))

        tok_wq = wload(W0, "Wq")

        wdma("t", "g", tok_consts)
        wdma("t", "g", tok_x)

        def ln_stats(src_chunk_ap, eps, sq_scratch):
            use_bank(("PA", 0))
            for d in range(DC):
                mm(PA[:, 0:512], Jc, src_chunk_ap(d), start=(d == 0),
                   stop=(d == DC - 1), sem=(d == DC - 1))
            t_mu_mm = Cn["t"]
            w("sc", "t", Cn["t"])
            for d in range(DC):
                op("sc", lambda e, d=d: e.activation(
                    sq_scratch(d), src_chunk_ap(d), AF.Square))
            t_sq = Cn["sc"]
            use_bank(("PA", 1))
            w("t", "sc", t_sq)
            for d in range(DC):
                mm(PA[:, 512:1024], Jc, sq_scratch(d), start=(d == 0),
                   stop=(d == DC - 1), sem=(d == DC - 1))
            t_ss_mm = Cn["t"]
            w("v", "t", t_mu_mm)
            op("v", lambda e: e.tensor_copy(MUT, PA[:, 0:512]))
            t_mucp = op("v", lambda e: e.tensor_copy(STF[:, 0:512], PA[:, 0:512]))
            free_bank(("PA", 0), "v", t_mucp)
            op("v", lambda e: e.tensor_mul(STF[:, 512:1024], STF[:, 0:512],
                                           STF[:, 0:512]))
            w("v", "t", t_ss_mm)
            t_var = op("v", lambda e: e.tensor_sub(STF[:, 0:512], PA[:, 512:1024],
                                                   STF[:, 512:1024]))
            free_bank(("PA", 1), "v", t_var)
            w("sc", "v", t_var)
            wdma("sc", "s", tok_p16)
            op("sc", lambda e, eps=eps: e.activation(STF[:, 512:1024],
                                                     STF[:, 0:512], AF.Ln,
                                                     bias=float(eps)))
            t_rst = op("sc", lambda e: e.activation(RST, STF[:, 512:1024],
                                                    AF.Exp, scale=-0.5))
            return t_rst

        for w_ in range(NW):
            xs = lambda d, w_=w_: SX[:, 2048 * d + 512 * w_:
                                     2048 * d + 512 * w_ + 512]
            sqs = lambda d, w_=w_: SH[:, 2048 * d + 512 * w_:
                                      2048 * d + 512 * w_ + 512]
            t_rst = ln_stats(xs, EPS_OUT, sqs)
            w("v", "sc", t_rst)
            for d in range(DC):
                hs = SH[:, 2048 * d + 512 * w_: 2048 * d + 512 * w_ + 512]
                op("v", lambda e, d=d, w_=w_, hs=hs: e.tensor_sub(
                    hs, SX[:, 2048 * d + 512 * w_: 2048 * d + 512 * w_ + 512],
                    MUT))
                op("v", lambda e, hs=hs: e.tensor_mul(hs, hs, RST))
        t_hn = Cn["v"]

        wdma("sc", "s", tok_ct)
        t_sig = op("sc", lambda e: e.activation(SIG[:, :], CT[:, :], AF.Sigmoid))
        w("v", "sc", t_sig)
        t_silu = op("v", lambda e: e.tensor_mul(SILU[:, :], CT[:, :], SIG[:, :]))

        rowfree = [0, 0]
        wdma("t", "g", tok_consts)
        for n in range(12):
            key = ("PA", n % 2)
            bank = PA[0:1, 0:512] if n % 2 == 0 else PA[0:1, 512:1024]
            rowslot = ROW0 if n % 2 == 0 else ROW1
            wdma("t", "g", wmod_tok[11])
            w("t", "v", t_silu)
            use_bank(key)
            for k in range(8):
                mm(bank, SILU[:, k:k + 1], wmod_rhs(k, n),
                   start=(k == 0), stop=(k == 7), sem=(k == 7))
            w("v", "t", Cn["t"])
            w("v", "t", rowfree[n % 2])
            tv = op("v", lambda e, bank=bank, rowslot=rowslot: e.tensor_copy(
                rowslot, bank))
            free_bank(key, "v", tv)
            w("t", "v", tv)
            use_bank(("PTB7",))
            for c in range(4):
                op("t", lambda e, c=c, n=n, rowslot=rowslot: e.transpose(
                    PTB[:, 8 * (n % 2) + 2 * c: 8 * (n % 2) + 2 * c + 1],
                    rowslot[0:1, 128 * c:128 * c + 128], I128[0:1, 0:1]),
                    sem=(c == 3))
            rowfree[n % 2] = Cn["t"]
            w("v", "t", Cn["t"])
            vv, cc4 = n // 2, 4 * (n % 2)
            ptb_view = PTB[:, 8 * (n % 2): 8 * (n % 2) + 8].rearrange(
                "p (c two) -> p c two", two=2)[:, :, 0:1]
            tvv = op("v", lambda e, vv=vv, cc4=cc4, ptb_view=ptb_view:
                     e.tensor_copy(
                         MODT[:, 8 * vv + cc4: 8 * vv + cc4 + 4], ptb_view))
            free_bank(("PTB7",), "v", tvv)
        t_modr = Cn["v"]

        wdma("v", "s", tok_vecs)
        fence("v")
        op("v", lambda e: e.tensor_scalar(DERV[:, 0:8], MODT[:, 8:16], 1.0, None,
                                          op0=ALU.add))
        op("v", lambda e: e.tensor_copy(DERV[:, 8:16], MODT[:, 0:8]))
        op("v", lambda e: e.tensor_scalar(DERV[:, 16:24], MODT[:, 32:40], 1.0, None,
                                          op0=ALU.add))
        op("v", lambda e: e.tensor_copy(DERV[:, 24:32], MODT[:, 24:32]))
        for (src0, dst) in ((16, 32), (40, 40)):
            op("v", lambda e, s=src0: e.tensor_scalar(
                STF[:, 8:16], MODT[:, s:s + 8], 0.0, None, op0=ALU.is_ge))
            fence("v")
            op("v", lambda e: e.tensor_scalar(
                STF[:, 8:16], STF[:, 8:16], 2.0, -1.0, op0=ALU.mult, op1=ALU.add))
            fence("v")
            op("v", lambda e, s=src0: e.tensor_mul(
                STF[:, 0:8], MODT[:, s:s + 8], STF[:, 8:16]))
            fence("v")
            op("v", lambda e: e.tensor_scalar(
                STF[:, 0:8], STF[:, 0:8], 0.25, None, op0=ALU.max))
            fence("v")
            op("v", lambda e, d=dst: e.tensor_mul(
                DERV[:, d:d + 8], STF[:, 0:8], STF[:, 8:16]))
            fence("v")
        op("v", lambda e: e.tensor_mul(DERV[:, 48:56], DERV[:, 32:40],
                                       VECS[:, 40:48]))
        t_derv = op("v", lambda e: e.tensor_mul(DERV[:, 56:64], DERV[:, 32:40],
                                                VECS[:, 48:56]))
        if "modT" in dbg:
            w("s", "v", t_derv)
            dma("s", dbg["modT"][:, :], MODT[:, :])
        if "DERV" in dbg:
            w("s", "v", t_derv)
            dma("s", dbg["DERV"][:, :], DERV[:, :])
        for d in range(DC):
            op("v", lambda e, d=d: e.tensor_scalar(
                SH[:, 2048 * d:2048 * (d + 1)], SH[:, 2048 * d:2048 * (d + 1)],
                DERV[:, d:d + 1], DERV[:, 8 + d:8 + d + 1],
                op0=ALU.mult, op1=ALU.add))
        t_h = Cn["v"]
        tok_dbg_h = None
        if "hT" in dbg:
            w("s", "v", t_h)
            tok_dbg_h = dma("s", dbg["hT"][:, :], SH[:, :])
        cut("p1")

        cut("p0a")

        wdma("v", "s", tok_idx)
        op("v", lambda e: e.tensor_copy(IDXF[:, :], IDXS[:, :]))
        fence("v")
        op("v", lambda e: e.tensor_scalar(IDXF[:, :], IDXF[:, :], 0.0, 7.0,
                                          op0=ALU.max, op1=ALU.min))
        fence("v")
        t_idxb = op("v", lambda e: e.tensor_copy(IDXB[:, :], IDXF[:, :]))
        wdma("t", "g", tok_ones1)
        w("t", "v", t_idxb)
        use_bank(("PD",))
        mm(PD[0:128, 0:2], ONES1[0:1, 0:128], IDXB[0:1, 0:2], True, True, sem=True)
        w("v", "t", Cn["t"])
        wdma("v", "s", tok_p16)
        t_ohe = op("v", lambda e: e.tensor_scalar(OHE[:, :], PD[0:128, 0:2],
                                                  P16T[:, 0:1], None,
                                                  op0=ALU.is_equal))
        t_ohz = op("g", lambda e: e.memset(OH[:, :], 0))
        wdma("v", "g", tok_consts)
        w("v", "g", t_ohz)
        fence("v")
        for k in range(2):
            op("v", lambda e, k=k: e.tensor_scalar(
                OH[:, 16 * k:16 * k + 16], I16t, OHE[:, k:k + 1], None,
                op0=ALU.mult))
        t_oh = Cn["v"]
        free_bank(("PD",), "v", t_ohe)
        wdma("t", "g", tok_tkf)
        use_bank(("PD",))
        mm(PD[0:128, 2:4], ONES1[0:1, 0:128], TKS[0:1, 0:2], True, True, sem=True)
        w("v", "t", Cn["t"])
        t_tkb = op("v", lambda e: e.tensor_copy(TKB[:, :], PD[0:128, 2:4]))
        free_bank(("PD",), "v", t_tkb)

        wdma("t", "g", tok_aall)
        w("t", "v", t_oh)
        for d in range(DC):
            key = ("PC", d % 2)
            bank = PC[:, 0:32] if d % 2 == 0 else PC[:, 512:544]
            use_bank(key)
            mm(bank, W1[:, 6144 + 128 * d:6144 + 128 * (d + 1)], OH[:, :],
               True, True, sem=True)
            w("v", "t", Cn["t"])
            tv = op("v", lambda e, d=d, bank=bank: e.tensor_copy(
                ASEL[:, 32 * d:32 * (d + 1)], bank))
            free_bank(key, "v", tv)
        wdma("t", "g", tok_ball)
        for n in range(8):
            use_bank(("PD",))
            mm(PD[0:32, 0:512], OH[:, :], ARENA[:, 512 * n:512 * (n + 1)],
               True, True, sem=True)
            w("v", "t", Cn["t"])
            tv = op("v", lambda e, n=n: e.tensor_copy(
                ARENA[0:32, 512 * n:512 * (n + 1)], PD[0:32, 0:512]))
            free_bank(("PD",), "v", tv)
        t_bsel = Cn["v"]
        t_p0b = Cn["t"]
        if "OH" in dbg:
            w("s", "v", t_bsel)
            dma("s", dbg["OH"][:, :], OH[:, :])
        if "BselT" in dbg:
            w("s", "v", t_bsel)
            dma("s", dbg["BselT"][:, :], ARENA[:, :])
        if "ASEL" in dbg:
            w("s", "v", t_bsel)
            dma("s", dbg["ASEL"][:, :], ASEL[:, :])
        cut("p0b")

        tok_wq2 = wload(W1, "Wq2", gate=("t", t_p0b))

        FMROT = [(PB, 0), (PB, 512), (PC, 0), (PC, 512)]

        def fm_proj(wblk, hoff_w, dstc, bias_col):
            for m in range(DC):
                bk, off = FMROT[m % 4]
                key = ("PB" if bk is PB else "PC", off // 512)
                bank = bk[:, off:off + 512]
                use_bank(key)
                for k in range(DC):
                    mm(bank, wblk(k, m),
                       SH[:, 2048 * k + hoff_w: 2048 * k + hoff_w + 512],
                       start=(k == 0), stop=(k == DC - 1), sem=(k == DC - 1))
                w("sc", "t", Cn["t"])
                tv = op("sc", lambda e, m=m, bank=bank: e.activation(
                    dstc(m), bank, AF.Identity,
                    bias=VECS[:, bias_col * 8 + m: bias_col * 8 + m + 1]))
                free_bank(key, "sc", tv)

        TPROT = [(PA, "PA"), (PB, "PB"), (PC, "PC")]

        def tok_proj(sax, dstsb, wbuf, base_tile, toks=None):
            for tt in range(4):
                i = base_tile + tt
                PT, pnm = TPROT[i % 3]
                use_bank((pnm, 0), (pnm, 1))
                for half in range(2):
                    bank = PT[:, 0:512] if half == 0 else PT[:, 512:1024]
                    for k in range(DC):
                        mm(bank,
                           sax[:, 512 * k + 128 * tt: 512 * k + 128 * tt + 128],
                           wbuf[:, 1024 * k + 512 * half:
                                1024 * k + 512 * half + 512],
                           start=(k == 0), stop=False)
                    mm(bank, ONES1[0:1, 0:128],
                       BROW[0:1, 512 * half: 512 * half + 512],
                       start=False, stop=True, sem=(half == 1))
                t_mm = Cn["t"]
                w("sc", "t", t_mm)
                op("sc", lambda e, i=i, PT=PT: e.activation(
                    dstsb[:, 1024 * i:1024 * (i + 1)], PT[:, :], AF.Relu))
                t_relu = Cn["sc"]
                w("v", "sc", t_relu)
                t_min = op("v", lambda e, PT=PT: e.tensor_scalar(
                    TMPE, PT[:, 0:1024], 0.0, None, op0=ALU.min))
                free_bank((pnm, 0), "v", t_min)
                free_bank((pnm, 1), "v", t_min)
                w("sc", "v", t_min)
                t_exp = op("sc", lambda e: e.activation(TMPE, TMPE, AF.Exp))
                w("v", "sc", t_exp)
                tadd = op("v", lambda e, i=i: e.tensor_add(
                    dstsb[:, 1024 * i:1024 * (i + 1)],
                    dstsb[:, 1024 * i:1024 * (i + 1)], TMPE))
                if toks is not None:
                    toks.append(tadd)

        wdma("t", "g", tok_wq)
        wdma("t", "g", tok_brow)
        w("t", "v", t_h)
        t_wq_done = None
        for w_ in range(NW):
            sax = SAa if w_ % 2 == 0 else SAb
            dstc = lambda m, sax=sax: sax[:, 512 * m: 512 * m + 512]
            fm_proj(W0b, 512 * w_, dstc, 0)
            if w_ == NW - 1:
                t_wq_done = Cn["t"]
            wdma("t", "g", tok_wq2)
            w("t", "sc", Cn["sc"])
            tok_proj(sax, SB, W1, 4 * w_)
        t_qchain = Cn["t"]
        t_q2v = Cn["v"]
        cut("qchain")
        tok_wk = wload(W0, "Wk", gate=("t", t_wq_done))
        tok_wk2 = wload(W1, "Wk2", gate=("t", t_qchain))
        w("g", "v", t_q2v)
        tok_brow2 = dma("g", BROW[:, :], brow_in[0:1, D:2 * D])

        elu_k = []
        t_wk_done = None
        for w_ in range(NW):
            sax = SAa if w_ % 2 == 0 else SAb
            wdma("t", "g", tok_wk)
            dstc = lambda m, sax=sax: sax[:, 512 * m: 512 * m + 512]
            fm_proj(W0b, 512 * w_, dstc, 1)
            if w_ == NW - 1:
                t_wk_done = Cn["t"]
            wdma("t", "g", tok_wk2)
            wdma("t", "g", tok_brow2)
            w("t", "sc", Cn["sc"])
            tok_proj(sax, SC, W1, 4 * w_, toks=elu_k)
            for tt in range(4):
                i = 4 * w_ + tt
                op("v", lambda e, i=i: e.tensor_mul(
                    TMPF, SB[:, 1024 * i:1024 * (i + 1)],
                    SC[:, 1024 * i:1024 * (i + 1)]))
                op("v", lambda e, i=i: e.tensor_reduce(
                    QKS[:, 16 * i:16 * (i + 1)],
                    TMPF.rearrange("p (h d) -> p h d", d=DH),
                    axis=AX.X, op=ALU.add))
        t_k2done = Cn["v"]
        t_kchain = Cn["t"]
        tok_wv = wload(W0, "Wv", gate=("t", t_wk_done))
        tok_wv2 = wload(W1, "Wv2", gate=("t", t_kchain))
        tok_dbg_q2 = tok_dbg_k2 = tok_dbg_qks = None
        if "q2" in dbg:
            w("s", "v", t_k2done)
            tok_dbg_q2 = dma("s", dbg["q2"][:, :], SB[:, :])
        if "k2" in dbg:
            w("s", "v", t_k2done)
            tok_dbg_k2 = dma("s", dbg["k2"][:, :], SC[:, :])
        if "qks" in dbg:
            w("s", "v", t_k2done)
            tok_dbg_qks = dma("s", dbg["qks"][:, :], QKS[:, :])
        cut("kchain")

        for i in range(NT):
            op("v", lambda e, i=i: e.tensor_scalar(
                STF[:, 16 * i:16 * (i + 1)], QKS[:, 16 * i:16 * (i + 1)],
                float(EPS_QK), None, op0=ALU.add))
        t_qe = Cn["v"]
        w("sc", "v", t_qe)
        op("sc", lambda e: e.activation(STF[:, 512:768], STF[:, 0:256], AF.Ln))
        fence("sc")
        t_recip = op("sc", lambda e: e.activation(
            TMPF[:, 0:256], STF[:, 512:768], AF.Exp, scale=-1.0))
        if "recip" in dbg:
            w("s", "sc", t_recip)
            dma("s", dbg["recip"][:, :], TMPF[:, 0:256])
        if "qks2" in dbg:
            w("s", "v", t_qe)
            dma("s", dbg["qks2"][:, :], QKS[:, :])

        use_bank(("PA", 0), ("PA", 1))
        for h in range(H):
            for i in range(NT):
                w("t", "v", elu_k[i])
                mm(PA[0:64, 64 * h:64 * (h + 1)],
                   SC[:, 1024 * i + 64 * h: 1024 * i + 64 * h + 64],
                   SB[:, 1024 * i + 64 * h: 1024 * i + 64 * h + 64],
                   start=(i == 0), stop=(i == NT - 1),
                   sem=(i == NT - 1 and h in (7, 15)))
        t_sdone = Cn["t"]
        w("v", "t", t_sdone)
        op("v", lambda e: e.tensor_copy(SPART[:, 0:512], PA[0:64, 0:512]))
        t_sev = op("v", lambda e: e.tensor_copy(SPART[:, 512:1024],
                                                PA[0:64, 512:1024]))
        free_bank(("PA", 0), "v", t_sev)
        free_bank(("PA", 1), "v", t_sev)
        cut("sphase")
        w("g", "v", t_sev)
        tok_ccin = dma("g", cc_in[:, :], SPART[:, :])
        wdma("g", "g", tok_ccin)
        t_cc = op("g", lambda e: e.collective_compute(
            "AllReduce", ALU.add,
            replica_groups=[[0, 1], [2, 3], [4, 5], [6, 7]],
            ins=[cc_in[:, :]], outs=[cc_out[:, :]]))

        cut("cc")
        w("t", "sc", t_recip)
        for i in range(NT):
            po = 128 * (i % 8)
            use_bank(("PTB7",))
            op("t", lambda e, i=i, po=po: e.transpose(
                PTB[0:16, po:po + 128], TMPF[:, 16 * i:16 * (i + 1)], I128),
                sem=True)
            w("v", "t", Cn["t"])
            tv = op("v", lambda e, i=i, po=po: e.tensor_copy(
                QKSRT[:, 128 * i:128 * (i + 1)], PTB[0:16, po:po + 128]))
            free_bank(("PTB7",), "v", tv)
        t_qksr = Cn["v"]
        if "qksrT" in dbg:
            w("s", "v", t_qksr)
            dma("s", dbg["qksrT"][:, :], POOL[:, 3072:5120])

        wdma("t", "g", tok_sel)
        t_wv_done = None
        for w_ in range(NW):
            sax = SAa if w_ % 2 == 0 else SAb
            wdma("t", "g", tok_wv)
            dstc = lambda m, sax=sax: sax[:, 512 * m: 512 * m + 512]
            fm_proj(W0b, 512 * w_, dstc, 2)
            if w_ == NW - 1:
                t_wv_done = Cn["t"]
            wdma("t", "g", tok_wv2)
            w("t", "sc", Cn["sc"])
            for m in range(DC):
                bk, off = FMROT[m % 4]
                key = ("PB" if bk is PB else "PC", off // 512)
                bank = bk[:, off:off + 512]
                use_bank(key)
                for k in range(DC):
                    mm(bank, W1b(k, m), sax[:, 512 * k: 512 * k + 512],
                       start=(k == 0), stop=(k == DC - 1), sem=(k == DC - 1))
                w("sc", "t", Cn["t"])
                w("sc", "t", t_sdone)
                if tok_dbg_q2:
                    wdma("sc", "s", tok_dbg_q2)
                if tok_dbg_qks:
                    wdma("sc", "s", tok_dbg_qks)
                w("sc", "v", t_k2done)
                tv = op("sc", lambda e, m=m, w_=w_, bank=bank: e.activation(
                    SB[:, 2048 * m + 512 * w_: 2048 * m + 512 * w_ + 512], bank,
                    AF.Identity, bias=VECS[:, 3 * 8 + m: 3 * 8 + m + 1]))
                free_bank(key, "sc", tv)
            t_v2w = Cn["sc"]
            w("sc", "v", t_derv)
            for d in range(DC):
                op("sc", lambda e, d=d, w_=w_: e.activation(
                    SX[:, 2048 * d + 512 * w_: 2048 * d + 512 * w_ + 512],
                    SX[:, 2048 * d + 512 * w_: 2048 * d + 512 * w_ + 512],
                    AF.Identity, bias=DERV[:, 56 + d:56 + d + 1]))
            t_gb2 = Cn["sc"]
            w("t", "v", t_qksr)
            for c in range(DC):
                key = ("PA", c % 2)
                bank = PA[:, 0:512] if c % 2 == 0 else PA[:, 512:1024]
                use_bank(key)
                mm(bank, SEL[0:16, 128 * c:128 * (c + 1)],
                   QKSRT[:, 512 * w_:512 * (w_ + 1)], True, True, sem=True)
                w("v", "t", Cn["t"])
                w("v", "sc", t_v2w)
                tv = op("v", lambda e, c=c, w_=w_, bank=bank: e.tensor_mul(
                    SB[:, 2048 * c + 512 * w_: 2048 * c + 512 * w_ + 512],
                    SB[:, 2048 * c + 512 * w_: 2048 * c + 512 * w_ + 512], bank))
                free_bank(key, "v", tv)
        t_v2 = Cn["sc"]
        t_vchain = Cn["t"]
        tok_wo = wload(W0, "Wo", gate=("t", t_wv_done))
        t_v2s = Cn["v"]
        t_bcast_t = Cn["t"]
        tok_dbg_v2 = None
        if "v2" in dbg:
            w("s", "v", t_v2s)
            tok_dbg_v2 = dma("s", dbg["v2"][:, :], SB[:, :])
        cut("v2s")

        w("g", "g", t_cc)
        w("g", "t", t_bcast_t)
        dma("g", STOT[:, :], cc_out[:, :])
        tok_stot = dma("g", STOT64[:, :], cc_out[:, :])
        if "Stot" in dbg:
            wdma("s", "g", tok_stot)
            dma("s", dbg["Stot"][:, :], STOT[:, :])

        wdma("t", "g", tok_stot)
        w("t", "v", t_v2s)
        aout_sc = {}
        aout_v = {}
        wo_end = {}
        y_tok = {}
        y_add = {}
        sq_v_tok = {}
        p4_var = {}
        p4_rst = {}
        p4_app = {}
        p4_sqmm = {}
        SQSL = [POOL[:, 2048:2560], POOL[:, 2560:3072],
                POOL[:, 4096:4608], POOL[:, 4608:5120]]
        TMP4 = POOL[:, 4608:5120]

        def out_units(nn):
            sax = SAa if nn % 2 == 0 else SAb
            for c in range(DC):
                for hh in range(2):
                    h = 2 * c + hh
                    j = c * 2 + hh
                    bk, off = FMROT[j % 4]
                    key = ("PB" if bk is PB else "PC", off // 512)
                    bank = bk[0:64, off:off + 512]
                    use_bank(key)
                    ST = STOT if hh == 0 else STOT64
                    mm(bank, ST[:, 64 * h:64 * (h + 1)],
                       SB[64 * hh:64 * (hh + 1),
                          2048 * c + 512 * nn: 2048 * c + 512 * nn + 512],
                       True, True, sem=True)
                    eng2 = "sc" if j % 2 == 0 else "v"
                    w(eng2, "t", Cn["t"])
                    if nn >= 2:
                        w(eng2, "t", wo_end[nn - 2])
                    if eng2 == "sc":
                        tv = op("sc", lambda e, c=c, hh=hh, bank=bank, sax=sax:
                                e.copy(
                                    sax[64 * hh:64 * (hh + 1),
                                        512 * c:512 * (c + 1)], bank))
                    else:
                        tv = op("v", lambda e, c=c, hh=hh, bank=bank, sax=sax:
                                e.tensor_copy(
                                    sax[64 * hh:64 * (hh + 1),
                                        512 * c:512 * (c + 1)], bank))
                    free_bank(key, eng2, tv)
            aout_sc[nn] = Cn["sc"]
            aout_v[nn] = Cn["v"]

        def wo_units(nn):
            sax = SAa if nn % 2 == 0 else SAb
            wdma("t", "g", tok_wo)
            w("t", "sc", aout_sc[nn])
            w("t", "v", aout_v[nn])
            for m in range(DC):
                bk, off = FMROT[m % 4]
                key = ("PB" if bk is PB else "PC", off // 512)
                bank = bk[:, off:off + 512]
                tslot = POOL[:, 1024 + 512 * (m % 2):1024 + 512 * (m % 2) + 512]
                use_bank(key)
                for k in range(DC):
                    mm(bank, W0b(k, m), sax[:, 512 * k: 512 * k + 512],
                       start=(k == 0), stop=(k == DC - 1), sem=(k == DC - 1))
                w("sc", "t", Cn["t"])
                wdma("sc", "g", tok_ccin)
                w("sc", "v", Cn["v"])
                op("sc", lambda e, bank=bank, tslot=tslot, m=m: e.activation(
                    tslot, bank, AF.Identity,
                    bias=VECS[:, 4 * 8 + m: 4 * 8 + m + 1]))
                t_bo = Cn["sc"]
                free_bank(key, "sc", t_bo)
                w("v", "sc", t_bo)
                w("v", "t", t_sdone)
                if tok_dbg_k2:
                    wdma("v", "s", tok_dbg_k2)
                op("v", lambda e, m=m, nn=nn, tslot=tslot: e.tensor_add(
                    SC[:, 2048 * m + 512 * nn: 2048 * m + 512 * nn + 512],
                    tslot,
                    SH[:, 2048 * m + 512 * nn: 2048 * m + 512 * nn + 512]))
                y_add[(nn, m)] = Cn["v"]
                prev = p4_sqmm.get(nn - 1)
                if prev:
                    w("v", "t", prev[(m % 4) + 4])
                op("v", lambda e, m=m, nn=nn: e.tensor_mul(
                    SQSL[m % 4],
                    SC[:, 2048 * m + 512 * nn: 2048 * m + 512 * nn + 512],
                    SC[:, 2048 * m + 512 * nn: 2048 * m + 512 * nn + 512]))
                sq_v_tok[(nn, m)] = Cn["v"]
            wo_end[nn] = Cn["t"]
            y_tok[nn] = Cn["v"]

        def p4_stats(w_):
            ys = lambda d, w_=w_: SC[:, 2048 * d + 512 * w_:
                                     2048 * d + 512 * w_ + 512]
            use_bank(("PA", 0), ("PA", 1))
            sq_mm = {}
            t_mu = 0
            for d in range(DC):
                w("t", "v", y_add[(w_, d)])
                mm(PA[:, 0:512], Jc, ys(d), start=(d == 0), stop=(d == DC - 1),
                   sem=(d == DC - 1))
                if d == DC - 1:
                    t_mu = Cn["t"]
                w("t", "v", sq_v_tok[(w_, d)])
                mm(PA[:, 512:1024], Jc, SQSL[d % 4], start=(d == 0),
                   stop=(d == DC - 1), sem=True)
                sq_mm[d] = Cn["t"]
            p4_sqmm[w_] = sq_mm
            w("v", "t", t_mu)
            op("v", lambda e: e.tensor_copy(MUT, PA[:, 0:512]))
            t_mucp = op("v", lambda e: e.tensor_copy(STF[:, 0:512],
                                                     PA[:, 0:512]))
            free_bank(("PA", 0), "v", t_mucp)
            op("v", lambda e: e.tensor_mul(STF[:, 512:1024], STF[:, 0:512],
                                           STF[:, 0:512]))
            w("v", "t", sq_mm[DC - 1])
            t_var = op("v", lambda e: e.tensor_sub(
                STF[:, 0:512], PA[:, 512:1024], STF[:, 512:1024]))
            free_bank(("PA", 1), "v", t_var)
            p4_var[w_] = t_var

        def p4_rstd(w_):
            w("sc", "v", p4_var[w_])
            if w_ > 0:
                w("sc", "v", p4_app[w_ - 1])
            op("sc", lambda e: e.activation(STF[:, 512:1024], STF[:, 0:512],
                                            AF.Ln, bias=float(EPS_IN)))
            p4_rst[w_] = op("sc", lambda e: e.activation(
                RST, STF[:, 512:1024], AF.Exp, scale=-0.5))

        def p4_apply(w_):
            w("v", "sc", p4_rst[w_])
            w("v", "sc", t_gb2)
            for d in range(DC):
                op("v", lambda e, d=d, w_=w_: e.tensor_sub(
                    TMP4,
                    SC[:, 2048 * d + 512 * w_: 2048 * d + 512 * w_ + 512],
                    MUT))
                op("v", lambda e: e.tensor_mul(TMP4, TMP4, RST))
                op("v", lambda e, d=d, w_=w_: e.scalar_tensor_tensor(
                    SX[:, 2048 * d + 512 * w_: 2048 * d + 512 * w_ + 512],
                    TMP4, DERV[:, 48 + d:48 + d + 1],
                    SX[:, 2048 * d + 512 * w_: 2048 * d + 512 * w_ + 512],
                    op0=ALU.mult, op1=ALU.add))
            p4_app[w_] = Cn["v"]

        out_units(0)
        out_units(1)
        wo_units(0)
        p4_stats(0)
        p4_rstd(0)
        out_units(2)
        p4_apply(0)
        wo_units(1)
        p4_stats(1)
        p4_rstd(1)
        out_units(3)
        t_sb_free = Cn["t"]
        p4_apply(1)
        wo_units(2)
        t_wo2t = Cn["t"]
        p4_stats(2)
        p4_rstd(2)
        p4_apply(2)
        wo_units(3)
        t_wo3t = Cn["t"]
        p4_stats(3)
        p4_rstd(3)
        p4_apply(3)
        t_y = y_tok[NW - 1]
        t_p3t = Cn["t"]
        tok_dbg_y = None
        if "yT" in dbg:
            w("s", "v", p4_app[NW - 1])
            tok_dbg_y = dma("s", dbg["yT"][:, :], SC[:, :])
        cut("p3")
        t_x1 = Cn["v"]
        t_p4t = Cn["t"]
        tok_dbg_x1 = None
        if "x1T" in dbg:
            w("s", "v", t_x1)
            tok_dbg_x1 = dma("s", dbg["x1T"][:, :], SX[:, :])
        cut("p4")

        wi_tok = []
        for kc in range(8):
            if kc < 4:
                dstap = SB[:, 4096 * kc:4096 * (kc + 1)]
                gate = ("t", t_sb_free)
                wdma("g", "s", tok_dbg_v2)
            elif kc < 6:
                dstap = W1[:, 4096 * (kc - 4):4096 * (kc - 4) + 4096]
                gate = ("t", t_vchain)
            elif kc == 6:
                dstap = SAa[:, :]
                gate = ("t", t_wo2t)
            else:
                dstap = SAb[:, :]
                gate = ("t", t_wo3t)
            w("g", gate[0], gate[1])
            wi_tok.append(dma("g", dstap, wiT_in[128 * kc:128 * (kc + 1), :]))
        wo_tok = []
        for fc in range(FC):
            if fc < 16:
                dstap = SH[:, 1024 * fc:1024 * (fc + 1)]
                w("g", "v", t_y)
                wdma("g", "s", tok_dbg_h)
            else:
                dstap = SC[:, 1024 * (fc - 16):1024 * (fc - 16) + 1024]
                w("g", "t", t_p4t)
                w("g", "v", t_x1)
                wdma("g", "s", tok_dbg_y)
            wo_tok.append(dma("g", dstap, woT_in[128 * fc:128 * (fc + 1), :]))

        h2_tok = {}
        u_tok = {}
        aend_tok = {0: None}
        interC_tok = {}

        def ln2_stage(tc, t_aend_prev):
            xs2 = lambda d: SX[:, 2048 * d + 256 * tc: 2048 * d + 256 * tc + 256]
            use_bank(("PD",))
            w("t", "v", t_x1)
            wdma("v", "s", tok_dbg_x1)
            for d in range(DC):
                mm(PD[:, 0:256], Jc, xs2(d), start=(d == 0), stop=(d == DC - 1),
                   sem=(d == DC - 1))
            t_mu_mm = Cn["t"]
            use_bank(("PB", 0))
            w("sc", "t", Cn["t"])
            for d in range(DC):
                slot = XSQ[:, 256 * (d % 2):256 * (d % 2) + 256]
                if d >= 2:
                    w("sc", "t", Cn["t"])
                op("sc", lambda e, d=d, slot=slot: e.activation(
                    slot, xs2(d), AF.Square))
                w("t", "sc", Cn["sc"])
                mm(PB[:, 0:256], Jc, slot, start=(d == 0), stop=(d == DC - 1),
                   sem=True)
            t_ss_mm = Cn["t"]
            w("v", "t", t_mu_mm)
            op("v", lambda e: e.tensor_copy(MUT2, PD[:, 0:256]))
            t_mucp = op("v", lambda e: e.tensor_copy(STF[:, 0:256], PD[:, 0:256]))
            free_bank(("PD",), "v", t_mucp)
            op("v", lambda e: e.tensor_mul(STF[:, 256:512], STF[:, 0:256],
                                           STF[:, 0:256]))
            w("v", "t", t_ss_mm)
            t_var = op("v", lambda e: e.tensor_sub(STF[:, 0:256], PB[:, 0:256],
                                                   STF[:, 256:512]))
            free_bank(("PB", 0), "v", t_var)
            w("sc", "v", t_var)
            op("sc", lambda e: e.activation(STF[:, 256:512], STF[:, 0:256],
                                            AF.Ln, bias=float(EPS_OUT)))
            t_rst2 = op("sc", lambda e: e.activation(RST2, STF[:, 256:512],
                                                     AF.Exp, scale=-0.5))
            w("v", "sc", t_rst2)
            if t_aend_prev:
                w("v", "t", t_aend_prev)
            for d in range(DC):
                op("v", lambda e, d=d: e.tensor_sub(
                    TMP6[:, 0:256], xs2(d), MUT2))
                op("v", lambda e: e.tensor_mul(TMP6[:, 0:256], TMP6[:, 0:256],
                                               RST2))
                op("v", lambda e, d=d: e.tensor_scalar(
                    H2TC[:, 256 * d:256 * (d + 1)], TMP6[:, 0:256],
                    DERV[:, 16 + d:16 + d + 1], DERV[:, 24 + d:24 + d + 1],
                    op0=ALU.mult, op1=ALU.add))
            h2_tok[tc] = Cn["v"]
            if tc == 0 and "h2tc0" in dbg:
                w("s", "v", h2_tok[tc])
                dma("s", dbg["h2tc0"][:, :], POOL[:, 0:2048])

        def u_stage(tc):
            w("t", "v", h2_tok[tc])
            use_bank(("PD",))
            for k in range(DC):
                mm(PD[0:32, 0:256], ASEL[:, 32 * k:32 * (k + 1)],
                   H2TC[:, 256 * k:256 * (k + 1)], start=(k == 0),
                   stop=(k == DC - 1), sem=(k == DC - 1))
            w("v", "t", Cn["t"])
            op("v", lambda e: e.tensor_copy(UB[0:32, 0:256], PD[0:32, 0:256]))
            op("v", lambda e: e.tensor_copy(UB[0:32, 256:512], PD[0:32, 0:256]))
            t_u01 = op("v", lambda e: e.tensor_scalar(
                UB[0:16, 256:512], PD[0:16, 0:256], -1.0, None, op0=ALU.mult))
            free_bank(("PD",), "v", t_u01)
            u_tok[tc] = t_u01
            if tc == 0 and "u01" in dbg:
                w("s", "v", t_u01)
                dma("s", dbg["u01"][:, :], UB[:, :])

        AROT = [(PA, 0, ("PA", 0)), (PA, 512, ("PA", 1)), (PD, 0, ("PD",)),
                (PB, 0, ("PB", 0)), (PB, 512, ("PB", 1)), (PC, 0, ("PC", 0))]

        def a_stage(tc, t_p3t):
            w("t", "v", u_tok[tc])
            for kc in range(8):
                wdma("t", "g", wi_tok[kc])

            def a_head(fc):
                bk, off, key = AROT[fc % 6]
                bank = bk[:, off:off + 256]
                r0s = R0P[:, 256 * (fc % 3):256 * (fc % 3) + 256]
                use_bank(key)
                mm(bank, ARENA[0:16, 128 * fc:128 * fc + 128], UB[0:16, 0:256],
                   start=True, stop=False)
                for k in range(DC):
                    mm(bank, wiTblk(k, fc), H2TC[:, 256 * k:256 * (k + 1)],
                       start=False, stop=False, sem=(k == DC - 1))
                t_head = Cn["t"]
                w("sc", "t", t_head)
                w("sc", "v", Cn["v"])
                op("sc", lambda e, bank=bank, r0s=r0s: e.activation(
                    r0s, bank, AF.Relu, scale=TKB[:, 0:1]))
                return Cn["sc"]

            def a_tail(fc, t_ev0):
                bk, off, key = AROT[fc % 6]
                bank = bk[:, off:off + 256]
                r0s = R0P[:, 256 * (fc % 3):256 * (fc % 3) + 256]
                r1s = R1P[:, 256 * (fc % 3):256 * (fc % 3) + 256]
                w("t", "sc", t_ev0)
                mm(bank, ARENA[0:32, 128 * fc:128 * fc + 128], UB[0:32, 256:512],
                   start=False, stop=True, sem=True)
                w("v", "t", Cn["t"])
                t_ev1 = op("v", lambda e, bank=bank, r1s=r1s: e.tensor_scalar(
                    r1s, bank, 0.0, TKB[:, 1:2], op0=ALU.max, op1=ALU.mult))
                free_bank(key, "v", t_ev1)
                if tc == 0:
                    w("v", "t", t_p3t)
                op("v", lambda e, fc=fc, r0s=r0s, r1s=r1s: e.tensor_add(
                    W0[:, 256 * fc:256 * (fc + 1)], r0s, r1s))

            pend = []
            for fc in range(FC):
                ev0 = a_head(fc)
                pend.append((fc, ev0))
                if len(pend) > 2:
                    a_tail(*pend.pop(0))
            aend_tok[tc + 1] = Cn["t"]
            for it in pend:
                a_tail(*it)
            interC_tok[tc] = Cn["v"]
            if tc == 0 and "interC0" in dbg:
                w("s", "v", interC_tok[tc])
                dma("s", dbg["interC0"][:, :], W0[:, :])
            if tc == 0:
                cut("aphase")

        def mm2_stage(tc):
            w("t", "v", interC_tok[tc])
            for fc in range(FC):
                wdma("t", "g", wo_tok[fc])
            dp_bank = [(PB, 0), (PB, 256), (PB, 512), (PB, 768),
                       (PC, 0), (PC, 256), (PC, 512), (PC, 768)]
            for dp in range(DC):
                bk, off = dp_bank[dp]
                key = ("PB", off // 512) if bk is PB else ("PC", off // 512)
                if dp % 2 == 0:
                    use_bank(key)
                for fc in range(FC):
                    mm(bk[:, off:off + 256], woTblk(fc, dp),
                       W0[:, 256 * fc:256 * (fc + 1)], start=(fc == 0),
                       stop=(fc == FC - 1), sem=(fc == FC - 1))
                if dp % 2 == 1:
                    t_dp = Cn["t"]
                    w("v", "t", t_dp)
                    for d2 in (dp - 1, dp):
                        bk2, off2 = dp_bank[d2]
                        op("v", lambda e, bk2=bk2, off2=off2, d2=d2:
                           e.tensor_scalar(
                               TMP6[:, 256:512], bk2[:, off2:off2 + 256],
                               DERV[:, 40 + d2:40 + d2 + 1], None, op0=ALU.mult))
                        tv = op("v", lambda e, d2=d2, tc=tc: e.tensor_add(
                            SX[:, 2048 * d2 + 256 * tc:
                               2048 * d2 + 256 * tc + 256],
                            SX[:, 2048 * d2 + 256 * tc:
                               2048 * d2 + 256 * tc + 256],
                            TMP6[:, 256:512]))
                        free_bank(key, "v", tv)
            t_x2 = Cn["v"]
            if tc == 0:
                cut("tc0")
            w("g", "v", t_x2)
            dma("g",
                out_xT.rearrange("(d p) t -> p d t", p=128)[:, :,
                                                            256 * tc:256 * (tc + 1)],
                SX.ap().rearrange("p (d t) -> p d t", d=DC)[:, :,
                                                            256 * tc:256 * (tc + 1)])

        ln2_stage(0, None)
        if True:
            cut("ln2s")
        u_stage(0)
        cut("u")
        for tc in range(TCN):
            a_stage(tc, t_p3t)
            if tc + 1 < TCN:
                ln2_stage(tc + 1, aend_tok[tc + 1])
            mm2_stage(tc)
            if tc + 1 < TCN:
                u_stage(tc + 1)

        w("g", "dg", D16["g"])
        w("s", "ds", D16["s"])

    with nc.allow_low_precision("bf16 recipe calibrated by problem tolerance"), \
            nc.Block() as blk:
        def run(name, e):
            try:
                emit(name, e)
            except _Cut:
                pass

        @blk.tensor
        def _(e):
            run("t", e)

        @blk.vector
        def _(e):
            run("v", e)

        @blk.scalar
        def _(e):
            run("sc", e)

        @blk.gpsimd
        def _(e):
            run("g", e)

        @blk.sync
        def _(e):
            run("s", e)

    return nc, dbg


_NC_CACHE = {}


def _get_nc(debug=(), stop_after=None):
    key = (tuple(sorted(debug)), stop_after)
    if key not in _NC_CACHE:
        _NC_CACHE[key] = build_nc(debug=key[0], stop_after=stop_after)
    return _NC_CACHE[key]


def _host_inputs(inputs, core):
    b = core // 2
    x = np.asarray(inputs["L0_attn_x"])
    xs = x[b, 2048 * (core % 2):2048 * (core % 2 + 1), :]
    m = {}
    m["xT"] = np.ascontiguousarray(xs.T).astype(np.float32)
    for w in ("Wq", "Wk", "Wv", "Wq2", "Wk2", "Wv2", "Wo"):
        m[w] = np.ascontiguousarray(np.asarray(inputs["L0_attn_" + w],
                                               dtype=np.float32))
    vecs = [inputs["L0_attn_bq"], inputs["L0_attn_bk"], inputs["L0_attn_bv"],
            inputs["L0_attn_bv2"], inputs["L0_attn_bo"],
            inputs["L0_attn_gamma"], inputs["L0_attn_beta"]]
    m["vecs"] = np.ascontiguousarray(np.concatenate(
        [np.asarray(v, dtype=np.float32).reshape(8, 128).T for v in vecs],
        axis=1))
    m["brow"] = np.concatenate(
        [np.asarray(inputs["L0_attn_bq2"], dtype=np.float32).reshape(1, -1),
         np.asarray(inputs["L0_attn_bk2"], dtype=np.float32).reshape(1, -1)],
        axis=1)
    m["cT"] = np.ascontiguousarray(
        np.asarray(inputs["L0_c"], dtype=np.float32)[b].reshape(8, 128).T)
    m["wmod"] = np.ascontiguousarray(np.asarray(inputs["L0_wmod"],
                                                dtype=np.float32))
    m["wiT"] = np.ascontiguousarray(
        np.asarray(inputs["L0_ffn_wi_w"], dtype=np.float32).T)
    m["woT"] = np.ascontiguousarray(
        np.asarray(inputs["L0_ffn_wo_w"], dtype=np.float32).T)
    m["Aall"] = np.ascontiguousarray(
        np.asarray(inputs["L0_ffn_lora_As"], dtype=np.float32).reshape(128, 1024))
    m["BallT"] = np.ascontiguousarray(
        np.asarray(inputs["L0_ffn_lora_Bs"],
                   dtype=np.float32).transpose(0, 2, 1).reshape(128, 4096))
    m["idx"] = np.asarray(inputs["L0_ffn_top_k_indices"]).reshape(1, 2).astype(
        np.int32)
    m["tkv"] = np.asarray(inputs["L0_ffn_top_k_values"],
                          dtype=np.float32).reshape(1, 2)
    constB = np.zeros((128, 272), dtype=np.float32)
    constB[:, 0:128] = 1.0 / 1024.0
    constB[:, 128:256] = np.eye(128, dtype=np.float32)
    constB[:, 256:272] = np.tile(np.eye(16, dtype=np.float32), (8, 1))
    m["constB"] = constB
    constF = np.zeros((128, 8), dtype=np.float32)
    constF[:, 0] = np.arange(128) // 16
    constF[:, 1] = -0.5
    constF[:, 2] = 1e-5
    constF[:, 3] = 1e-3
    constF[:, 4] = 0.0
    constF[:, 5] = 1.0
    m["constF"] = constF
    sel = np.zeros((16, 1024), dtype=np.float32)
    for h in range(16):
        c, hh = h // 2, h % 2
        sel[h, 128 * c + 64 * hh:128 * c + 64 * hh + 64] = 1.0
    m["sel"] = sel
    m["ones_in"] = np.ones((1, 128), dtype=np.float32)
    return m


def kernel(**inputs):
    nc, dbg = _get_nc()
    in_maps = [_host_inputs(inputs, core) for core in range(8)]
    res = run_bass_kernel_spmd(nc, in_maps, core_ids=list(range(8)))
    outs = [np.asarray(res.results[c]["out_xT"]).T for c in range(8)]
    return np.concatenate(outs, axis=0).reshape(4, 4096, 1024).astype(np.float32)


# version 13
# speedup vs baseline: 1.5401x; 1.0035x over previous
import numpy as np
import concourse.bass as bass
import concourse.mybir as mybir
from concourse.bass_utils import run_bass_kernel_spmd

F32 = mybir.dt.float32
BF16 = mybir.dt.bfloat16
I32 = mybir.dt.int32
AF = mybir.ActivationFunctionType
ALU = mybir.AluOpType
AX = mybir.AxisListType

T = 2048
D = 1024
DC = 8
NW = 4
NT = 16
H = 16
DH = 64
FF = 4096
FC = 32
TCN = 8
EPS_IN = 1e-3
EPS_OUT = 1e-5
EPS_QK = 1e-6


class _Cut(Exception):
    pass


def build_nc(debug=(), stop_after=None):
    nc = bass.Bass("TRN2", target_bir_lowering=False)
    dbgn = set(debug)

    def din(name, shape, dtype=F32):
        return nc.dram_tensor(name, list(shape), dtype, kind="ExternalInput").ap()

    xT = din("xT", [D, T])
    WTS = {w: din(w, [D, D]) for w in ("Wq", "Wk", "Wv", "Wq2", "Wk2", "Wv2", "Wo")}
    vecs_in = din("vecs", [128, 7 * DC])
    brow_in = din("brow", [1, 2 * D])
    cT_in = din("cT", [128, DC])
    wmod_in = din("wmod", [D, 6 * D])
    wiT_in = din("wiT", [D, FF])
    woT_in = din("woT", [FF, D])
    Aall_in = din("Aall", [128, D])
    BallT_in = din("BallT", [128, FF])
    idx_in = din("idx", [1, 2], I32)
    tkv_in = din("tkv", [1, 2])
    constB_in = din("constB", [128, 272])
    constF_in = din("constF", [128, 8])
    sel_in = din("sel", [16, 1024])
    ones_in = din("ones_in", [1, 128])

    out_xT = nc.dram_tensor("out_xT", [D, T], F32, kind="ExternalOutput").ap()
    cc_in = nc.dram_tensor("cc_in", [64, H * DH], BF16, kind="Internal").ap()
    cc_out = nc.dram_tensor("cc_out", [64, H * DH], BF16, kind="Internal").ap()

    dbg = {}
    def dbgt(name, shape, dtype=F32):
        if name in dbgn:
            dbg[name] = nc.dram_tensor("dbg_" + name, list(shape), dtype,
                                       kind="ExternalOutput").ap()
    dbgt("modT", [128, 48]); dbgt("hT", [128, 16384], BF16)
    dbgt("q2", [128, 16384], BF16); dbgt("k2", [128, 16384], BF16)
    dbgt("Stot", [64, 1024], BF16); dbgt("qks", [128, 256])
    dbgt("v2", [128, 16384], BF16); dbgt("yT", [128, 16384], BF16)
    dbgt("x1T", [128, 16384], BF16); dbgt("h2tc0", [128, 2048], BF16)
    dbgt("interC0", [128, 8192], BF16); dbgt("BselT", [128, 4096], BF16)
    dbgt("u01", [64, 512], BF16); dbgt("OH", [128, 32], BF16)
    dbgt("qksrT", [128, 2048], BF16); dbgt("ASEL", [128, 256], BF16)
    dbgt("recip", [128, 256], BF16); dbgt("qks2", [128, 256])
    dbgt("DERV", [128, 64])

    A = nc.alloc_sbuf_tensor
    SX = A("SX", [128, 16384], BF16)
    SH = A("SH", [128, 16384], BF16)
    SB = A("SB", [128, 16384], BF16)
    SC = A("SC", [128, 16384], BF16)
    SAa = A("SAa", [128, 4096], BF16)
    SAb = A("SAb", [128, 4096], BF16)
    W0 = A("W0", [128, 8192], BF16)
    W1 = A("W1", [128, 8192], BF16)
    POOL = A("POOL", [128, 5120], BF16)
    CONST = A("CONSTB", [128, 272], BF16)
    P16T = A("P16T", [128, 8], F32)
    VECS = A("VECS", [128, 7 * DC], F32)
    DERV = A("DERV", [128, 8 * DC], F32)
    MODT = A("MODT", [128, 48], F32)
    CT = A("CT", [128, DC], F32)
    SIG = A("SIG", [128, DC], F32)
    SILU = A("SILU", [128, DC], BF16)
    ARENA = A("ARENA", [128, 4096], BF16)
    SEL = A("SEL", [16, 1024], BF16)
    UB = A("UB", [64, 512], BF16)
    BROW = A("BROW", [1, D], BF16)
    STF = A("STF", [128, 1024], F32)
    ASEL = A("ASEL", [128, 256], BF16)
    TKB = A("TKB", [128, 2], F32)
    IDXS = A("IDXS", [1, 2], I32)
    IDXF = A("IDXF", [1, 2], F32)
    IDXB = A("IDXB", [1, 2], BF16)
    OHE = A("OHE", [128, 2], F32)
    OH = A("OH", [128, 32], BF16)
    TKS = A("TKS", [1, 2], BF16)
    ONES1 = A("ONES1", [1, 128], BF16)

    MUT = POOL[:, 0:512]
    RST = POOL[:, 512:1024]
    TMPE = POOL[:, 1024:2048]
    TMPF = POOL[:, 2048:3072]
    QKSRT = POOL[0:16, 3072:5120]
    SPART = POOL[0:64, 1024:2048]
    ROW0 = POOL[0:1, 3072:3584]
    ROW1 = POOL[0:1, 3584:4096]
    H2TC = POOL[:, 0:2048]
    R0P = POOL[:, 2048:2816]
    R1P = POOL[:, 2816:3584]
    MUT2 = POOL[:, 3584:3840]
    RST2 = POOL[:, 3840:4096]
    TMP6 = POOL[:, 4096:4608]
    XSQ = POOL[:, 4608:5120]
    QKS = STF[:, 768:1024]
    STOT = POOL[0:64, 3072:4096]
    STOT64 = POOL[64:128, 3072:4096]

    Jc = CONST[:, 0:128]
    I128 = CONST[:, 128:256]
    I16t = CONST[:, 256:272]

    PA = nc.alloc_psum_tensor("PA", [128, 1024], F32)
    PB = nc.alloc_psum_tensor("PB", [128, 1024], F32)
    PC = nc.alloc_psum_tensor("PC", [128, 1024], F32)
    PD = nc.alloc_psum_tensor("PD", [128, 512], F32)
    PTB = nc.alloc_psum_tensor("PTB", [128, 1024], BF16)

    sems = {}
    for name in ("t", "v", "sc", "g", "s", "dg", "ds"):
        sems[name] = nc.alloc_semaphore("sem_" + name)

    for col, val in ((1, -0.5), (2, 1e-5), (3, 1e-3), (4, 0.0), (5, 1.0)):
        nc.const_aps.register(F32, float(val), P16T[:, col:col + 1])

    def W0b(k, m):
        return W0[:, 1024 * k + 128 * m: 1024 * k + 128 * m + 128]

    def W1b(k, m):
        return W1[:, 1024 * k + 128 * m: 1024 * k + 128 * m + 128]

    def wiTblk(k, fc):
        if k < 4:
            return SB[:, 4096 * k + 128 * fc: 4096 * k + 128 * fc + 128]
        if k < 6:
            return W1[:, 4096 * (k - 4) + 128 * fc: 4096 * (k - 4) + 128 * fc + 128]
        if k == 6:
            return SAa[:, 128 * fc: 128 * fc + 128]
        return SAb[:, 128 * fc: 128 * fc + 128]

    def woTblk(fc, dp):
        if fc < 16:
            return SH[:, 1024 * fc + 128 * dp: 1024 * fc + 128 * dp + 128]
        return SC[:, 1024 * (fc - 16) + 128 * dp: 1024 * (fc - 16) + 128 * dp + 128]

    def wmod_rhs(k, n):
        if k < 4:
            base = [SB, SB, SC, SC][k]
            off = (k % 2) * 6144
            return base[:, off + 512 * n: off + 512 * n + 512]
        if k == 4:
            if n < 4:
                return SC[:, 14336 + 512 * n: 14336 + 512 * n + 512]
            if n < 8:
                return SAb[:, 2048 + 512 * (n - 4): 2048 + 512 * (n - 4) + 512]
            return POOL[:, 1024 + 512 * (n - 8): 1024 + 512 * (n - 8) + 512]
        if k == 5:
            return W1[:, 512 * n: 512 * n + 512]
        if k == 6:
            if n < 8:
                return SAa[:, 512 * n: 512 * n + 512]
            return SAb[:, 512 * (n - 8): 512 * (n - 8) + 512]
        if n < 8:
            return SB[:, 12288 + 512 * n: 12288 + 512 * n + 512]
        return SC[:, 12288 + 512 * (n - 8): 12288 + 512 * (n - 8) + 512]

    def emit(my, e):
        def cut(name):
            if stop_after == name:
                raise _Cut()
        Cn = {k: 0 for k in ("t", "v", "sc", "g", "s")}
        D16 = {"g": 0, "s": 0}
        lastw = {}
        bankfree = {}

        def op(eng2, builder, sem=True):
            if my == eng2:
                ins = builder(e)
                if sem:
                    ins.then_inc(sems[eng2], 1)
            if sem:
                Cn[eng2] += 1
            return Cn[eng2]

        def w(eng2, sem_name, val):
            if my != eng2 or val is None or val <= 0:
                return
            if lastw.get(sem_name, -1) >= val:
                return
            lastw[sem_name] = val
            e.wait_ge(sems[sem_name], val)

        def dma(q, out, in_, **kw):
            D16[q] += 16
            if my == q:
                e.dma_start(out=out, in_=in_, **kw).then_inc(
                    sems["dg" if q == "g" else "ds"], 16)
            return D16[q]

        def wdma(eng2, q, tok):
            if tok:
                w(eng2, "dg" if q == "g" else "ds", tok)

        def fence(eng2):
            w(eng2, eng2, Cn[eng2])

        def mm(out, lhsT, rhs, start, stop, sem=False):
            return op("t", lambda e: e.matmul(out, lhsT, rhs, start=start,
                                              stop=stop), sem=sem)

        def use_bank(*keys):
            for k in keys:
                if k in bankfree:
                    eng2, tok = bankfree[k]
                    w("t", eng2, tok)

        def free_bank(key, eng2, tok):
            bankfree[key] = (eng2, tok)

        tok_p16 = dma("s", P16T[:, :], constF_in[:, :])
        tok_vecs = dma("s", VECS[:, :], vecs_in[:, :])
        tok_ct = dma("s", CT[:, :], cT_in[:, :])
        tok_idx = dma("s", IDXS[:, :], idx_in[:, :])

        tok_consts = dma("g", CONST[:, :], constB_in[:, :])
        tok_ones1 = dma("g", ONES1[:, :], ones_in[:, :])
        tok_sel = dma("g", SEL[:, :], sel_in[:, :])
        tok_brow = dma("g", BROW[:, :], brow_in[0:1, 0:D])
        tok_x = dma("g", SX.ap().rearrange("p (d t) -> p d t", d=DC),
                    xT.rearrange("(d p) t -> p d t", p=128))
        wmod_tok = []
        for k in range(4):
            dsts = [SB, SB, SC, SC][k]
            off = (k % 2) * 6144
            wmod_tok.append(dma("g", dsts[:, off:off + 6144],
                                wmod_in[128 * k:128 * (k + 1), :]))
        wmod_tok.append(dma("g", SC[:, 14336:16384], wmod_in[512:640, 0:2048]))
        wmod_tok.append(dma("g", SAb[:, 2048:4096], wmod_in[512:640, 2048:4096]))
        wmod_tok.append(dma("g", POOL[:, 1024:3072], wmod_in[512:640, 4096:6144]))
        wmod_tok.append(dma("g", W1[:, 0:6144], wmod_in[640:768, :]))
        wmod_tok.append(dma("g", SAa[:, 0:4096], wmod_in[768:896, 0:4096]))
        wmod_tok.append(dma("g", SAb[:, 0:2048], wmod_in[768:896, 4096:6144]))
        wmod_tok.append(dma("g", SB[:, 12288:16384], wmod_in[896:1024, 0:4096]))
        wmod_tok.append(dma("g", SC[:, 12288:14336], wmod_in[896:1024, 4096:6144]))
        tok_aall = dma("g", W1[:, 6144:7168], Aall_in[:, :])
        tok_ball = dma("g", ARENA[:, :], BallT_in[:, :])
        tok_tkf = dma("g", TKS[:, :], tkv_in[:, :])

        def wload(buf, name, gate=None):
            if gate is not None:
                w("g", gate[0], gate[1])
            return dma("g", buf.ap().rearrange("p (k m j) -> p k m j", k=8, j=128),
                       WTS[name].rearrange("(k p) (m j) -> p k m j", p=128, j=128))

        tok_wq = wload(W0, "Wq")

        wdma("t", "g", tok_consts)
        wdma("t", "g", tok_x)

        def ln_stats(src_chunk_ap, eps, sq_scratch):
            use_bank(("PA", 0))
            for d in range(DC):
                mm(PA[:, 0:512], Jc, src_chunk_ap(d), start=(d == 0),
                   stop=(d == DC - 1), sem=(d == DC - 1))
            t_mu_mm = Cn["t"]
            w("sc", "t", Cn["t"])
            for d in range(DC):
                op("sc", lambda e, d=d: e.activation(
                    sq_scratch(d), src_chunk_ap(d), AF.Square))
            t_sq = Cn["sc"]
            use_bank(("PA", 1))
            w("t", "sc", t_sq)
            for d in range(DC):
                mm(PA[:, 512:1024], Jc, sq_scratch(d), start=(d == 0),
                   stop=(d == DC - 1), sem=(d == DC - 1))
            t_ss_mm = Cn["t"]
            w("v", "t", t_mu_mm)
            op("v", lambda e: e.tensor_copy(MUT, PA[:, 0:512]))
            t_mucp = op("v", lambda e: e.tensor_copy(STF[:, 0:512], PA[:, 0:512]))
            free_bank(("PA", 0), "v", t_mucp)
            op("v", lambda e: e.tensor_mul(STF[:, 512:1024], STF[:, 0:512],
                                           STF[:, 0:512]))
            w("v", "t", t_ss_mm)
            t_var = op("v", lambda e: e.tensor_sub(STF[:, 0:512], PA[:, 512:1024],
                                                   STF[:, 512:1024]))
            free_bank(("PA", 1), "v", t_var)
            w("sc", "v", t_var)
            wdma("sc", "s", tok_p16)
            op("sc", lambda e, eps=eps: e.activation(STF[:, 512:1024],
                                                     STF[:, 0:512], AF.Ln,
                                                     bias=float(eps)))
            t_rst = op("sc", lambda e: e.activation(RST, STF[:, 512:1024],
                                                    AF.Exp, scale=-0.5))
            return t_rst

        for w_ in range(NW):
            xs = lambda d, w_=w_: SX[:, 2048 * d + 512 * w_:
                                     2048 * d + 512 * w_ + 512]
            sqs = lambda d, w_=w_: SH[:, 2048 * d + 512 * w_:
                                      2048 * d + 512 * w_ + 512]
            t_rst = ln_stats(xs, EPS_OUT, sqs)
            w("v", "sc", t_rst)
            for d in range(DC):
                hs = SH[:, 2048 * d + 512 * w_: 2048 * d + 512 * w_ + 512]
                op("v", lambda e, d=d, w_=w_, hs=hs: e.tensor_sub(
                    hs, SX[:, 2048 * d + 512 * w_: 2048 * d + 512 * w_ + 512],
                    MUT))
                op("v", lambda e, hs=hs: e.tensor_mul(hs, hs, RST))
        t_hn = Cn["v"]

        wdma("sc", "s", tok_ct)
        t_sig = op("sc", lambda e: e.activation(SIG[:, :], CT[:, :], AF.Sigmoid))
        w("v", "sc", t_sig)
        t_silu = op("v", lambda e: e.tensor_mul(SILU[:, :], CT[:, :], SIG[:, :]))

        rowfree = [0, 0]
        wdma("t", "g", tok_consts)
        for n in range(12):
            key = ("PA", n % 2)
            bank = PA[0:1, 0:512] if n % 2 == 0 else PA[0:1, 512:1024]
            rowslot = ROW0 if n % 2 == 0 else ROW1
            wdma("t", "g", wmod_tok[11])
            w("t", "v", t_silu)
            use_bank(key)
            for k in range(8):
                mm(bank, SILU[:, k:k + 1], wmod_rhs(k, n),
                   start=(k == 0), stop=(k == 7), sem=(k == 7))
            w("v", "t", Cn["t"])
            w("v", "t", rowfree[n % 2])
            tv = op("v", lambda e, bank=bank, rowslot=rowslot: e.tensor_copy(
                rowslot, bank))
            free_bank(key, "v", tv)
            w("t", "v", tv)
            use_bank(("PTB7",))
            for c in range(4):
                op("t", lambda e, c=c, n=n, rowslot=rowslot: e.transpose(
                    PTB[:, 8 * (n % 2) + 2 * c: 8 * (n % 2) + 2 * c + 1],
                    rowslot[0:1, 128 * c:128 * c + 128], I128[0:1, 0:1]),
                    sem=(c == 3))
            rowfree[n % 2] = Cn["t"]
            w("v", "t", Cn["t"])
            vv, cc4 = n // 2, 4 * (n % 2)
            ptb_view = PTB[:, 8 * (n % 2): 8 * (n % 2) + 8].rearrange(
                "p (c two) -> p c two", two=2)[:, :, 0:1]
            tvv = op("v", lambda e, vv=vv, cc4=cc4, ptb_view=ptb_view:
                     e.tensor_copy(
                         MODT[:, 8 * vv + cc4: 8 * vv + cc4 + 4], ptb_view))
            free_bank(("PTB7",), "v", tvv)
        t_modr = Cn["v"]

        wdma("v", "s", tok_idx)
        op("v", lambda e: e.tensor_copy(IDXF[:, :], IDXS[:, :]))
        fence("v")
        op("v", lambda e: e.tensor_scalar(IDXF[:, :], IDXF[:, :], 0.0, 7.0,
                                          op0=ALU.max, op1=ALU.min))
        fence("v")
        t_idxb = op("v", lambda e: e.tensor_copy(IDXB[:, :], IDXF[:, :]))
        wdma("t", "g", tok_ones1)
        w("t", "v", t_idxb)
        use_bank(("PD",))
        mm(PD[0:128, 0:2], ONES1[0:1, 0:128], IDXB[0:1, 0:2], True, True, sem=True)
        w("v", "t", Cn["t"])
        wdma("v", "s", tok_p16)
        t_ohe = op("v", lambda e: e.tensor_scalar(OHE[:, :], PD[0:128, 0:2],
                                                  P16T[:, 0:1], None,
                                                  op0=ALU.is_equal))
        t_ohz = op("g", lambda e: e.memset(OH[:, :], 0))
        wdma("v", "g", tok_consts)
        w("v", "g", t_ohz)
        fence("v")
        for k in range(2):
            op("v", lambda e, k=k: e.tensor_scalar(
                OH[:, 16 * k:16 * k + 16], I16t, OHE[:, k:k + 1], None,
                op0=ALU.mult))
        t_oh = Cn["v"]
        free_bank(("PD",), "v", t_ohe)
        wdma("t", "g", tok_tkf)
        use_bank(("PD",))
        mm(PD[0:128, 2:4], ONES1[0:1, 0:128], TKS[0:1, 0:2], True, True, sem=True)
        w("v", "t", Cn["t"])
        t_tkb = op("v", lambda e: e.tensor_copy(TKB[:, :], PD[0:128, 2:4]))
        free_bank(("PD",), "v", t_tkb)

        wdma("t", "g", tok_aall)
        w("t", "v", t_oh)
        for d in range(DC):
            key = ("PC", d % 2)
            bank = PC[:, 0:32] if d % 2 == 0 else PC[:, 512:544]
            use_bank(key)
            mm(bank, W1[:, 6144 + 128 * d:6144 + 128 * (d + 1)], OH[:, :],
               True, True, sem=True)
            w("v", "t", Cn["t"])
            tv = op("v", lambda e, d=d, bank=bank: e.tensor_copy(
                ASEL[:, 32 * d:32 * (d + 1)], bank))
            free_bank(key, "v", tv)
        wdma("t", "g", tok_ball)
        for n in range(8):
            use_bank(("PD",))
            mm(PD[0:32, 0:512], OH[:, :], ARENA[:, 512 * n:512 * (n + 1)],
               True, True, sem=True)
            w("v", "t", Cn["t"])
            tv = op("v", lambda e, n=n: e.tensor_copy(
                ARENA[0:32, 512 * n:512 * (n + 1)], PD[0:32, 0:512]))
            free_bank(("PD",), "v", tv)
        t_bsel = Cn["v"]
        t_p0b = Cn["t"]
        if "OH" in dbg:
            w("s", "v", t_bsel)
            dma("s", dbg["OH"][:, :], OH[:, :])
        if "BselT" in dbg:
            w("s", "v", t_bsel)
            dma("s", dbg["BselT"][:, :], ARENA[:, :])
        if "ASEL" in dbg:
            w("s", "v", t_bsel)
            dma("s", dbg["ASEL"][:, :], ASEL[:, :])
        cut("p0b")


        wdma("v", "s", tok_vecs)
        fence("v")
        op("v", lambda e: e.tensor_scalar(DERV[:, 0:8], MODT[:, 8:16], 1.0, None,
                                          op0=ALU.add))
        op("v", lambda e: e.tensor_copy(DERV[:, 8:16], MODT[:, 0:8]))
        op("v", lambda e: e.tensor_scalar(DERV[:, 16:24], MODT[:, 32:40], 1.0, None,
                                          op0=ALU.add))
        op("v", lambda e: e.tensor_copy(DERV[:, 24:32], MODT[:, 24:32]))
        for (src0, dst) in ((16, 32), (40, 40)):
            op("v", lambda e, s=src0: e.tensor_scalar(
                STF[:, 8:16], MODT[:, s:s + 8], 0.0, None, op0=ALU.is_ge))
            fence("v")
            op("v", lambda e: e.tensor_scalar(
                STF[:, 8:16], STF[:, 8:16], 2.0, -1.0, op0=ALU.mult, op1=ALU.add))
            fence("v")
            op("v", lambda e, s=src0: e.tensor_mul(
                STF[:, 0:8], MODT[:, s:s + 8], STF[:, 8:16]))
            fence("v")
            op("v", lambda e: e.tensor_scalar(
                STF[:, 0:8], STF[:, 0:8], 0.25, None, op0=ALU.max))
            fence("v")
            op("v", lambda e, d=dst: e.tensor_mul(
                DERV[:, d:d + 8], STF[:, 0:8], STF[:, 8:16]))
            fence("v")
        op("v", lambda e: e.tensor_mul(DERV[:, 48:56], DERV[:, 32:40],
                                       VECS[:, 40:48]))
        t_derv = op("v", lambda e: e.tensor_mul(DERV[:, 56:64], DERV[:, 32:40],
                                                VECS[:, 48:56]))
        if "modT" in dbg:
            w("s", "v", t_derv)
            dma("s", dbg["modT"][:, :], MODT[:, :])
        if "DERV" in dbg:
            w("s", "v", t_derv)
            dma("s", dbg["DERV"][:, :], DERV[:, :])
        for d in range(DC):
            op("v", lambda e, d=d: e.tensor_scalar(
                SH[:, 2048 * d:2048 * (d + 1)], SH[:, 2048 * d:2048 * (d + 1)],
                DERV[:, d:d + 1], DERV[:, 8 + d:8 + d + 1],
                op0=ALU.mult, op1=ALU.add))
        t_h = Cn["v"]
        tok_dbg_h = None
        if "hT" in dbg:
            w("s", "v", t_h)
            tok_dbg_h = dma("s", dbg["hT"][:, :], SH[:, :])
        cut("p1")

        cut("p0a")


        tok_wq2 = wload(W1, "Wq2", gate=("t", t_p0b))

        FMROT = [(PB, 0), (PB, 512), (PC, 0), (PC, 512)]

        def fm_proj(wblk, hoff_w, dstc, bias_col):
            for m in range(DC):
                bk, off = FMROT[m % 4]
                key = ("PB" if bk is PB else "PC", off // 512)
                bank = bk[:, off:off + 512]
                use_bank(key)
                for k in range(DC):
                    mm(bank, wblk(k, m),
                       SH[:, 2048 * k + hoff_w: 2048 * k + hoff_w + 512],
                       start=(k == 0), stop=(k == DC - 1), sem=(k == DC - 1))
                w("sc", "t", Cn["t"])
                tv = op("sc", lambda e, m=m, bank=bank: e.activation(
                    dstc(m), bank, AF.Identity,
                    bias=VECS[:, bias_col * 8 + m: bias_col * 8 + m + 1]))
                free_bank(key, "sc", tv)

        TPROT = [(PA, "PA"), (PB, "PB"), (PC, "PC")]

        def tok_proj(sax, dstsb, wbuf, base_tile, toks=None):
            for tt in range(4):
                i = base_tile + tt
                PT, pnm = TPROT[i % 3]
                use_bank((pnm, 0), (pnm, 1))
                for half in range(2):
                    bank = PT[:, 0:512] if half == 0 else PT[:, 512:1024]
                    for k in range(DC):
                        mm(bank,
                           sax[:, 512 * k + 128 * tt: 512 * k + 128 * tt + 128],
                           wbuf[:, 1024 * k + 512 * half:
                                1024 * k + 512 * half + 512],
                           start=(k == 0), stop=False)
                    mm(bank, ONES1[0:1, 0:128],
                       BROW[0:1, 512 * half: 512 * half + 512],
                       start=False, stop=True, sem=(half == 1))
                t_mm = Cn["t"]
                w("sc", "t", t_mm)
                op("sc", lambda e, i=i, PT=PT: e.activation(
                    dstsb[:, 1024 * i:1024 * (i + 1)], PT[:, :], AF.Relu))
                t_relu = Cn["sc"]
                w("v", "sc", t_relu)
                t_min = op("v", lambda e, PT=PT: e.tensor_scalar(
                    TMPE, PT[:, 0:1024], 0.0, None, op0=ALU.min))
                free_bank((pnm, 0), "v", t_min)
                free_bank((pnm, 1), "v", t_min)
                w("sc", "v", t_min)
                t_exp = op("sc", lambda e: e.activation(TMPE, TMPE, AF.Exp))
                w("v", "sc", t_exp)
                tadd = op("v", lambda e, i=i: e.tensor_add(
                    dstsb[:, 1024 * i:1024 * (i + 1)],
                    dstsb[:, 1024 * i:1024 * (i + 1)], TMPE))
                if toks is not None:
                    toks.append(tadd)

        wdma("t", "g", tok_wq)
        wdma("t", "g", tok_brow)
        w("t", "v", t_h)
        t_wq_done = None
        for w_ in range(NW):
            sax = SAa if w_ % 2 == 0 else SAb
            dstc = lambda m, sax=sax: sax[:, 512 * m: 512 * m + 512]
            fm_proj(W0b, 512 * w_, dstc, 0)
            if w_ == NW - 1:
                t_wq_done = Cn["t"]
            wdma("t", "g", tok_wq2)
            w("t", "sc", Cn["sc"])
            tok_proj(sax, SB, W1, 4 * w_)
        t_qchain = Cn["t"]
        t_q2v = Cn["v"]
        cut("qchain")
        tok_wk = wload(W0, "Wk", gate=("t", t_wq_done))
        tok_wk2 = wload(W1, "Wk2", gate=("t", t_qchain))
        w("g", "v", t_q2v)
        tok_brow2 = dma("g", BROW[:, :], brow_in[0:1, D:2 * D])

        elu_k = []
        t_wk_done = None
        for w_ in range(NW):
            sax = SAa if w_ % 2 == 0 else SAb
            wdma("t", "g", tok_wk)
            dstc = lambda m, sax=sax: sax[:, 512 * m: 512 * m + 512]
            fm_proj(W0b, 512 * w_, dstc, 1)
            if w_ == NW - 1:
                t_wk_done = Cn["t"]
            wdma("t", "g", tok_wk2)
            wdma("t", "g", tok_brow2)
            w("t", "sc", Cn["sc"])
            tok_proj(sax, SC, W1, 4 * w_, toks=elu_k)
            for tt in range(4):
                i = 4 * w_ + tt
                op("v", lambda e, i=i: e.tensor_mul(
                    TMPF, SB[:, 1024 * i:1024 * (i + 1)],
                    SC[:, 1024 * i:1024 * (i + 1)]))
                op("v", lambda e, i=i: e.tensor_reduce(
                    QKS[:, 16 * i:16 * (i + 1)],
                    TMPF.rearrange("p (h d) -> p h d", d=DH),
                    axis=AX.X, op=ALU.add))
        t_k2done = Cn["v"]
        t_kchain = Cn["t"]
        tok_wv = wload(W0, "Wv", gate=("t", t_wk_done))
        tok_wv2 = wload(W1, "Wv2", gate=("t", t_kchain))
        tok_dbg_q2 = tok_dbg_k2 = tok_dbg_qks = None
        if "q2" in dbg:
            w("s", "v", t_k2done)
            tok_dbg_q2 = dma("s", dbg["q2"][:, :], SB[:, :])
        if "k2" in dbg:
            w("s", "v", t_k2done)
            tok_dbg_k2 = dma("s", dbg["k2"][:, :], SC[:, :])
        if "qks" in dbg:
            w("s", "v", t_k2done)
            tok_dbg_qks = dma("s", dbg["qks"][:, :], QKS[:, :])
        cut("kchain")

        for i in range(NT):
            op("v", lambda e, i=i: e.tensor_scalar(
                STF[:, 16 * i:16 * (i + 1)], QKS[:, 16 * i:16 * (i + 1)],
                float(EPS_QK), None, op0=ALU.add))
        t_qe = Cn["v"]
        w("sc", "v", t_qe)
        op("sc", lambda e: e.activation(STF[:, 512:768], STF[:, 0:256], AF.Ln))
        fence("sc")
        t_recip = op("sc", lambda e: e.activation(
            TMPF[:, 0:256], STF[:, 512:768], AF.Exp, scale=-1.0))
        if "recip" in dbg:
            w("s", "sc", t_recip)
            dma("s", dbg["recip"][:, :], TMPF[:, 0:256])
        if "qks2" in dbg:
            w("s", "v", t_qe)
            dma("s", dbg["qks2"][:, :], QKS[:, :])

        use_bank(("PA", 0), ("PA", 1))
        for h in range(H):
            for i in range(NT):
                w("t", "v", elu_k[i])
                mm(PA[0:64, 64 * h:64 * (h + 1)],
                   SC[:, 1024 * i + 64 * h: 1024 * i + 64 * h + 64],
                   SB[:, 1024 * i + 64 * h: 1024 * i + 64 * h + 64],
                   start=(i == 0), stop=(i == NT - 1),
                   sem=(i == NT - 1 and h in (7, 15)))
        t_sdone = Cn["t"]
        w("v", "t", t_sdone)
        op("v", lambda e: e.tensor_copy(SPART[:, 0:512], PA[0:64, 0:512]))
        t_sev = op("v", lambda e: e.tensor_copy(SPART[:, 512:1024],
                                                PA[0:64, 512:1024]))
        free_bank(("PA", 0), "v", t_sev)
        free_bank(("PA", 1), "v", t_sev)
        cut("sphase")
        w("g", "v", t_sev)
        tok_ccin = dma("g", cc_in[:, :], SPART[:, :])
        wdma("g", "g", tok_ccin)
        t_cc = op("g", lambda e: e.collective_compute(
            "AllReduce", ALU.add,
            replica_groups=[[0, 1], [2, 3], [4, 5], [6, 7]],
            ins=[cc_in[:, :]], outs=[cc_out[:, :]]))

        cut("cc")
        w("t", "sc", t_recip)
        for i in range(NT):
            po = 128 * (i % 8)
            use_bank(("PTB7",))
            op("t", lambda e, i=i, po=po: e.transpose(
                PTB[0:16, po:po + 128], TMPF[:, 16 * i:16 * (i + 1)], I128),
                sem=True)
            w("v", "t", Cn["t"])
            tv = op("v", lambda e, i=i, po=po: e.tensor_copy(
                QKSRT[:, 128 * i:128 * (i + 1)], PTB[0:16, po:po + 128]))
            free_bank(("PTB7",), "v", tv)
        t_qksr = Cn["v"]
        if "qksrT" in dbg:
            w("s", "v", t_qksr)
            dma("s", dbg["qksrT"][:, :], POOL[:, 3072:5120])

        wdma("t", "g", tok_sel)
        t_wv_done = None
        for w_ in range(NW):
            sax = SAa if w_ % 2 == 0 else SAb
            wdma("t", "g", tok_wv)
            dstc = lambda m, sax=sax: sax[:, 512 * m: 512 * m + 512]
            fm_proj(W0b, 512 * w_, dstc, 2)
            if w_ == NW - 1:
                t_wv_done = Cn["t"]
            wdma("t", "g", tok_wv2)
            w("t", "sc", Cn["sc"])
            for m in range(DC):
                bk, off = FMROT[m % 4]
                key = ("PB" if bk is PB else "PC", off // 512)
                bank = bk[:, off:off + 512]
                use_bank(key)
                for k in range(DC):
                    mm(bank, W1b(k, m), sax[:, 512 * k: 512 * k + 512],
                       start=(k == 0), stop=(k == DC - 1), sem=(k == DC - 1))
                w("sc", "t", Cn["t"])
                w("sc", "t", t_sdone)
                if tok_dbg_q2:
                    wdma("sc", "s", tok_dbg_q2)
                if tok_dbg_qks:
                    wdma("sc", "s", tok_dbg_qks)
                w("sc", "v", t_k2done)
                tv = op("sc", lambda e, m=m, w_=w_, bank=bank: e.activation(
                    SB[:, 2048 * m + 512 * w_: 2048 * m + 512 * w_ + 512], bank,
                    AF.Identity, bias=VECS[:, 3 * 8 + m: 3 * 8 + m + 1]))
                free_bank(key, "sc", tv)
            t_v2w = Cn["sc"]
            w("sc", "v", t_derv)
            for d in range(DC):
                op("sc", lambda e, d=d, w_=w_: e.activation(
                    SX[:, 2048 * d + 512 * w_: 2048 * d + 512 * w_ + 512],
                    SX[:, 2048 * d + 512 * w_: 2048 * d + 512 * w_ + 512],
                    AF.Identity, bias=DERV[:, 56 + d:56 + d + 1]))
            t_gb2 = Cn["sc"]
            w("t", "v", t_qksr)
            for c in range(DC):
                key = ("PA", c % 2)
                bank = PA[:, 0:512] if c % 2 == 0 else PA[:, 512:1024]
                use_bank(key)
                mm(bank, SEL[0:16, 128 * c:128 * (c + 1)],
                   QKSRT[:, 512 * w_:512 * (w_ + 1)], True, True, sem=True)
                w("v", "t", Cn["t"])
                w("v", "sc", t_v2w)
                tv = op("v", lambda e, c=c, w_=w_, bank=bank: e.tensor_mul(
                    SB[:, 2048 * c + 512 * w_: 2048 * c + 512 * w_ + 512],
                    SB[:, 2048 * c + 512 * w_: 2048 * c + 512 * w_ + 512], bank))
                free_bank(key, "v", tv)
        t_v2 = Cn["sc"]
        t_vchain = Cn["t"]
        tok_wo = wload(W0, "Wo", gate=("t", t_wv_done))
        t_v2s = Cn["v"]
        t_bcast_t = Cn["t"]
        tok_dbg_v2 = None
        if "v2" in dbg:
            w("s", "v", t_v2s)
            tok_dbg_v2 = dma("s", dbg["v2"][:, :], SB[:, :])
        cut("v2s")

        w("g", "g", t_cc)
        w("g", "t", t_bcast_t)
        dma("g", STOT[:, :], cc_out[:, :])
        tok_stot = dma("g", STOT64[:, :], cc_out[:, :])
        if "Stot" in dbg:
            wdma("s", "g", tok_stot)
            dma("s", dbg["Stot"][:, :], STOT[:, :])

        wdma("t", "g", tok_stot)
        w("t", "v", t_v2s)
        aout_sc = {}
        aout_v = {}
        wo_end = {}
        y_tok = {}
        y_add = {}
        sq_v_tok = {}
        p4_var = {}
        p4_rst = {}
        p4_app = {}
        p4_sqmm = {}
        SQSL = [POOL[:, 2048:2560], POOL[:, 2560:3072],
                POOL[:, 4096:4608], POOL[:, 4608:5120]]
        TMP4 = POOL[:, 4608:5120]
        MUT4 = POOL[:, 2816:3328]
        RST4 = STF[:, 512:1024]
        TMP4L = POOL[:, 3584:4096]

        def out_units(nn):
            sax = SAa if nn % 2 == 0 else SAb
            for c in range(DC):
                for hh in range(2):
                    h = 2 * c + hh
                    j = c * 2 + hh
                    bk, off = FMROT[j % 4]
                    key = ("PB" if bk is PB else "PC", off // 512)
                    bank = bk[0:64, off:off + 512]
                    use_bank(key)
                    ST = STOT if hh == 0 else STOT64
                    mm(bank, ST[:, 64 * h:64 * (h + 1)],
                       SB[64 * hh:64 * (hh + 1),
                          2048 * c + 512 * nn: 2048 * c + 512 * nn + 512],
                       True, True, sem=True)
                    eng2 = "sc" if j % 2 == 0 else "v"
                    w(eng2, "t", Cn["t"])
                    if nn >= 2:
                        w(eng2, "t", wo_end[nn - 2])
                    if eng2 == "sc":
                        tv = op("sc", lambda e, c=c, hh=hh, bank=bank, sax=sax:
                                e.copy(
                                    sax[64 * hh:64 * (hh + 1),
                                        512 * c:512 * (c + 1)], bank))
                    else:
                        tv = op("v", lambda e, c=c, hh=hh, bank=bank, sax=sax:
                                e.tensor_copy(
                                    sax[64 * hh:64 * (hh + 1),
                                        512 * c:512 * (c + 1)], bank))
                    free_bank(key, eng2, tv)
            aout_sc[nn] = Cn["sc"]
            aout_v[nn] = Cn["v"]

        def wo_units(nn):
            sax = SAa if nn % 2 == 0 else SAb
            wdma("t", "g", tok_wo)
            w("t", "sc", aout_sc[nn])
            w("t", "v", aout_v[nn])
            for m in range(DC):
                bk, off = FMROT[m % 4]
                key = ("PB" if bk is PB else "PC", off // 512)
                bank = bk[:, off:off + 512]
                tslot = POOL[:, 1024 + 512 * (m % 2):1024 + 512 * (m % 2) + 512]
                use_bank(key)
                for k in range(DC):
                    mm(bank, W0b(k, m), sax[:, 512 * k: 512 * k + 512],
                       start=(k == 0), stop=(k == DC - 1), sem=(k == DC - 1))
                w("sc", "t", Cn["t"])
                wdma("sc", "g", tok_ccin)
                w("sc", "v", Cn["v"])
                op("sc", lambda e, bank=bank, tslot=tslot, m=m: e.activation(
                    tslot, bank, AF.Identity,
                    bias=VECS[:, 4 * 8 + m: 4 * 8 + m + 1]))
                t_bo = Cn["sc"]
                free_bank(key, "sc", t_bo)
                w("v", "sc", t_bo)
                w("v", "t", t_sdone)
                if tok_dbg_k2:
                    wdma("v", "s", tok_dbg_k2)
                op("v", lambda e, m=m, nn=nn, tslot=tslot: e.tensor_add(
                    SC[:, 2048 * m + 512 * nn: 2048 * m + 512 * nn + 512],
                    tslot,
                    SH[:, 2048 * m + 512 * nn: 2048 * m + 512 * nn + 512]))
                y_add[(nn, m)] = Cn["v"]
                prev = p4_sqmm.get(nn - 1)
                if prev:
                    w("v", "t", prev[(m % 4) + 4])
                op("v", lambda e, m=m, nn=nn: e.tensor_mul(
                    SQSL[m % 4],
                    SC[:, 2048 * m + 512 * nn: 2048 * m + 512 * nn + 512],
                    SC[:, 2048 * m + 512 * nn: 2048 * m + 512 * nn + 512]))
                sq_v_tok[(nn, m)] = Cn["v"]
            wo_end[nn] = Cn["t"]
            y_tok[nn] = Cn["v"]

        def p4_stats(w_):
            ys = lambda d, w_=w_: SC[:, 2048 * d + 512 * w_:
                                     2048 * d + 512 * w_ + 512]
            use_bank(("PA", 0), ("PA", 1))
            sq_mm = {}
            t_mu = 0
            for d in range(DC):
                w("t", "v", y_add[(w_, d)])
                mm(PA[:, 0:512], Jc, ys(d), start=(d == 0), stop=(d == DC - 1),
                   sem=(d == DC - 1))
                if d == DC - 1:
                    t_mu = Cn["t"]
                w("t", "v", sq_v_tok[(w_, d)])
                mm(PA[:, 512:1024], Jc, SQSL[d % 4], start=(d == 0),
                   stop=(d == DC - 1), sem=True)
                sq_mm[d] = Cn["t"]
            p4_sqmm[w_] = sq_mm
            mu_ap = MUT if w_ < NW - 1 else MUT4
            w("v", "t", t_mu)
            op("v", lambda e, mu_ap=mu_ap: e.tensor_copy(mu_ap, PA[:, 0:512]))
            t_mucp = op("v", lambda e: e.tensor_copy(STF[:, 0:512],
                                                     PA[:, 0:512]))
            free_bank(("PA", 0), "v", t_mucp)
            op("v", lambda e: e.tensor_mul(STF[:, 512:1024], STF[:, 0:512],
                                           STF[:, 0:512]))
            w("v", "t", sq_mm[DC - 1])
            t_var = op("v", lambda e: e.tensor_sub(
                STF[:, 0:512], PA[:, 512:1024], STF[:, 512:1024]))
            free_bank(("PA", 1), "v", t_var)
            p4_var[w_] = t_var

        def p4_rstd(w_):
            w("sc", "v", p4_var[w_])
            if 0 < w_ < NW - 1:
                w("sc", "v", p4_app[w_ - 1])
            op("sc", lambda e: e.activation(STF[:, 512:1024], STF[:, 0:512],
                                            AF.Ln, bias=float(EPS_IN)))
            rs_ap = RST if w_ < NW - 1 else RST4
            p4_rst[w_] = op("sc", lambda e, rs_ap=rs_ap: e.activation(
                rs_ap, STF[:, 512:1024], AF.Exp, scale=-0.5))

        def p4_apply(w_):
            mu_ap = MUT if w_ < NW - 1 else MUT4
            rs_ap = RST if w_ < NW - 1 else RST4
            tmp_ap = TMP4 if w_ < NW - 1 else TMP4L
            w("v", "sc", p4_rst[w_])
            w("v", "sc", t_gb2)
            for d in range(DC):
                op("v", lambda e, d=d, w_=w_, tmp_ap=tmp_ap, mu_ap=mu_ap:
                   e.tensor_sub(
                       tmp_ap,
                       SC[:, 2048 * d + 512 * w_: 2048 * d + 512 * w_ + 512],
                       mu_ap))
                op("v", lambda e, tmp_ap=tmp_ap, rs_ap=rs_ap: e.tensor_mul(
                    tmp_ap, tmp_ap, rs_ap))
                op("v", lambda e, d=d, w_=w_, tmp_ap=tmp_ap:
                   e.scalar_tensor_tensor(
                       SX[:, 2048 * d + 512 * w_: 2048 * d + 512 * w_ + 512],
                       tmp_ap, DERV[:, 48 + d:48 + d + 1],
                       SX[:, 2048 * d + 512 * w_: 2048 * d + 512 * w_ + 512],
                       op0=ALU.mult, op1=ALU.add))
            p4_app[w_] = Cn["v"]

        h2_tok = {}
        u_tok = {}
        aend_tok = {0: None}
        interC_tok = {}
        ln2_ss = {}

        def ln2_stage(tc, t_aend_prev, early=False):
            xs2 = lambda d: SX[:, 2048 * d + 256 * tc: 2048 * d + 256 * tc + 256]
            use_bank(("PD",))
            w("t", "v", p4_app[min(tc // 2, NW - 1)])
            if tc:
                wdma("v", "s", tok_dbg_x1)
            for d in range(DC):
                mm(PD[:, 0:256], Jc, xs2(d), start=(d == 0), stop=(d == DC - 1),
                   sem=(d == DC - 1))
            t_mu_mm = Cn["t"]
            use_bank(("PB", 0))
            w("sc", "t", Cn["t"])
            for d in range(DC):
                slot = XSQ[:, 256 * (d % 2):256 * (d % 2) + 256]
                if d >= 2:
                    w("sc", "t", Cn["t"])
                op("sc", lambda e, d=d, slot=slot, xs2=xs2: e.activation(
                    slot, xs2(d), AF.Square))
                w("t", "sc", Cn["sc"])
                mm(PB[:, 0:256], Jc, slot, start=(d == 0), stop=(d == DC - 1),
                   sem=True)
            t_ss_mm = Cn["t"]
            ln2_ss[tc] = t_ss_mm
            if early:
                w("v", "sc", p4_rst[NW - 1])
            w("v", "t", t_mu_mm)
            op("v", lambda e: e.tensor_copy(MUT2, PD[:, 0:256]))
            t_mucp = op("v", lambda e: e.tensor_copy(STF[:, 0:256], PD[:, 0:256]))
            free_bank(("PD",), "v", t_mucp)
            op("v", lambda e: e.tensor_mul(STF[:, 256:512], STF[:, 0:256],
                                           STF[:, 0:256]))
            w("v", "t", t_ss_mm)
            t_var = op("v", lambda e: e.tensor_sub(STF[:, 0:256], PB[:, 0:256],
                                                   STF[:, 256:512]))
            free_bank(("PB", 0), "v", t_var)
            w("sc", "v", t_var)
            op("sc", lambda e: e.activation(STF[:, 256:512], STF[:, 0:256],
                                            AF.Ln, bias=float(EPS_OUT)))
            t_rst2 = op("sc", lambda e: e.activation(RST2, STF[:, 256:512],
                                                     AF.Exp, scale=-0.5))
            w("v", "sc", t_rst2)
            if t_aend_prev:
                w("v", "t", t_aend_prev)
            for d in range(DC):
                op("v", lambda e, d=d, xs2=xs2: e.tensor_sub(
                    TMP6[:, 0:256], xs2(d), MUT2))
                op("v", lambda e: e.tensor_mul(TMP6[:, 0:256], TMP6[:, 0:256],
                                               RST2))
                op("v", lambda e, d=d: e.tensor_scalar(
                    H2TC[:, 256 * d:256 * (d + 1)], TMP6[:, 0:256],
                    DERV[:, 16 + d:16 + d + 1], DERV[:, 24 + d:24 + d + 1],
                    op0=ALU.mult, op1=ALU.add))
            h2_tok[tc] = Cn["v"]
            if tc == 0 and "h2tc0" in dbg:
                w("s", "v", h2_tok[tc])
                dma("s", dbg["h2tc0"][:, :], POOL[:, 0:2048])

        def u_stage(tc):
            w("t", "v", h2_tok[tc])
            use_bank(("PD",))
            for k in range(DC):
                mm(PD[0:32, 0:256], ASEL[:, 32 * k:32 * (k + 1)],
                   H2TC[:, 256 * k:256 * (k + 1)], start=(k == 0),
                   stop=(k == DC - 1), sem=(k == DC - 1))
            w("v", "t", Cn["t"])
            op("v", lambda e: e.tensor_copy(UB[0:32, 0:256], PD[0:32, 0:256]))
            op("v", lambda e: e.tensor_copy(UB[0:32, 256:512], PD[0:32, 0:256]))
            t_u01 = op("v", lambda e: e.tensor_scalar(
                UB[0:16, 256:512], PD[0:16, 0:256], -1.0, None, op0=ALU.mult))
            free_bank(("PD",), "v", t_u01)
            u_tok[tc] = t_u01
            if tc == 0 and "u01" in dbg:
                w("s", "v", t_u01)
                dma("s", dbg["u01"][:, :], UB[:, :])

        out_units(0)
        out_units(1)
        wo_units(0)
        p4_stats(0)
        p4_rstd(0)
        out_units(2)
        p4_apply(0)
        wo_units(1)
        p4_stats(1)
        p4_rstd(1)
        out_units(3)
        t_sb_free = Cn["t"]
        p4_apply(1)
        wo_units(2)
        t_wo2t = Cn["t"]
        p4_stats(2)
        p4_rstd(2)
        p4_apply(2)
        wo_units(3)
        t_wo3t = Cn["t"]
        p4_stats(3)
        p4_rstd(3)
        ln2_stage(0, None, early=True)
        cut("ln2s")
        u_stage(0)
        cut("u")
        p4_apply(3)
        t_y = y_tok[NW - 1]
        t_p3t = Cn["t"]
        tok_dbg_y = None
        if "yT" in dbg:
            w("s", "v", p4_app[NW - 1])
            tok_dbg_y = dma("s", dbg["yT"][:, :], SC[:, :])
        cut("p3")
        t_x1 = Cn["v"]
        t_p4t = Cn["t"]
        tok_dbg_x1 = None
        if "x1T" in dbg:
            w("s", "v", t_x1)
            tok_dbg_x1 = dma("s", dbg["x1T"][:, :], SX[:, :])
        cut("p4")

        wi_tok = []
        for kc in range(8):
            if kc < 4:
                dstap = SB[:, 4096 * kc:4096 * (kc + 1)]
                gate = ("t", t_sb_free)
                wdma("g", "s", tok_dbg_v2)
            elif kc < 6:
                dstap = W1[:, 4096 * (kc - 4):4096 * (kc - 4) + 4096]
                gate = ("t", t_vchain)
            elif kc == 6:
                dstap = SAa[:, :]
                gate = ("t", t_wo2t)
            else:
                dstap = SAb[:, :]
                gate = ("t", t_wo3t)
            w("g", gate[0], gate[1])
            wi_tok.append(dma("g", dstap, wiT_in[128 * kc:128 * (kc + 1), :]))
        wo_tok = []
        for fc in range(FC):
            if fc < 16:
                dstap = SH[:, 1024 * fc:1024 * (fc + 1)]
                w("g", "v", t_y)
                wdma("g", "s", tok_dbg_h)
            else:
                dstap = SC[:, 1024 * (fc - 16):1024 * (fc - 16) + 1024]
                w("g", "t", t_p4t)
                w("g", "v", t_x1)
                wdma("g", "s", tok_dbg_y)
            wo_tok.append(dma("g", dstap, woT_in[128 * fc:128 * (fc + 1), :]))


        AROT = [(PA, 0, ("PA", 0)), (PA, 512, ("PA", 1)), (PD, 0, ("PD",)),
                (PB, 0, ("PB", 0)), (PB, 512, ("PB", 1)), (PC, 0, ("PC", 0))]

        def a_stage(tc, t_p3t):
            w("t", "v", u_tok[tc])
            for kc in range(8):
                wdma("t", "g", wi_tok[kc])

            def a_head(fc):
                bk, off, key = AROT[fc % 6]
                bank = bk[:, off:off + 256]
                r0s = R0P[:, 256 * (fc % 3):256 * (fc % 3) + 256]
                use_bank(key)
                mm(bank, ARENA[0:16, 128 * fc:128 * fc + 128], UB[0:16, 0:256],
                   start=True, stop=False)
                for k in range(DC):
                    mm(bank, wiTblk(k, fc), H2TC[:, 256 * k:256 * (k + 1)],
                       start=False, stop=False, sem=(k == DC - 1))
                t_head = Cn["t"]
                w("sc", "t", t_head)
                w("sc", "v", Cn["v"])
                op("sc", lambda e, bank=bank, r0s=r0s: e.activation(
                    r0s, bank, AF.Relu, scale=TKB[:, 0:1]))
                return Cn["sc"]

            def a_tail(fc, t_ev0):
                bk, off, key = AROT[fc % 6]
                bank = bk[:, off:off + 256]
                r0s = R0P[:, 256 * (fc % 3):256 * (fc % 3) + 256]
                r1s = R1P[:, 256 * (fc % 3):256 * (fc % 3) + 256]
                w("t", "sc", t_ev0)
                mm(bank, ARENA[0:32, 128 * fc:128 * fc + 128], UB[0:32, 256:512],
                   start=False, stop=True, sem=True)
                w("v", "t", Cn["t"])
                t_ev1 = op("v", lambda e, bank=bank, r1s=r1s: e.tensor_scalar(
                    r1s, bank, 0.0, TKB[:, 1:2], op0=ALU.max, op1=ALU.mult))
                free_bank(key, "v", t_ev1)
                if tc == 0:
                    w("v", "t", t_p3t)
                op("v", lambda e, fc=fc, r0s=r0s, r1s=r1s: e.tensor_add(
                    W0[:, 256 * fc:256 * (fc + 1)], r0s, r1s))

            pend = []
            for fc in range(FC):
                ev0 = a_head(fc)
                pend.append((fc, ev0))
                if len(pend) > 2:
                    a_tail(*pend.pop(0))
            aend_tok[tc + 1] = Cn["t"]
            for it in pend:
                a_tail(*it)
            interC_tok[tc] = Cn["v"]
            if tc == 0 and "interC0" in dbg:
                w("s", "v", interC_tok[tc])
                dma("s", dbg["interC0"][:, :], W0[:, :])
            if tc == 0:
                cut("aphase")

        def mm2_stage(tc):
            w("t", "v", interC_tok[tc])
            for fc in range(FC):
                wdma("t", "g", wo_tok[fc])
            dp_bank = [(PB, 0), (PB, 256), (PB, 512), (PB, 768),
                       (PC, 0), (PC, 256), (PC, 512), (PC, 768)]
            for dp in range(DC):
                bk, off = dp_bank[dp]
                key = ("PB", off // 512) if bk is PB else ("PC", off // 512)
                if dp % 2 == 0:
                    use_bank(key)
                for fc in range(FC):
                    mm(bk[:, off:off + 256], woTblk(fc, dp),
                       W0[:, 256 * fc:256 * (fc + 1)], start=(fc == 0),
                       stop=(fc == FC - 1), sem=(fc == FC - 1))
                if dp % 2 == 1:
                    t_dp = Cn["t"]
                    w("v", "t", t_dp)
                    for d2 in (dp - 1, dp):
                        bk2, off2 = dp_bank[d2]
                        op("v", lambda e, bk2=bk2, off2=off2, d2=d2:
                           e.tensor_scalar(
                               TMP6[:, 256:512], bk2[:, off2:off2 + 256],
                               DERV[:, 40 + d2:40 + d2 + 1], None, op0=ALU.mult))
                        tv = op("v", lambda e, d2=d2, tc=tc: e.tensor_add(
                            SX[:, 2048 * d2 + 256 * tc:
                               2048 * d2 + 256 * tc + 256],
                            SX[:, 2048 * d2 + 256 * tc:
                               2048 * d2 + 256 * tc + 256],
                            TMP6[:, 256:512]))
                        free_bank(key, "v", tv)
            t_x2 = Cn["v"]
            if tc == 0:
                cut("tc0")
            w("g", "v", t_x2)
            dma("g",
                out_xT.rearrange("(d p) t -> p d t", p=128)[:, :,
                                                            256 * tc:256 * (tc + 1)],
                SX.ap().rearrange("p (d t) -> p d t", d=DC)[:, :,
                                                            256 * tc:256 * (tc + 1)])

        for tc in range(TCN):
            a_stage(tc, t_p3t)
            if tc + 1 < TCN:
                ln2_stage(tc + 1, aend_tok[tc + 1])
            mm2_stage(tc)
            if tc + 1 < TCN:
                u_stage(tc + 1)

        w("g", "dg", D16["g"])
        w("s", "ds", D16["s"])

    with nc.allow_low_precision("bf16 recipe calibrated by problem tolerance"), \
            nc.Block() as blk:
        def run(name, e):
            try:
                emit(name, e)
            except _Cut:
                pass

        @blk.tensor
        def _(e):
            run("t", e)

        @blk.vector
        def _(e):
            run("v", e)

        @blk.scalar
        def _(e):
            run("sc", e)

        @blk.gpsimd
        def _(e):
            run("g", e)

        @blk.sync
        def _(e):
            run("s", e)

    return nc, dbg


_NC_CACHE = {}


def _get_nc(debug=(), stop_after=None):
    key = (tuple(sorted(debug)), stop_after)
    if key not in _NC_CACHE:
        _NC_CACHE[key] = build_nc(debug=key[0], stop_after=stop_after)
    return _NC_CACHE[key]


def _host_inputs(inputs, core):
    b = core // 2
    x = np.asarray(inputs["L0_attn_x"])
    xs = x[b, 2048 * (core % 2):2048 * (core % 2 + 1), :]
    m = {}
    m["xT"] = np.ascontiguousarray(xs.T).astype(np.float32)
    for w in ("Wq", "Wk", "Wv", "Wq2", "Wk2", "Wv2", "Wo"):
        m[w] = np.ascontiguousarray(np.asarray(inputs["L0_attn_" + w],
                                               dtype=np.float32))
    vecs = [inputs["L0_attn_bq"], inputs["L0_attn_bk"], inputs["L0_attn_bv"],
            inputs["L0_attn_bv2"], inputs["L0_attn_bo"],
            inputs["L0_attn_gamma"], inputs["L0_attn_beta"]]
    m["vecs"] = np.ascontiguousarray(np.concatenate(
        [np.asarray(v, dtype=np.float32).reshape(8, 128).T for v in vecs],
        axis=1))
    m["brow"] = np.concatenate(
        [np.asarray(inputs["L0_attn_bq2"], dtype=np.float32).reshape(1, -1),
         np.asarray(inputs["L0_attn_bk2"], dtype=np.float32).reshape(1, -1)],
        axis=1)
    m["cT"] = np.ascontiguousarray(
        np.asarray(inputs["L0_c"], dtype=np.float32)[b].reshape(8, 128).T)
    m["wmod"] = np.ascontiguousarray(np.asarray(inputs["L0_wmod"],
                                                dtype=np.float32))
    m["wiT"] = np.ascontiguousarray(
        np.asarray(inputs["L0_ffn_wi_w"], dtype=np.float32).T)
    m["woT"] = np.ascontiguousarray(
        np.asarray(inputs["L0_ffn_wo_w"], dtype=np.float32).T)
    m["Aall"] = np.ascontiguousarray(
        np.asarray(inputs["L0_ffn_lora_As"], dtype=np.float32).reshape(128, 1024))
    m["BallT"] = np.ascontiguousarray(
        np.asarray(inputs["L0_ffn_lora_Bs"],
                   dtype=np.float32).transpose(0, 2, 1).reshape(128, 4096))
    m["idx"] = np.asarray(inputs["L0_ffn_top_k_indices"]).reshape(1, 2).astype(
        np.int32)
    m["tkv"] = np.asarray(inputs["L0_ffn_top_k_values"],
                          dtype=np.float32).reshape(1, 2)
    constB = np.zeros((128, 272), dtype=np.float32)
    constB[:, 0:128] = 1.0 / 1024.0
    constB[:, 128:256] = np.eye(128, dtype=np.float32)
    constB[:, 256:272] = np.tile(np.eye(16, dtype=np.float32), (8, 1))
    m["constB"] = constB
    constF = np.zeros((128, 8), dtype=np.float32)
    constF[:, 0] = np.arange(128) // 16
    constF[:, 1] = -0.5
    constF[:, 2] = 1e-5
    constF[:, 3] = 1e-3
    constF[:, 4] = 0.0
    constF[:, 5] = 1.0
    m["constF"] = constF
    sel = np.zeros((16, 1024), dtype=np.float32)
    for h in range(16):
        c, hh = h // 2, h % 2
        sel[h, 128 * c + 64 * hh:128 * c + 64 * hh + 64] = 1.0
    m["sel"] = sel
    m["ones_in"] = np.ones((1, 128), dtype=np.float32)
    return m


def kernel(**inputs):
    nc, dbg = _get_nc()
    in_maps = [_host_inputs(inputs, core) for core in range(8)]
    res = run_bass_kernel_spmd(nc, in_maps, core_ids=list(range(8)))
    outs = [np.asarray(res.results[c]["out_xT"]).T for c in range(8)]
    return np.concatenate(outs, axis=0).reshape(4, 4096, 1024).astype(np.float32)
